# version 07
# baseline (speedup 1.0000x reference)
import numpy as np
from contextlib import ExitStack

import concourse.bass as bass
import concourse.mybir as mybir
from concourse.bass import IndirectOffsetOnAxis
from concourse.bass_utils import run_bass_kernel_spmd
import ml_dtypes

F32 = mybir.dt.float32
BF16 = mybir.dt.bfloat16
I32 = mybir.dt.int32
U32 = mybir.dt.uint32
AF = mybir.ActivationFunctionType
ALU = mybir.AluOpType
AX = mybir.AxisListType

S = 2048
D = 1024
T = 16384
NH = 16
DH = 64
E0, K0, C0, F0 = 64, 8, 512, 768
SF = 768
E1, K1, C1, F1 = 8, 2, 640, 4096
OOB = 6.0e8


class P:

    PIPELINE_SAME_ENGINE = True

    def __init__(self, nc):
        self.nc = nc
        self.sem = nc.alloc_semaphore("ser")
        self.count = 0
        self.last_engine = None
        self.wait_val = 0

    def run(self, engine, fn, inc=1, wait=None, pipe=False):
        eng = getattr(self.nc, engine)
        inst = fn(eng)
        if wait is not None:
            inst._wait_ge(self.sem, wait)
            self.wait_val = wait
        elif (pipe or (self.PIPELINE_SAME_ENGINE and engine in ("tensor", "sync"))) \
                and engine == self.last_engine:
            inst._wait_ge(self.sem, self.wait_val)
        else:
            inst._wait_ge(self.sem, self.count)
            self.wait_val = self.count
        inst.then_inc(self.sem, inc)
        self.count += inc
        self.last_engine = engine
        return inst

    def dma(self, out, in_, engine="sync", **kw):
        return self.run(engine, lambda e: e.dma_start(out, in_, **kw), inc=16)

    def mm(self, out, lhsT, rhs, start, stop):
        return self.run("tensor", lambda e: e.matmul(out, lhsT, rhs, start=start, stop=stop))

    def v(self, fn):
        return self.run("vector", fn)

    def s(self, fn):
        return self.run("scalar", fn)

    def g(self, fn):
        return self.run("gpsimd", fn)

    def finish(self):
        for e in ("sync", "tensor", "vector", "scalar", "gpsimd"):
            getattr(self.nc, e).wait_ge(self.sem, self.count)


def new_nc():
    return bass.Bass("TRN2", target_bir_lowering=False)


def mk_ident_bf(nc, p, ctx):
    t = ctx.enter_context(nc.sbuf_tensor("ident_bf", [128, 128], BF16))
    p.g(lambda e: e.memset(t[:, :], 1.0))
    p.g(lambda e: e.affine_select(out=t[:, :], in_=t[:, :], compare_op=ALU.is_equal,
                                  fill=0.0, base=0, pattern=[[-1, 128]], channel_multiplier=1))
    return t



def mk_causal_add(nc, p, ctx, name, val):
    t = ctx.enter_context(nc.sbuf_tensor(name, [128, 128], F32))
    p.g(lambda e: e.memset(t[:, :], 0.0))
    p.g(lambda e: e.affine_select(out=t[:, :], in_=t[:, :], compare_op=ALU.is_ge,
                                  fill=val, base=0, pattern=[[-1, 128]], channel_multiplier=1))
    return t

def mk_strict_upper(nc, p, ctx):
    t = ctx.enter_context(nc.sbuf_tensor("strictU", [128, 128], F32))
    p.g(lambda e: e.memset(t[:, :], 0.0))
    p.g(lambda e: e.affine_select(out=t[:, :], in_=t[:, :], compare_op=ALU.is_ge,
                                  fill=1.0, base=0, pattern=[[-1, 128]], channel_multiplier=1))
    return t


def mk_lower_incl(nc, p, ctx):
    t = ctx.enter_context(nc.sbuf_tensor("lowIncl", [128, 128], F32))
    p.g(lambda e: e.memset(t[:, :], 1.0))
    p.g(lambda e: e.affine_select(out=t[:, :], in_=t[:, :], compare_op=ALU.is_ge,
                                  fill=0.0, base=0, pattern=[[-1, 128]], channel_multiplier=1))
    return t


def mk_ones(nc, p, ctx):
    oc = ctx.enter_context(nc.sbuf_tensor("ones_col", [128, 1], F32))
    orow = ctx.enter_context(nc.sbuf_tensor("ones_row", [1, 128], F32))
    p.g(lambda e: e.memset(oc[:, :], 1.0))
    p.g(lambda e: e.memset(orow[:, :], 1.0))
    return oc, orow


def load_cast_bf(nc, p, w_ap, dst, dst_off, cols, scratch):
    p.dma(scratch[:, 0:cols], w_ap)
    p.v(lambda e: e.tensor_copy(dst[:, dst_off:dst_off + cols], scratch[:, 0:cols]))


def layernorm_fm(nc, p, ctx, acc, xn, scr1, ones_col, ones_row, eps_tile, ntiles=8, ntok=S, tag=""):
    nch = ntok // 512
    with ExitStack() as c2:
        ps_sum = c2.enter_context(nc.psum_tensor("ps_sum" + tag, [1, ntok], F32))
        ps_sq = c2.enter_context(nc.psum_tensor("ps_sq" + tag, [1, ntok], F32))
        mean_row = c2.enter_context(nc.sbuf_tensor("mean_row" + tag, [1, ntok], F32))
        var_row = c2.enter_context(nc.sbuf_tensor("var_row" + tag, [1, ntok], F32))
        msq_row = c2.enter_context(nc.sbuf_tensor("msq_row" + tag, [1, ntok], F32))
        rstd_row = c2.enter_context(nc.sbuf_tensor("rstd_row" + tag, [1, ntok], F32))
        for f in range(ntiles):
            p.s(lambda e, f=f: e.activation(out=scr1[:, 0:ntok], in_=acc[:, f * ntok:(f + 1) * ntok],
                                            func=AF.Square))
            for n in range(nch):
                p.mm(ps_sum[0:1, n * 512:(n + 1) * 512], ones_col[:, 0:1],
                     acc[:, f * ntok + n * 512: f * ntok + (n + 1) * 512],
                     start=(f == 0), stop=(f == ntiles - 1))
                p.mm(ps_sq[0:1, n * 512:(n + 1) * 512], ones_col[:, 0:1],
                     scr1[:, n * 512:(n + 1) * 512],
                     start=(f == 0), stop=(f == ntiles - 1))
        p.s(lambda e: e.activation(out=mean_row[0:1, :], in_=ps_sum[0:1, :], func=AF.Copy, scale=1.0 / D))
        p.s(lambda e: e.activation(out=var_row[0:1, :], in_=ps_sq[0:1, :], func=AF.Copy, scale=1.0 / D))
        p.v(lambda e: e.tensor_mul(msq_row[0:1, :], mean_row[0:1, :], mean_row[0:1, :]))
        p.v(lambda e: e.tensor_sub(var_row[0:1, :], var_row[0:1, :], msq_row[0:1, :]))
        p.s(lambda e: e.activation(out=var_row[0:1, :], in_=var_row[0:1, :], func=AF.Sqrt,
                                   bias=eps_tile[0:1, 0:1], scale=1.0))
        p.v(lambda e: e.reciprocal(rstd_row[0:1, :], var_row[0:1, :]))
    with ExitStack() as c2:
        ps_bc = c2.enter_context(nc.psum_tensor("ps_bc" + tag, [128, 512], F32))
        mean_b = c2.enter_context(nc.sbuf_tensor("mean_b" + tag, [128, ntok], F32))
        rstd_b = c2.enter_context(nc.sbuf_tensor("rstd_b" + tag, [128, ntok], F32))
        for n in range(nch):
            p.mm(ps_bc[:, :], ones_row[0:1, :], mean_row[0:1, n * 512:(n + 1) * 512], start=True, stop=True)
            p.v(lambda e, n=n: e.tensor_copy(mean_b[:, n * 512:(n + 1) * 512], ps_bc[:, :]))
            p.mm(ps_bc[:, :], ones_row[0:1, :], rstd_row[0:1, n * 512:(n + 1) * 512], start=True, stop=True)
            p.v(lambda e, n=n: e.tensor_copy(rstd_b[:, n * 512:(n + 1) * 512], ps_bc[:, :]))
        for f in range(ntiles):
            p.v(lambda e, f=f: e.tensor_sub(scr1[:, 0:ntok], acc[:, f * ntok:(f + 1) * ntok], mean_b[:, :]))
            p.v(lambda e, f=f: e.tensor_mul(xn[:, f * ntok:(f + 1) * ntok], scr1[:, 0:ntok], rstd_b[:, :]))


def softmax_row(nc, p, scr_in, scr_exp, mx, attn_out, kt_len, out_dtype_note=None):
    p.v(lambda e: e.tensor_reduce(mx[:, 0:1], scr_in[:, 0:kt_len], axis=AX.X, op=ALU.max))
    p.v(lambda e: e.tensor_scalar(out=mx[:, 1:2], in0=mx[:, 0:1], scalar1=-1.0, scalar2=None, op0=ALU.mult))
    p.s(lambda e: e.activation(out=scr_exp[:, 0:kt_len], in_=scr_in[:, 0:kt_len], func=AF.Exp,
                               bias=mx[:, 1:2], scale=1.0))
    p.v(lambda e: e.tensor_reduce(mx[:, 2:3], scr_exp[:, 0:kt_len], axis=AX.X, op=ALU.add))
    p.v(lambda e: e.reciprocal(mx[:, 3:4], mx[:, 2:3]))
    p.v(lambda e: e.tensor_scalar(out=attn_out[:, 0:kt_len], in0=scr_exp[:, 0:kt_len],
                                  scalar1=mx[:, 3:4], scalar2=None, op0=ALU.mult))


def build_s1():
    nc = new_nc()
    xT = nc.dram_tensor("xT", [D, S], F32, kind="ExternalInput").ap()
    wq = nc.dram_tensor("wq", [NH, DH, DH], F32, kind="ExternalInput").ap()
    wk = nc.dram_tensor("wk", [NH, DH, DH], F32, kind="ExternalInput").ap()
    wv = nc.dram_tensor("wv", [NH, DH, DH], F32, kind="ExternalInput").ap()
    wo = nc.dram_tensor("wo", [D, D], F32, kind="ExternalInput").ap()
    wsgu = nc.dram_tensor("wsgu", [D, 2 * SF], F32, kind="ExternalInput").ap()
    wsd = nc.dram_tensor("wsd", [SF, D], F32, kind="ExternalInput").ap()
    rwT = nc.dram_tensor("rwT", [D, E0], F32, kind="ExternalInput").ap()

    accT_o = nc.dram_tensor("accT", [D, S], F32, kind="ExternalOutput").ap()
    xnT_o = nc.dram_tensor("xnT", [D, S], BF16, kind="ExternalOutput").ap()
    sel_o = nc.dram_tensor("sel_p", [128, 128], F32, kind="ExternalOutput").ap()
    wnorm_o = nc.dram_tensor("wnorm_p", [128, 128], F32, kind="ExternalOutput").ap()

    with ExitStack() as ctx:
        acc = ctx.enter_context(nc.sbuf_tensor("acc", [128, 8 * S], F32))
        xn = ctx.enter_context(nc.sbuf_tensor("xn", [128, 8 * S], BF16))
        scr1 = ctx.enter_context(nc.sbuf_tensor("scr1", [128, S], F32))
        scr2 = ctx.enter_context(nc.sbuf_tensor("scr2", [128, S], F32))
        eps_tile = ctx.enter_context(nc.sbuf_tensor("eps", [1, 1], F32))
        block = ctx.enter_context(nc.Block())

        @block.sync
        def _(sync):
            p = P(nc)
            ident = mk_ident_bf(nc, p, ctx)
            cmask = mk_causal_add(nc, p, ctx, "cmask", -1e9)
            ones_col, ones_row = mk_ones(nc, p, ctx)
            p.g(lambda e: e.memset(eps_tile[:, :], 1e-5))
            for f in range(8):
                p.dma(acc[:, f * S:(f + 1) * S], xT[f * 128:(f + 1) * 128, :])
            layernorm_fm(nc, p, ctx, acc, xn, scr1, ones_col, ones_row, eps_tile)

            with ExitStack() as c2m:
              M = c2m.enter_context(nc.sbuf_tensor("M", [128, 16 * D], BF16))
              with ExitStack() as c2:
                wq_bf = c2.enter_context(nc.sbuf_tensor("wq_bf", [128, NH * DH], BF16))
                wk_bf = c2.enter_context(nc.sbuf_tensor("wk_bf", [128, NH * DH], BF16))
                wv_bf = c2.enter_context(nc.sbuf_tensor("wv_bf", [128, NH * DH], BF16))
                wf32 = c2.enter_context(nc.sbuf_tensor("wf32", [128, NH * DH], F32))
                for name, w_ap, dst in (("q", wq, wq_bf), ("k", wk, wk_bf), ("v", wv, wv_bf)):
                    for h in range(NH):
                        p.dma(wf32[0:64, h * DH:(h + 1) * DH], w_ap[h])
                        p.dma(wf32[64:128, h * DH:(h + 1) * DH], w_ap[h])
                    p.v(lambda e, dst=dst: e.tensor_copy(dst[:, :], wf32[:, :]))
                QT = c2.enter_context(nc.sbuf_tensor("QT", [64, S], BF16))
                KT = c2.enter_context(nc.sbuf_tensor("KT", [64, S], BF16))
                V = c2.enter_context(nc.sbuf_tensor("V", [128, 16 * DH], BF16))
                attn_bf = c2.enter_context(nc.sbuf_tensor("attn_bf", [128, S], BF16))
                attnT = c2.enter_context(nc.sbuf_tensor("attnT", [128, 256], BF16))
                mx = c2.enter_context(nc.sbuf_tensor("mx", [128, 4], F32))
                ps_s = c2.enter_context(nc.psum_tensor("ps_s", [128, S], F32))
                ps_qk = c2.enter_context(nc.psum_tensor("ps_qk", [128, 512], F32))
                ps_v = ps_qk
                ps_trA = c2.enter_context(nc.psum_tensor("ps_trA", [128, 128], BF16))
                ps_trB = c2.enter_context(nc.psum_tensor("ps_trB", [128, 128], BF16))
                ps_o = c2.enter_context(nc.psum_tensor("ps_o", [128, 64], F32))
                for h in range(NH):
                    ft, fo = h // 2, (h % 2) * 64
                    xh = lambda a, b: xn[fo:fo + 64, ft * S + a: ft * S + b]
                    for n in range(4):
                        p.mm(ps_qk[0:64, :], wq_bf[fo:fo + 64, h * DH:(h + 1) * DH],
                             xh(n * 512, (n + 1) * 512), start=True, stop=True)
                        p.v(lambda e, n=n: e.tensor_copy(QT[0:64, n * 512:(n + 1) * 512], ps_qk[0:64, :]))
                        p.mm(ps_qk[0:64, :], wk_bf[fo:fo + 64, h * DH:(h + 1) * DH],
                             xh(n * 512, (n + 1) * 512), start=True, stop=True)
                        p.v(lambda e, n=n: e.tensor_copy(KT[0:64, n * 512:(n + 1) * 512], ps_qk[0:64, :]))
                    for tc in range(16):
                        p.mm(ps_v[0:128, 0:64], xh(tc * 128, (tc + 1) * 128),
                             wv_bf[fo:fo + 64, h * DH:(h + 1) * DH], start=True, stop=True)
                        p.v(lambda e, tc=tc: e.tensor_copy(V[:, tc * DH:(tc + 1) * DH], ps_v[0:128, 0:64]))
                    for qi in range(16):
                        kt_len = (qi + 1) * 128
                        for st in range(0, kt_len, 512):
                            w = min(512, kt_len - st)
                            p.mm(ps_s[:, st:st + w], QT[0:64, qi * 128:(qi + 1) * 128],
                                 KT[0:64, st:st + w], start=True, stop=True)
                        p.s(lambda e, L=kt_len: e.activation(out=scr1[:, 0:L], in_=ps_s[:, 0:L],
                                                             func=AF.Copy, scale=0.125))
                        dg = scr1[:, qi * 128:(qi + 1) * 128]
                        p.v(lambda e, dg=dg: e.tensor_add(dg, dg, cmask[:, :]))
                        softmax_row(nc, p, scr1, scr2, mx, attn_bf, kt_len)
                        c_copy = {}
                        c_mm = {}
                        c_sm = p.count
                        for kj in range(qi + 1):
                            b = kj % 2
                            pst = ps_trA if b == 0 else ps_trB
                            w_tr = c_sm if kj < 2 else c_copy[kj - 2]
                            p.run("tensor", lambda e, kj=kj, pst=pst: e.transpose(
                                pst[:, :], attn_bf[:, kj * 128:(kj + 1) * 128], ident[:, :]),
                                wait=w_tr)
                            c_tr = p.count
                            w_cp = c_tr if kj < 2 else max(c_tr, c_mm[kj - 2])
                            p.run("vector", lambda e, b=b, pst=pst: e.tensor_copy(
                                attnT[:, b * 128:(b + 1) * 128], pst[:, :]), wait=w_cp)
                            c_copy[kj] = p.count
                            p.run("tensor", lambda e, kj=kj, b=b: e.matmul(
                                ps_o[0:128, 0:64], attnT[:, b * 128:(b + 1) * 128],
                                V[:, kj * DH:(kj + 1) * DH], start=(kj == 0), stop=(kj == qi)),
                                wait=c_copy[kj])
                            c_mm[kj] = p.count
                        p.v(lambda e, qi=qi, h=h: e.tensor_copy(
                            M[:, qi * D + h * DH: qi * D + (h + 1) * DH], ps_o[0:128, 0:64]))

              with ExitStack() as c2:
                wo_bf = c2.enter_context(nc.sbuf_tensor("wo_bf", [128, 8 * D], BF16))
                for k in range(8):
                    load_cast_bf(nc, p, wo[k * 128:(k + 1) * 128, :], wo_bf, k * D, D, scr1)
                ps_b = c2.enter_context(nc.psum_tensor("ps_b", [128, 512], F32))
                for j0 in range(2):
                    for mo in range(8):
                        for rc in range(2):
                            for k in range(8):
                                p.mm(ps_b[:, :], wo_bf[:, k * D + mo * 128:k * D + (mo + 1) * 128],
                                     M[:, (8 * j0 + k) * D + rc * 512:(8 * j0 + k) * D + (rc + 1) * 512],
                                     start=(k == 0), stop=(k == 7))
                            dest = acc[:, mo * S:(mo + 1) * S].rearrange(
                                "p (r two) -> p r two", two=2)[:, rc * 512:(rc + 1) * 512, j0]
                            p.v(lambda e, dest=dest: e.tensor_tensor(dest, dest, ps_b[:, :], ALU.add))

            with ExitStack() as c2:
                wsgu_bf = c2.enter_context(nc.sbuf_tensor("wsgu_bf", [128, 8 * 2 * SF], BF16))
                wsd_bf = c2.enter_context(nc.sbuf_tensor("wsd_bf", [128, 6 * D], BF16))
                scrw = c2.enter_context(nc.sbuf_tensor("scrw", [128, 2 * SF], F32))
                for k in range(8):
                    load_cast_bf(nc, p, wsgu[k * 128:(k + 1) * 128, :], wsgu_bf, k * 2 * SF, 2 * SF, scrw)
                for k in range(6):
                    load_cast_bf(nc, p, wsd[k * 128:(k + 1) * 128, :], wsd_bf, k * D, D, scrw)
                hT = c2.enter_context(nc.sbuf_tensor("hT", [128, 6 * S], BF16))
                ps_b = c2.enter_context(nc.psum_tensor("ps_b2", [128, 512], F32))
                for m in range(12):
                    for n in range(4):
                        for k in range(8):
                            p.mm(ps_b[:, :], wsgu_bf[:, k * 2 * SF + m * 128:k * 2 * SF + (m + 1) * 128],
                                 xn[:, k * S + n * 512:k * S + (n + 1) * 512],
                                 start=(k == 0), stop=(k == 7))
                        if m < 6:
                            p.s(lambda e, m=m, n=n: e.activation(
                                out=hT[:, m * S + n * 512:m * S + (n + 1) * 512],
                                in_=ps_b[:, :], func=AF.Silu))
                        else:
                            dst = hT[:, (m - 6) * S + n * 512:(m - 6) * S + (n + 1) * 512]
                            p.v(lambda e, dst=dst: e.tensor_tensor(dst, ps_b[:, :], dst, ALU.mult))
                for mo in range(8):
                    for n in range(4):
                        for k in range(6):
                            p.mm(ps_b[:, :], wsd_bf[:, k * D + mo * 128:k * D + (mo + 1) * 128],
                                 hT[:, k * S + n * 512:k * S + (n + 1) * 512],
                                 start=(k == 0), stop=(k == 5))
                        dest = acc[:, mo * S + n * 512:mo * S + (n + 1) * 512]
                        p.v(lambda e, dest=dest: e.tensor_tensor(dest, dest, ps_b[:, :], ALU.add))

            with ExitStack() as c2:
                rw_bf = c2.enter_context(nc.sbuf_tensor("rw_bf", [128, 8 * E0], BF16))
                scrw = c2.enter_context(nc.sbuf_tensor("scrw2", [128, E0], F32))
                for k in range(8):
                    load_cast_bf(nc, p, rwT[k * 128:(k + 1) * 128, :], rw_bf, k * E0, E0, scrw)
                sel_p = c2.enter_context(nc.sbuf_tensor("sel_sb", [128, 128], F32))
                wnorm_p = c2.enter_context(nc.sbuf_tensor("wnorm_sb", [128, 128], F32))
                probs = c2.enter_context(nc.sbuf_tensor("probs", [128, E0], F32))
                lgt = c2.enter_context(nc.sbuf_tensor("lgt", [128, E0], F32))
                mx = c2.enter_context(nc.sbuf_tensor("mx2", [128, 4], F32))
                mx8 = c2.enter_context(nc.sbuf_tensor("mx8", [128, 8], F32))
                mi8 = c2.enter_context(nc.sbuf_tensor("mi8", [128, 8], U32))
                ps_r = c2.enter_context(nc.psum_tensor("ps_r", [128, E0], F32))
                for t in range(16):
                    for k in range(8):
                        p.mm(ps_r[:, :], xn[:, k * S + t * 128:k * S + (t + 1) * 128],
                             rw_bf[:, k * E0:(k + 1) * E0], start=(k == 0), stop=(k == 7))
                    p.v(lambda e: e.tensor_copy(lgt[:, :], ps_r[:, :]))
                    softmax_row(nc, p, lgt, scr2, mx, probs, E0)
                    p.v(lambda e: e.max(out=mx8[:, :], in_=probs[:, 0:E0]))
                    p.v(lambda e: e.max_index(out=mi8[:, :], in_max=mx8[:, :], in_values=probs[:, 0:E0]))
                    p.v(lambda e: e.tensor_reduce(mx[:, 0:1], mx8[:, :], axis=AX.X, op=ALU.add))
                    p.v(lambda e: e.reciprocal(mx[:, 1:2], mx[:, 0:1]))
                    p.v(lambda e, t=t: e.tensor_scalar(out=wnorm_p[:, t * 8:(t + 1) * 8], in0=mx8[:, :],
                                                       scalar1=mx[:, 1:2], scalar2=None, op0=ALU.mult))
                    p.v(lambda e, t=t: e.tensor_copy(sel_p[:, t * 8:(t + 1) * 8], mi8[:, :]))
                p.dma(sel_o, sel_p[:, :])
                p.dma(wnorm_o, wnorm_p[:, :])

            for f in range(8):
                p.dma(accT_o[f * 128:(f + 1) * 128, :], acc[:, f * S:(f + 1) * S])
                p.dma(xnT_o[f * 128:(f + 1) * 128, :], xn[:, f * S:(f + 1) * S])
            p.finish()
    return nc


def build_s2():
    nc = new_nc()
    xn_tok = nc.dram_tensor("xn_tok", [T, D], BF16, kind="ExternalInput").ap()
    sel_all = nc.dram_tensor("sel_all", [128, 128 * 8], F32, kind="ExternalInput").ap()
    wg8 = nc.dram_tensor("wg8", [8, D, F0], F32, kind="ExternalInput").ap()
    wu8 = nc.dram_tensor("wu8", [8, D, F0], F32, kind="ExternalInput").ap()
    wd8 = nc.dram_tensor("wd8", [8, F0, D], F32, kind="ExternalInput").ap()
    eids = nc.dram_tensor("eids", [1, 8], F32, kind="ExternalInput").ap()
    basev = nc.dram_tensor("basev", [1, 1], F32, kind="ExternalInput").ap()

    y_o = nc.dram_tensor("y_loc", [8 * C0, D], BF16, kind="ExternalOutput").ap()
    smap_o = nc.dram_tensor("smap_p", [128, 128 * 8], F32, kind="ExternalOutput").ap()
    xbuf = nc.dram_tensor("xbuf", [8 * C0, D], BF16, kind="Internal").ap()

    with ExitStack() as ctx:
        block = None
        sel_sb = ctx.enter_context(nc.sbuf_tensor("sel_sb", [128, 128 * 8], F32))
        smap_sb = ctx.enter_context(nc.sbuf_tensor("smap_sb", [128, 128 * 8], F32))
        slot_i = ctx.enter_context(nc.sbuf_tensor("slot_i", [128, 8], I32))
        eid_b = ctx.enter_context(nc.sbuf_tensor("eid_b", [128, 8], F32))
        eidoff = ctx.enter_context(nc.sbuf_tensor("eidoff", [128, 8], F32))
        base_b = ctx.enter_context(nc.sbuf_tensor("base_b", [128, 1], F32))
        eid_s = ctx.enter_context(nc.sbuf_tensor("eid_s", [1, 8], F32))
        base_s = ctx.enter_context(nc.sbuf_tensor("base_s", [1, 1], F32))
        A = ctx.enter_context(nc.sbuf_tensor("A", [128, 8], F32))
        eq = ctx.enter_context(nc.sbuf_tensor("eq", [128, 8], F32))
        pos = ctx.enter_context(nc.sbuf_tensor("pos", [128, 8], F32))
        tmp = ctx.enter_context(nc.sbuf_tensor("tmp", [128, 8], F32))
        bad = ctx.enter_context(nc.sbuf_tensor("bad", [128, 8], F32))
        xtile = ctx.enter_context(nc.sbuf_tensor("xtile", [128, D], BF16))
        zero_t = ctx.enter_context(nc.sbuf_tensor("zero_t", [128, D], BF16))
        block = ctx.enter_context(nc.Block())

        @block.sync
        def _(sync):
            p = P(nc)
            ones_col, ones_row = mk_ones(nc, p, ctx)
            strictU = mk_strict_upper(nc, p, ctx)
            lowI = mk_lower_incl(nc, p, ctx)
            ident = mk_ident_bf(nc, p, ctx)
            breg = nc.gpsimd.alloc_register("bnd")
            nc.gpsimd.reg_mov(breg, 8 * C0 - 1)
            p.dma(sel_sb[:, :], sel_all)
            p.dma(eid_s[:, :], eids)
            p.dma(base_s[:, :], basev)
            p.g(lambda e: e.memset(zero_t[:, :], 0.0))
            for r in range(32):
                p.dma(xbuf[r * 128:(r + 1) * 128, :], zero_t[:, :])
            with ExitStack() as c2:
                ps_bc = c2.enter_context(nc.psum_tensor("ps_bcS2", [128, 8], F32))
                p.mm(ps_bc[:, :], ones_row[0:1, :], eid_s[0:1, :], start=True, stop=True)
                p.v(lambda e: e.tensor_copy(eid_b[:, :], ps_bc[:, :]))
                p.mm(ps_bc[:, 0:1], ones_row[0:1, :], base_s[0:1, :], start=True, stop=True)
                p.v(lambda e: e.tensor_copy(base_b[:, :], ps_bc[:, 0:1]))
            p.v(lambda e: e.tensor_scalar(out=eidoff[:, :], in0=eid_b[:, :], scalar1=float(C0),
                                          scalar2=None, op0=ALU.mult))
            with ExitStack() as c2:
                ps_cum = c2.enter_context(nc.psum_tensor("ps_cum", [128, 8], F32))
                for n in range(128):
                    p.v(lambda e, n=n: e.tensor_scalar(out=A[:, :], in0=eid_b[:, :],
                                                       scalar1=sel_sb[:, n * 8:n * 8 + 1],
                                                       scalar2=None, op0=ALU.is_equal))
                    for k in range(1, 8):
                        p.v(lambda e, n=n, k=k: e.tensor_scalar(out=eq[:, :], in0=eid_b[:, :],
                                                                scalar1=sel_sb[:, n * 8 + k:n * 8 + k + 1],
                                                                scalar2=None, op0=ALU.is_equal))
                        p.v(lambda e: e.tensor_add(A[:, :], A[:, :], eq[:, :]))
                    p.mm(ps_cum[:, :], strictU[:, :], A[:, :], start=(n == 0), stop=False)
                    p.v(lambda e: e.tensor_copy(pos[:, :], ps_cum[:, :]))
                    p.v(lambda e: e.tensor_scalar(out=bad[:, :], in0=pos[:, :], scalar1=float(C0),
                                                  scalar2=None, op0=ALU.is_ge))
                    p.v(lambda e: e.tensor_scalar(out=tmp[:, :], in0=A[:, :], scalar1=-1.0,
                                                  scalar2=1.0, op0=ALU.mult, op1=ALU.add))
                    p.v(lambda e: e.tensor_add(bad[:, :], bad[:, :], tmp[:, :]))
                    p.v(lambda e: e.tensor_add(tmp[:, :], eidoff[:, :], pos[:, :]))
                    p.v(lambda e, n=n: e.scalar_tensor_tensor(
                        out=smap_sb[:, n * 8:(n + 1) * 8], in0=bad[:, :], scalar=OOB, in1=tmp[:, :],
                        op0=ALU.mult, op1=ALU.add))
                    p.v(lambda e, n=n: e.tensor_scalar(out=tmp[:, :], in0=smap_sb[:, n * 8:(n + 1) * 8],
                                                       scalar1=base_b[:, 0:1], scalar2=None,
                                                       op0=ALU.subtract))
                    p.mm(ps_cum[:, :], lowI[:, :], A[:, :], start=False, stop=(n == 127))
                    p.dma(xtile[:, :], xn_tok[n * 128:(n + 1) * 128, :])
                    p.v(lambda e: e.tensor_copy(slot_i[:, :], tmp[:, :]))
                    for k in range(8):
                        p.run("gpsimd", lambda e, k=k: e.indirect_dma_start(
                            out=xbuf, out_offset=IndirectOffsetOnAxis(slot_i[:, k:k + 1], axis=0),
                            in_=xtile[:, :], in_offset=None,
                            bounds_check=breg, oob_is_err=False), inc=16, pipe=(k > 0))
            p.dma(smap_o, smap_sb[:, :])

            with ExitStack() as c2:
                wgb = c2.enter_context(nc.sbuf_tensor("wgb", [128, 8 * F0], BF16))
                wub = c2.enter_context(nc.sbuf_tensor("wub", [128, 8 * F0], BF16))
                wdb = c2.enter_context(nc.sbuf_tensor("wdb", [128, 6 * D], BF16))
                scrw = c2.enter_context(nc.sbuf_tensor("scrw", [128, D], F32))
                xb = c2.enter_context(nc.sbuf_tensor("xb", [128, 4 * D], BF16))
                xbT = c2.enter_context(nc.sbuf_tensor("xbT", [128, 8 * C0], BF16))
                gs = c2.enter_context(nc.sbuf_tensor("gs", [128, 6 * C0], BF16))
                hh = c2.enter_context(nc.sbuf_tensor("hh", [128, 6 * C0], BF16))
                yt = c2.enter_context(nc.sbuf_tensor("yt", [128, D], BF16))
                ps_b = c2.enter_context(nc.psum_tensor("ps_b", [128, 512], F32))
                ps_tr = c2.enter_context(nc.psum_tensor("ps_tr2", [128, 128], BF16))
                for le in range(8):
                    for k in range(8):
                        load_cast_bf(nc, p, wg8[le, k * 128:(k + 1) * 128, :], wgb, k * F0, F0, scrw)
                        load_cast_bf(nc, p, wu8[le, k * 128:(k + 1) * 128, :], wub, k * F0, F0, scrw)
                    for k in range(6):
                        load_cast_bf(nc, p, wd8[le, k * 128:(k + 1) * 128, :], wdb, k * D, D, scrw)
                    for tc in range(4):
                        p.dma(xb[:, tc * D:(tc + 1) * D],
                              xbuf[le * C0 + tc * 128: le * C0 + (tc + 1) * 128, :])
                    for tc in range(4):
                        for k in range(8):
                            p.run("tensor", lambda e, tc=tc, k=k: e.transpose(
                                ps_tr[:, :], xb[:, tc * D + k * 128:tc * D + (k + 1) * 128], ident[:, :]))
                            p.v(lambda e, tc=tc, k=k: e.tensor_copy(
                                xbT[:, k * C0 + tc * 128:k * C0 + (tc + 1) * 128], ps_tr[:, :]))
                    for mf in range(6):
                        for k in range(8):
                            p.mm(ps_b[:, :], wgb[:, k * F0 + mf * 128:k * F0 + (mf + 1) * 128],
                                 xbT[:, k * C0:(k + 1) * C0], start=(k == 0), stop=(k == 7))
                        p.s(lambda e, mf=mf: e.activation(out=gs[:, mf * C0:(mf + 1) * C0],
                                                          in_=ps_b[:, :], func=AF.Silu))
                        for k in range(8):
                            p.mm(ps_b[:, :], wub[:, k * F0 + mf * 128:k * F0 + (mf + 1) * 128],
                                 xbT[:, k * C0:(k + 1) * C0], start=(k == 0), stop=(k == 7))
                        p.v(lambda e, mf=mf: e.tensor_tensor(
                            hh[:, mf * C0:(mf + 1) * C0], ps_b[:, :],
                            gs[:, mf * C0:(mf + 1) * C0], ALU.mult))
                    for tc in range(4):
                        for nch in range(2):
                            for k in range(6):
                                p.mm(ps_b[:, :], hh[:, k * C0 + tc * 128:k * C0 + (tc + 1) * 128],
                                     wdb[:, k * D + nch * 512:k * D + (nch + 1) * 512],
                                     start=(k == 0), stop=(k == 5))
                            p.v(lambda e, nch=nch: e.tensor_copy(yt[:, nch * 512:(nch + 1) * 512],
                                                                 ps_b[:, :]))
                        p.dma(y_o[le * C0 + tc * 128:le * C0 + (tc + 1) * 128, :], yt[:, :])
            p.finish()
    return nc


def build_s3():
    nc = new_nc()
    accT = nc.dram_tensor("accT", [D, S], F32, kind="ExternalInput").ap()
    y_full = nc.dram_tensor("y_full", [E0 * C0, D], BF16, kind="ExternalInput").ap()
    smap_own = nc.dram_tensor("smap_own", [128, 16 * E0], F32, kind="ExternalInput").ap()
    sel_own = nc.dram_tensor("sel_own", [128, 128], F32, kind="ExternalInput").ap()
    wnorm_own = nc.dram_tensor("wnorm_own", [128, 128], F32, kind="ExternalInput").ap()

    x1T_o = nc.dram_tensor("x1T", [D, S], F32, kind="ExternalOutput").ap()
    xn1T_o = nc.dram_tensor("xn1T", [D, S], BF16, kind="ExternalOutput").ap()

    with ExitStack() as ctx:
        acc = ctx.enter_context(nc.sbuf_tensor("acc", [128, 8 * S], F32))
        xn = ctx.enter_context(nc.sbuf_tensor("xn", [128, 8 * S], BF16))
        scr1 = ctx.enter_context(nc.sbuf_tensor("scr1", [128, S], F32))
        eps_tile = ctx.enter_context(nc.sbuf_tensor("eps", [1, 1], F32))
        smap_sb = ctx.enter_context(nc.sbuf_tensor("smap_sb", [128, 16 * E0], F32))
        sel_sb = ctx.enter_context(nc.sbuf_tensor("sel_sb", [128, 128], F32))
        wn_sb = ctx.enter_context(nc.sbuf_tensor("wn_sb", [128, 128], F32))
        iota64 = ctx.enter_context(nc.sbuf_tensor("iota64", [128, E0], F32))
        oh = ctx.enter_context(nc.sbuf_tensor("oh", [128, E0], F32))
        ohs = ctx.enter_context(nc.sbuf_tensor("ohs", [128, E0], F32))
        slots = ctx.enter_context(nc.sbuf_tensor("slots", [128, 8], F32))
        slot_i = ctx.enter_context(nc.sbuf_tensor("slot_i", [128, 8], I32))
        wmask = ctx.enter_context(nc.sbuf_tensor("wmask", [128, 8], F32))
        valid = ctx.enter_context(nc.sbuf_tensor("valid", [128, 8], F32))
        gbuf = ctx.enter_context(nc.sbuf_tensor("gbuf", [128, 8 * D], BF16))
        routed = ctx.enter_context(nc.sbuf_tensor("routed", [128, D], F32))
        routed_bf = ctx.enter_context(nc.sbuf_tensor("routed_bf", [128, D], BF16))
        block = ctx.enter_context(nc.Block())

        @block.sync
        def _(sync):
            p = P(nc)
            ones_col, ones_row = mk_ones(nc, p, ctx)
            ident = mk_ident_bf(nc, p, ctx)
            p.g(lambda e: e.memset(eps_tile[:, :], 1e-5))
            p.g(lambda e: e.iota(iota64[:, :], pattern=[[1, E0]], base=0, channel_multiplier=0,
                                 allow_small_or_imprecise_dtypes=True))
            p.g(lambda e: e.memset(gbuf[:, :], 0.0))
            for f in range(8):
                p.dma(acc[:, f * S:(f + 1) * S], accT[f * 128:(f + 1) * 128, :])
            breg = nc.gpsimd.alloc_register("bnd")
            nc.gpsimd.reg_mov(breg, E0 * C0 - 1)
            p.dma(smap_sb[:, :], smap_own)
            p.dma(sel_sb[:, :], sel_own)
            p.dma(wn_sb[:, :], wnorm_own)
            with ExitStack() as c2:
                ps_tr = c2.enter_context(nc.psum_tensor("ps_tr", [128, 128], BF16))
                for n in range(16):
                    for k in range(8):
                        p.v(lambda e, n=n, k=k: e.tensor_scalar(
                            out=oh[:, :], in0=iota64[:, :], scalar1=sel_sb[:, n * 8 + k:n * 8 + k + 1],
                            scalar2=None, op0=ALU.is_equal))
                        p.v(lambda e, n=n: e.tensor_mul(ohs[:, :], oh[:, :], smap_sb[:, n * E0:(n + 1) * E0]))
                        p.v(lambda e, k=k: e.tensor_reduce(slots[:, k:k + 1], ohs[:, :], axis=AX.X, op=ALU.add))
                    p.v(lambda e: e.tensor_scalar(out=valid[:, :], in0=slots[:, :], scalar1=OOB / 2,
                                                  scalar2=None, op0=ALU.is_lt))
                    p.v(lambda e, n=n: e.tensor_mul(wmask[:, :], valid[:, :], wn_sb[:, n * 8:(n + 1) * 8]))
                    p.v(lambda e: e.tensor_copy(slot_i[:, :], slots[:, :]))
                    for k in range(8):
                        p.run("gpsimd", lambda e, k=k: e.indirect_dma_start(
                            out=gbuf[:, k * D:(k + 1) * D], out_offset=None,
                            in_=y_full, in_offset=IndirectOffsetOnAxis(slot_i[:, k:k + 1], axis=0),
                            bounds_check=breg, oob_is_err=False), inc=16, pipe=(k > 0))
                    for k in range(8):
                        if k == 0:
                            p.v(lambda e: e.tensor_scalar(out=routed[:, :], in0=gbuf[:, 0:D],
                                                          scalar1=wmask[:, 0:1], scalar2=None, op0=ALU.mult))
                        else:
                            p.v(lambda e, k=k: e.scalar_tensor_tensor(
                                out=routed[:, :], in0=gbuf[:, k * D:(k + 1) * D], scalar=wmask[:, k:k + 1],
                                in1=routed[:, :], op0=ALU.mult, op1=ALU.add))
                    p.v(lambda e: e.tensor_copy(routed_bf[:, :], routed[:, :]))
                    for f in range(8):
                        p.run("tensor", lambda e, f=f: e.transpose(
                            ps_tr[:, :], routed_bf[:, f * 128:(f + 1) * 128], ident[:, :]))
                        dest = acc[:, f * S + n * 128:f * S + (n + 1) * 128]
                        p.v(lambda e, dest=dest: e.tensor_tensor(dest, dest, ps_tr[:, :], ALU.add))
            layernorm_fm(nc, p, ctx, acc, xn, scr1, ones_col, ones_row, eps_tile)
            for f in range(8):
                p.dma(x1T_o[f * 128:(f + 1) * 128, :], acc[:, f * S:(f + 1) * S])
                p.dma(xn1T_o[f * 128:(f + 1) * 128, :], xn[:, f * S:(f + 1) * S])
            p.finish()
    return nc


def build_s3b():
    nc = new_nc()
    x1T = nc.dram_tensor("x1T", [D, S], F32, kind="ExternalInput").ap()
    xn1T = nc.dram_tensor("xn1T", [D, S], BF16, kind="ExternalInput").ap()
    qT = nc.dram_tensor("qT", [D, D], F32, kind="ExternalInput").ap()
    kT = nc.dram_tensor("kT", [D, D], F32, kind="ExternalInput").ap()
    vT = nc.dram_tensor("vT", [D, D], F32, kind="ExternalInput").ap()
    oT = nc.dram_tensor("oT", [D, D], F32, kind="ExternalInput").ap()
    wr = nc.dram_tensor("wr", [D, E1], F32, kind="ExternalInput").ap()

    acc2T_o = nc.dram_tensor("acc2T", [D, S], F32, kind="ExternalOutput").ap()
    sel1_o = nc.dram_tensor("sel1_p", [128, 32], F32, kind="ExternalOutput").ap()
    gate1_o = nc.dram_tensor("gate1_p", [128, 32], F32, kind="ExternalOutput").ap()

    with ExitStack() as ctx:
        acc = ctx.enter_context(nc.sbuf_tensor("acc", [128, 8 * S], F32))
        xn = ctx.enter_context(nc.sbuf_tensor("xn", [128, 8 * S], BF16))
        scr1 = ctx.enter_context(nc.sbuf_tensor("scr1", [128, S], F32))
        scr2 = ctx.enter_context(nc.sbuf_tensor("scr2", [128, S], F32))
        outcat = ctx.enter_context(nc.sbuf_tensor("outcat", [128, 8 * S], BF16))
        block = ctx.enter_context(nc.Block())

        @block.sync
        def _(sync):
            p = P(nc)
            ident = mk_ident_bf(nc, p, ctx)
            cmask = mk_causal_add(nc, p, ctx, "cmask", -1e30)
            for f in range(8):
                p.dma(acc[:, f * S:(f + 1) * S], x1T[f * 128:(f + 1) * 128, :])
                p.dma(xn[:, f * S:(f + 1) * S], xn1T[f * 128:(f + 1) * 128, :])
            with ExitStack() as c2:
                wqb = c2.enter_context(nc.sbuf_tensor("wqb", [128, 8 * DH], BF16))
                wkb = c2.enter_context(nc.sbuf_tensor("wkb", [128, 8 * DH], BF16))
                wvb = c2.enter_context(nc.sbuf_tensor("wvb", [128, 8 * DH], BF16))
                QT = c2.enter_context(nc.sbuf_tensor("QT", [64, S], BF16))
                KT = c2.enter_context(nc.sbuf_tensor("KT", [64, S], BF16))
                V = c2.enter_context(nc.sbuf_tensor("V", [128, 16 * DH], BF16))
                attn_bf = c2.enter_context(nc.sbuf_tensor("attn_bf", [128, S], BF16))
                attnT = c2.enter_context(nc.sbuf_tensor("attnT", [128, 256], BF16))
                mx = c2.enter_context(nc.sbuf_tensor("mx", [128, 4], F32))
                ps_s = c2.enter_context(nc.psum_tensor("ps_s", [128, S], F32))
                ps_qk = c2.enter_context(nc.psum_tensor("ps_qk", [128, 512], F32))
                ps_v = ps_qk
                ps_trA = c2.enter_context(nc.psum_tensor("ps_trA", [128, 128], BF16))
                ps_trB = c2.enter_context(nc.psum_tensor("ps_trB", [128, 128], BF16))
                ps_o = c2.enter_context(nc.psum_tensor("ps_o", [128, 128], F32))
                for h in range(NH):
                    ft, fo = h // 2, (h % 2) * 64
                    for k in range(8):
                        load_cast_bf(nc, p, qT[k * 128:(k + 1) * 128, h * DH:(h + 1) * DH],
                                     wqb, k * DH, DH, scr1)
                        load_cast_bf(nc, p, kT[k * 128:(k + 1) * 128, h * DH:(h + 1) * DH],
                                     wkb, k * DH, DH, scr1)
                        load_cast_bf(nc, p, vT[k * 128:(k + 1) * 128, h * DH:(h + 1) * DH],
                                     wvb, k * DH, DH, scr1)
                    for n in range(4):
                        for k in range(8):
                            p.mm(ps_qk[0:64, :], wqb[:, k * DH:(k + 1) * DH],
                                 xn[:, k * S + n * 512:k * S + (n + 1) * 512],
                                 start=(k == 0), stop=(k == 7))
                        p.v(lambda e, n=n: e.tensor_copy(QT[0:64, n * 512:(n + 1) * 512], ps_qk[0:64, :]))
                        for k in range(8):
                            p.mm(ps_qk[0:64, :], wkb[:, k * DH:(k + 1) * DH],
                                 xn[:, k * S + n * 512:k * S + (n + 1) * 512],
                                 start=(k == 0), stop=(k == 7))
                        p.v(lambda e, n=n: e.tensor_copy(KT[0:64, n * 512:(n + 1) * 512], ps_qk[0:64, :]))
                    for tc in range(16):
                        for k in range(8):
                            p.mm(ps_v[0:128, 0:64], xn[:, k * S + tc * 128:k * S + (tc + 1) * 128],
                                 wvb[:, k * DH:(k + 1) * DH],
                                 start=(k == 0), stop=(k == 7))
                        p.v(lambda e, tc=tc: e.tensor_copy(V[:, tc * DH:(tc + 1) * DH], ps_v[0:128, 0:64]))
                    for qi in range(16):
                        kt_len = (qi + 1) * 128
                        for st in range(0, kt_len, 512):
                            w = min(512, kt_len - st)
                            p.mm(ps_s[:, st:st + w], QT[0:64, qi * 128:(qi + 1) * 128],
                                 KT[0:64, st:st + w], start=True, stop=True)
                        p.s(lambda e, L=kt_len: e.activation(out=scr1[:, 0:L], in_=ps_s[:, 0:L],
                                                             func=AF.Copy, scale=0.125))
                        dg = scr1[:, qi * 128:(qi + 1) * 128]
                        p.v(lambda e, dg=dg: e.tensor_add(dg, dg, cmask[:, :]))
                        softmax_row(nc, p, scr1, scr2, mx, attn_bf, kt_len)
                        c_copy = {}
                        c_mm = {}
                        c_sm = p.count
                        for kj in range(qi + 1):
                            b = kj % 2
                            pst = ps_trA if b == 0 else ps_trB
                            w_tr = c_sm if kj < 2 else c_copy[kj - 2]
                            p.run("tensor", lambda e, kj=kj, pst=pst: e.transpose(
                                pst[:, :], attn_bf[:, kj * 128:(kj + 1) * 128], ident[:, :]),
                                wait=w_tr)
                            c_tr = p.count
                            w_cp = c_tr if kj < 2 else max(c_tr, c_mm[kj - 2])
                            p.run("vector", lambda e, b=b, pst=pst: e.tensor_copy(
                                attnT[:, b * 128:(b + 1) * 128], pst[:, :]), wait=w_cp)
                            c_copy[kj] = p.count
                            p.run("tensor", lambda e, kj=kj, b=b: e.matmul(
                                ps_o[0:64, :], V[:, kj * DH:(kj + 1) * DH],
                                attnT[:, b * 128:(b + 1) * 128], start=(kj == 0), stop=(kj == qi)),
                                wait=c_copy[kj])
                            c_mm[kj] = p.count
                        p.v(lambda e, qi=qi, ft=ft, fo=fo: e.tensor_copy(
                            outcat[fo:fo + 64, ft * S + qi * 128:ft * S + (qi + 1) * 128], ps_o[0:64, :]))
            with ExitStack() as c2:
                wob = c2.enter_context(nc.sbuf_tensor("wob", [128, 8 * D], BF16))
                for k in range(8):
                    load_cast_bf(nc, p, oT[k * 128:(k + 1) * 128, :], wob, k * D, D, scr1)
                ps_b = c2.enter_context(nc.psum_tensor("ps_b", [128, 512], F32))
                for mo in range(8):
                    for n in range(4):
                        for k in range(8):
                            p.mm(ps_b[:, :], wob[:, k * D + mo * 128:k * D + (mo + 1) * 128],
                                 outcat[:, k * S + n * 512:k * S + (n + 1) * 512],
                                 start=(k == 0), stop=(k == 7))
                        dest = acc[:, mo * S + n * 512:mo * S + (n + 1) * 512]
                        p.v(lambda e, dest=dest: e.tensor_tensor(dest, dest, ps_b[:, :], ALU.add))
            with ExitStack() as c2:
                wrb = c2.enter_context(nc.sbuf_tensor("wrb", [128, 8 * E1], BF16))
                scrw = c2.enter_context(nc.sbuf_tensor("scrw", [128, E1], F32))
                for k in range(8):
                    load_cast_bf(nc, p, wr[k * 128:(k + 1) * 128, :], wrb, k * E1, E1, scrw)
                sel1_p = c2.enter_context(nc.sbuf_tensor("sel1_sb", [128, 32], F32))
                gate1_p = c2.enter_context(nc.sbuf_tensor("gate1_sb", [128, 32], F32))
                probs = c2.enter_context(nc.sbuf_tensor("probs", [128, E1], F32))
                lgt = c2.enter_context(nc.sbuf_tensor("lgt", [128, E1], F32))
                mx = c2.enter_context(nc.sbuf_tensor("mx2", [128, 4], F32))
                mx8 = c2.enter_context(nc.sbuf_tensor("mx8", [128, 8], F32))
                mi8 = c2.enter_context(nc.sbuf_tensor("mi8", [128, 8], U32))
                ps_r = c2.enter_context(nc.psum_tensor("ps_r", [128, E1], F32))
                for t in range(16):
                    for k in range(8):
                        p.mm(ps_r[:, :], xn[:, k * S + t * 128:k * S + (t + 1) * 128],
                             wrb[:, k * E1:(k + 1) * E1], start=(k == 0), stop=(k == 7))
                    p.v(lambda e: e.tensor_copy(lgt[:, :], ps_r[:, :]))
                    softmax_row(nc, p, lgt, scr2, mx, probs, E1)
                    p.v(lambda e: e.max(out=mx8[:, :], in_=probs[:, 0:E1]))
                    p.v(lambda e: e.max_index(out=mi8[:, :], in_max=mx8[:, :], in_values=probs[:, 0:E1]))
                    p.v(lambda e, t=t: e.tensor_copy(gate1_p[:, t * 2:(t + 1) * 2], mx8[:, 0:2]))
                    p.v(lambda e, t=t: e.tensor_copy(sel1_p[:, t * 2:(t + 1) * 2], mi8[:, 0:2]))
                p.dma(sel1_o, sel1_p[:, :])
                p.dma(gate1_o, gate1_p[:, :])
            for f in range(8):
                p.dma(acc2T_o[f * 128:(f + 1) * 128, :], acc[:, f * S:(f + 1) * S])
            p.finish()
    return nc


def build_s4():
    nc = new_nc()
    xn1_tok = nc.dram_tensor("xn1_tok", [T, D], BF16, kind="ExternalInput").ap()
    sel1_all = nc.dram_tensor("sel1_all", [128, 128 * 2], F32, kind="ExternalInput").ap()
    wi_e = nc.dram_tensor("wi_e", [D, F1], F32, kind="ExternalInput").ap()
    wo_e = nc.dram_tensor("wo_e", [F1, D], F32, kind="ExternalInput").ap()
    eid1 = nc.dram_tensor("eid1", [1, 1], F32, kind="ExternalInput").ap()

    y1_o = nc.dram_tensor("y1_loc", [2 * C1, D], BF16, kind="ExternalOutput").ap()
    smap1_o = nc.dram_tensor("smap1_p", [128, 128 * 2], F32, kind="ExternalOutput").ap()
    xbuf1 = nc.dram_tensor("xbuf1", [2 * C1, D], BF16, kind="Internal").ap()

    with ExitStack() as ctx:
        sel_sb = ctx.enter_context(nc.sbuf_tensor("sel_sb", [128, 128 * 2], F32))
        A_sb = ctx.enter_context(nc.sbuf_tensor("A_sb", [128, 128 * 2], F32))
        pos_sb = ctx.enter_context(nc.sbuf_tensor("pos_sb", [128, 128 * 2], F32))
        smap_sb = ctx.enter_context(nc.sbuf_tensor("smap_sb", [128, 128 * 2], F32))
        slot_i = ctx.enter_context(nc.sbuf_tensor("slot_i", [128, 8], I32))
        eid_b = ctx.enter_context(nc.sbuf_tensor("eid_b", [128, 1], F32))
        eoff_b = ctx.enter_context(nc.sbuf_tensor("eoff_b", [128, 1], F32))
        eid_s = ctx.enter_context(nc.sbuf_tensor("eid_s", [1, 1], F32))
        tot_sb = ctx.enter_context(nc.sbuf_tensor("tot_sb", [1, 4], F32))
        tot0b = ctx.enter_context(nc.sbuf_tensor("tot0b", [128, 2], F32))
        pos = ctx.enter_context(nc.sbuf_tensor("pos", [128, 2], F32))
        tmp = ctx.enter_context(nc.sbuf_tensor("tmp", [128, 2], F32))
        tmp2 = ctx.enter_context(nc.sbuf_tensor("tmp2", [128, 2], F32))
        bad = ctx.enter_context(nc.sbuf_tensor("bad", [128, 2], F32))
        xtile = ctx.enter_context(nc.sbuf_tensor("xtile", [128, D], BF16))
        zero_t = ctx.enter_context(nc.sbuf_tensor("zero_t", [128, D], BF16))
        block = ctx.enter_context(nc.Block())

        @block.sync
        def _(sync):
            p = P(nc)
            ones_col, ones_row = mk_ones(nc, p, ctx)
            strictU = mk_strict_upper(nc, p, ctx)
            lowI = mk_lower_incl(nc, p, ctx)
            ident = mk_ident_bf(nc, p, ctx)
            breg = nc.gpsimd.alloc_register("bnd")
            nc.gpsimd.reg_mov(breg, 2 * C1 - 1)
            p.dma(sel_sb[:, :], sel1_all)
            p.dma(eid_s[:, :], eid1)
            p.g(lambda e: e.memset(zero_t[:, :], 0.0))
            for r in range(10):
                p.dma(xbuf1[r * 128:(r + 1) * 128, :], zero_t[:, :])
            with ExitStack() as c2:
                ps_bc = c2.enter_context(nc.psum_tensor("ps_bcS4", [128, 1], F32))
                p.mm(ps_bc[:, :], ones_row[0:1, :], eid_s[0:1, :], start=True, stop=True)
                p.v(lambda e: e.tensor_copy(eid_b[:, :], ps_bc[:, :]))
                p.v(lambda e: e.tensor_scalar(out=eoff_b[:, :], in0=eid_b[:, :],
                                              scalar1=float(2 * C1), scalar2=None, op0=ALU.mult))
                ps_cum = c2.enter_context(nc.psum_tensor("ps_cum", [128, 2], F32))
                for n in range(128):
                    for k in range(2):
                        p.v(lambda e, n=n, k=k: e.tensor_scalar(
                            out=A_sb[:, n * 2 + k:n * 2 + k + 1], in0=eid_b[:, 0:1],
                            scalar1=sel_sb[:, n * 2 + k:n * 2 + k + 1],
                            scalar2=None, op0=ALU.is_equal))
                    p.mm(ps_cum[:, :], strictU[:, :], A_sb[:, n * 2:(n + 1) * 2],
                         start=(n % 64 == 0), stop=False)
                    p.v(lambda e, n=n: e.tensor_copy(pos_sb[:, n * 2:(n + 1) * 2], ps_cum[:, :]))
                    p.mm(ps_cum[:, :], lowI[:, :], A_sb[:, n * 2:(n + 1) * 2],
                         start=False, stop=(n % 64 == 63))
                    if n % 64 == 63:
                        g = n // 64
                        p.v(lambda e, g=g: e.tensor_copy(tot_sb[0:1, g * 2:(g + 1) * 2], ps_cum[0:1, :]))
                for g in range(2):
                    p.mm(ps_bc[:, :], ones_row[0:1, :], tot_sb[0:1, g * 2:g * 2 + 1], start=True, stop=True)
                    p.v(lambda e, g=g: e.tensor_copy(tot0b[:, g:g + 1], ps_bc[:, :]))
                for n in range(128):
                    g = n // 64
                    p.v(lambda e, n=n: e.tensor_copy(pos[:, :], pos_sb[:, n * 2:(n + 1) * 2]))
                    p.v(lambda e, g=g: e.tensor_scalar(out=pos[:, 1:2], in0=pos[:, 1:2],
                                                       scalar1=tot0b[:, g:g + 1], scalar2=None,
                                                       op0=ALU.add))
                    p.v(lambda e: e.tensor_scalar(out=bad[:, :], in0=pos[:, :], scalar1=float(C1),
                                                  scalar2=None, op0=ALU.is_ge))
                    p.v(lambda e, n=n: e.tensor_scalar(out=tmp[:, :], in0=A_sb[:, n * 2:(n + 1) * 2],
                                                       scalar1=-1.0, scalar2=1.0,
                                                       op0=ALU.mult, op1=ALU.add))
                    p.v(lambda e: e.tensor_add(bad[:, :], bad[:, :], tmp[:, :]))
                    p.v(lambda e, g=g: e.tensor_scalar(out=tmp[:, :], in0=pos[:, :],
                                                       scalar1=float(g * C1), scalar2=None, op0=ALU.add))
                    p.v(lambda e: e.scalar_tensor_tensor(out=tmp2[:, :], in0=bad[:, :], scalar=OOB,
                                                         in1=tmp[:, :], op0=ALU.mult, op1=ALU.add))
                    p.v(lambda e, n=n: e.tensor_scalar(out=smap_sb[:, n * 2:(n + 1) * 2], in0=tmp2[:, :],
                                                       scalar1=eoff_b[:, 0:1], scalar2=None, op0=ALU.add))
                    p.dma(xtile[:, :], xn1_tok[n * 128:(n + 1) * 128, :])
                    p.v(lambda e: e.tensor_copy(slot_i[:, 0:2], tmp2[:, :]))
                    for k in range(2):
                        p.run("gpsimd", lambda e, k=k: e.indirect_dma_start(
                            out=xbuf1, out_offset=IndirectOffsetOnAxis(slot_i[:, k:k + 1], axis=0),
                            in_=xtile[:, :], in_offset=None,
                            bounds_check=breg, oob_is_err=False), inc=16, pipe=(k > 0))
            p.dma(smap1_o, smap_sb[:, :])

            with ExitStack() as c2:
                xbT = c2.enter_context(nc.sbuf_tensor("xbT", [128, 2 * 8 * C1], BF16))
                hh = c2.enter_context(nc.sbuf_tensor("hh", [128, 2 * 32 * C1], BF16))
                with ExitStack() as c3:
                    xb = c3.enter_context(nc.sbuf_tensor("xb", [128, 5 * D], BF16))
                    ps_tr = c3.enter_context(nc.psum_tensor("ps_tr2", [128, 128], BF16))
                    for g in range(2):
                        for tc in range(5):
                            p.dma(xb[:, tc * D:(tc + 1) * D],
                                  xbuf1[g * C1 + tc * 128:g * C1 + (tc + 1) * 128, :])
                        for tc in range(5):
                            for k in range(8):
                                p.run("tensor", lambda e, tc=tc, k=k: e.transpose(
                                    ps_tr[:, :], xb[:, tc * D + k * 128:tc * D + (k + 1) * 128],
                                    ident[:, :]))
                                p.v(lambda e, g=g, tc=tc, k=k: e.tensor_copy(
                                    xbT[:, g * 8 * C1 + k * C1 + tc * 128:
                                        g * 8 * C1 + k * C1 + (tc + 1) * 128], ps_tr[:, :]))
                with ExitStack() as c3:
                    wib = c3.enter_context(nc.sbuf_tensor("wib", [128, 8 * F1], BF16))
                    scrw = c3.enter_context(nc.sbuf_tensor("scrw", [128, 2048], F32))
                    ps_b = c3.enter_context(nc.psum_tensor("ps_b", [128, 512], F32))
                    for k in range(8):
                        for half in range(2):
                            load_cast_bf(nc, p, wi_e[k * 128:(k + 1) * 128, half * 2048:(half + 1) * 2048],
                                         wib, k * F1 + half * 2048, 2048, scrw)
                    for g in range(2):
                        for mf in range(32):
                            for (n0, w) in ((0, 512), (512, 128)):
                                for k in range(8):
                                    p.mm(ps_b[:, 0:w], wib[:, k * F1 + mf * 128:k * F1 + (mf + 1) * 128],
                                         xbT[:, g * 8 * C1 + k * C1 + n0:g * 8 * C1 + k * C1 + n0 + w],
                                         start=(k == 0), stop=(k == 7))
                                p.s(lambda e, g=g, mf=mf, n0=n0, w=w: e.activation(
                                    out=hh[:, g * 32 * C1 + mf * C1 + n0:g * 32 * C1 + mf * C1 + n0 + w],
                                    in_=ps_b[:, 0:w], func=AF.Relu))
                with ExitStack() as c3:
                    wob = c3.enter_context(nc.sbuf_tensor("wob", [128, 32 * D], BF16))
                    scrw = c3.enter_context(nc.sbuf_tensor("scrw2", [128, D], F32))
                    yt = c3.enter_context(nc.sbuf_tensor("yt", [128, D], BF16))
                    ps_b = c3.enter_context(nc.psum_tensor("ps_b2", [128, 512], F32))
                    for k in range(32):
                        load_cast_bf(nc, p, wo_e[k * 128:(k + 1) * 128, :], wob, k * D, D, scrw)
                    for g in range(2):
                        for tc in range(5):
                            for nch in range(2):
                                for k in range(32):
                                    p.mm(ps_b[:, :],
                                         hh[:, g * 32 * C1 + k * C1 + tc * 128:
                                            g * 32 * C1 + k * C1 + (tc + 1) * 128],
                                         wob[:, k * D + nch * 512:k * D + (nch + 1) * 512],
                                         start=(k == 0), stop=(k == 31))
                                p.v(lambda e, nch=nch: e.tensor_copy(yt[:, nch * 512:(nch + 1) * 512],
                                                                     ps_b[:, :]))
                            p.dma(y1_o[g * C1 + tc * 128:g * C1 + (tc + 1) * 128, :], yt[:, :])
            p.finish()
    return nc


def build_s5():
    nc = new_nc()
    acc2T = nc.dram_tensor("acc2T", [D, S], F32, kind="ExternalInput").ap()
    y1_full = nc.dram_tensor("y1_full", [E1 * 2 * C1, D], BF16, kind="ExternalInput").ap()
    smap1_own = nc.dram_tensor("smap1_own", [128, 16 * 2 * E1], F32, kind="ExternalInput").ap()
    sel1_own = nc.dram_tensor("sel1_own", [128, 32], F32, kind="ExternalInput").ap()
    gate1_own = nc.dram_tensor("gate1_own", [128, 32], F32, kind="ExternalInput").ap()

    outT_o = nc.dram_tensor("outT", [D, S], F32, kind="ExternalOutput").ap()

    with ExitStack() as ctx:
        acc = ctx.enter_context(nc.sbuf_tensor("acc", [128, 8 * S], F32))
        smap_sb = ctx.enter_context(nc.sbuf_tensor("smap_sb", [128, 16 * 2 * E1], F32))
        sel_sb = ctx.enter_context(nc.sbuf_tensor("sel_sb", [128, 32], F32))
        gt_sb = ctx.enter_context(nc.sbuf_tensor("gt_sb", [128, 32], F32))
        iota8 = ctx.enter_context(nc.sbuf_tensor("iota8", [128, E1], F32))
        oh = ctx.enter_context(nc.sbuf_tensor("oh", [128, E1], F32))
        ohs = ctx.enter_context(nc.sbuf_tensor("ohs", [128, E1], F32))
        slots = ctx.enter_context(nc.sbuf_tensor("slots", [128, 2], F32))
        slot_i = ctx.enter_context(nc.sbuf_tensor("slot_i", [128, 2], I32))
        wmask = ctx.enter_context(nc.sbuf_tensor("wmask", [128, 2], F32))
        valid = ctx.enter_context(nc.sbuf_tensor("valid", [128, 2], F32))
        gbuf = ctx.enter_context(nc.sbuf_tensor("gbuf", [128, 8 * D], BF16))
        routed = ctx.enter_context(nc.sbuf_tensor("routed", [128, D], F32))
        routed_bf = ctx.enter_context(nc.sbuf_tensor("routed_bf", [128, D], BF16))
        block = ctx.enter_context(nc.Block())

        @block.sync
        def _(sync):
            p = P(nc)
            ident = mk_ident_bf(nc, p, ctx)
            p.g(lambda e: e.iota(iota8[:, :], pattern=[[1, E1]], base=0, channel_multiplier=0,
                                 allow_small_or_imprecise_dtypes=True))
            p.g(lambda e: e.memset(gbuf[:, :], 0.0))
            for f in range(8):
                p.dma(acc[:, f * S:(f + 1) * S], acc2T[f * 128:(f + 1) * 128, :])
            breg = nc.gpsimd.alloc_register("bnd")
            nc.gpsimd.reg_mov(breg, E1 * 2 * C1 - 1)
            p.dma(smap_sb[:, :], smap1_own)
            p.dma(sel_sb[:, :], sel1_own)
            p.dma(gt_sb[:, :], gate1_own)
            with ExitStack() as c2:
                ps_tr = c2.enter_context(nc.psum_tensor("ps_tr", [128, 128], BF16))
                for n in range(16):
                    for k in range(2):
                        p.v(lambda e, n=n, k=k: e.tensor_scalar(
                            out=oh[:, :], in0=iota8[:, :], scalar1=sel_sb[:, n * 2 + k:n * 2 + k + 1],
                            scalar2=None, op0=ALU.is_equal))
                        p.v(lambda e, n=n, k=k: e.tensor_mul(
                            ohs[:, :], oh[:, :], smap_sb[:, n * 16 + k * 8:n * 16 + (k + 1) * 8]))
                        p.v(lambda e, k=k: e.tensor_reduce(slots[:, k:k + 1], ohs[:, :], axis=AX.X, op=ALU.add))
                    p.v(lambda e: e.tensor_scalar(out=valid[:, :], in0=slots[:, :], scalar1=OOB / 2,
                                                  scalar2=None, op0=ALU.is_lt))
                    p.v(lambda e, n=n: e.tensor_mul(wmask[:, :], valid[:, :], gt_sb[:, n * 2:(n + 1) * 2]))
                    p.v(lambda e: e.tensor_copy(slot_i[:, :], slots[:, :]))
                    for k in range(2):
                        p.run("gpsimd", lambda e, k=k: e.indirect_dma_start(
                            out=gbuf[:, k * D:(k + 1) * D], out_offset=None,
                            in_=y1_full, in_offset=IndirectOffsetOnAxis(slot_i[:, k:k + 1], axis=0),
                            bounds_check=breg, oob_is_err=False), inc=16, pipe=(k > 0))
                    for k in range(2):
                        if k == 0:
                            p.v(lambda e: e.tensor_scalar(out=routed[:, :], in0=gbuf[:, 0:D],
                                                          scalar1=wmask[:, 0:1], scalar2=None, op0=ALU.mult))
                        else:
                            p.v(lambda e, k=k: e.scalar_tensor_tensor(
                                out=routed[:, :], in0=gbuf[:, k * D:(k + 1) * D], scalar=wmask[:, k:k + 1],
                                in1=routed[:, :], op0=ALU.mult, op1=ALU.add))
                    p.v(lambda e: e.tensor_copy(routed_bf[:, :], routed[:, :]))
                    for f in range(8):
                        p.run("tensor", lambda e, f=f: e.transpose(
                            ps_tr[:, :], routed_bf[:, f * 128:(f + 1) * 128], ident[:, :]))
                        dest = acc[:, f * S + n * 128:f * S + (n + 1) * 128]
                        p.v(lambda e, dest=dest: e.tensor_tensor(dest, dest, ps_tr[:, :], ALU.add))
            for f in range(8):
                p.dma(outT_o[f * 128:(f + 1) * 128, :], acc[:, f * S:(f + 1) * S])
            p.finish()
    return nc


RG = [[0, 1, 2, 3, 4, 5, 6, 7]]


def build_fused():
    nc = new_nc()
    xT = nc.dram_tensor("xT", [D, S], F32, kind="ExternalInput").ap()
    wq = nc.dram_tensor("wq", [NH, DH, DH], F32, kind="ExternalInput").ap()
    wk = nc.dram_tensor("wk", [NH, DH, DH], F32, kind="ExternalInput").ap()
    wv = nc.dram_tensor("wv", [NH, DH, DH], F32, kind="ExternalInput").ap()
    wo = nc.dram_tensor("wo", [D, D], F32, kind="ExternalInput").ap()
    wsgu = nc.dram_tensor("wsgu", [D, 2 * SF], F32, kind="ExternalInput").ap()
    wsd = nc.dram_tensor("wsd", [SF, D], F32, kind="ExternalInput").ap()
    rwT = nc.dram_tensor("rwT", [D, E0], F32, kind="ExternalInput").ap()
    qT = nc.dram_tensor("qT", [D, D], F32, kind="ExternalInput").ap()
    kT = nc.dram_tensor("kT", [D, D], F32, kind="ExternalInput").ap()
    vT = nc.dram_tensor("vT", [D, D], F32, kind="ExternalInput").ap()
    oT = nc.dram_tensor("oT", [D, D], F32, kind="ExternalInput").ap()
    wr = nc.dram_tensor("wr", [D, E1], F32, kind="ExternalInput").ap()
    wg8 = nc.dram_tensor("wg8", [8, D, F0], F32, kind="ExternalInput").ap()
    wu8 = nc.dram_tensor("wu8", [8, D, F0], F32, kind="ExternalInput").ap()
    wd8 = nc.dram_tensor("wd8", [8, F0, D], F32, kind="ExternalInput").ap()
    eids = nc.dram_tensor("eids", [1, 8], F32, kind="ExternalInput").ap()
    basev = nc.dram_tensor("basev", [1, 1], F32, kind="ExternalInput").ap()
    wi_e = nc.dram_tensor("wi_e", [D, F1], F32, kind="ExternalInput").ap()
    wo_e = nc.dram_tensor("wo_e", [F1, D], F32, kind="ExternalInput").ap()
    eid1 = nc.dram_tensor("eid1", [1, 1], F32, kind="ExternalInput").ap()
    cvec = nc.dram_tensor("cvec", [1, 1], I32, kind="ExternalInput").ap()
    outT_o = nc.dram_tensor("outT", [D, S], F32, kind="ExternalOutput").ap()
    xtok_m = nc.dram_tensor("xtok_m", [S, D], BF16, kind="Internal").ap()
    xtok_all = nc.dram_tensor("xtok_all", [T, D], BF16, kind="Internal").ap()
    sel_m_d = nc.dram_tensor("sel_m_d", [128, 128], F32, kind="Internal").ap()
    sel_all_d = nc.dram_tensor("sel_all_d", [1024, 128], F32, kind="Internal").ap()
    smap_m_d = nc.dram_tensor("smap_m_d", [128, 1024], F32, kind="Internal").ap()
    smap_all_d = nc.dram_tensor("smap_all_d", [1024, 1024], F32, kind="Internal").ap()
    y_m_d = nc.dram_tensor("y_m_d", [8 * C0, D], BF16, kind="Internal").ap()
    y_all_d = nc.dram_tensor("y_all_d", [E0 * C0, D], BF16, kind="Internal").ap()
    xbuf = nc.dram_tensor("xbuf", [8 * C0, D], BF16, kind="Internal").ap()
    xtok1_m = nc.dram_tensor("xtok1_m", [S, D], BF16, kind="Internal").ap()
    xtok1_all = nc.dram_tensor("xtok1_all", [T, D], BF16, kind="Internal").ap()
    sel1_m_d = nc.dram_tensor("sel1_m_d", [128, 32], F32, kind="Internal").ap()
    sel1_all_d = nc.dram_tensor("sel1_all_d", [1024, 32], F32, kind="Internal").ap()
    smap1_m_d = nc.dram_tensor("smap1_m_d", [128, 256], F32, kind="Internal").ap()
    smap1_all_d = nc.dram_tensor("smap1_all_d", [1024, 256], F32, kind="Internal").ap()
    y1_m_d = nc.dram_tensor("y1_m_d", [2 * C1, D], BF16, kind="Internal").ap()
    y1_all_d = nc.dram_tensor("y1_all_d", [E1 * 2 * C1, D], BF16, kind="Internal").ap()
    xbuf1 = nc.dram_tensor("xbuf1", [2 * C1, D], BF16, kind="Internal").ap()

    with ExitStack() as ctx:
        acc = ctx.enter_context(nc.sbuf_tensor("acc", [128, 8 * S], F32))
        eps_tile = ctx.enter_context(nc.sbuf_tensor("eps", [1, 1], F32))
        sel_p = ctx.enter_context(nc.sbuf_tensor("sel_p_sb", [128, 128], F32))
        wnorm_p = ctx.enter_context(nc.sbuf_tensor("wnorm_p_sb", [128, 128], F32))
        sel1_p = ctx.enter_context(nc.sbuf_tensor("sel1_p_sb", [128, 32], F32))
        gate1_p = ctx.enter_context(nc.sbuf_tensor("gate1_p_sb", [128, 32], F32))
        block = ctx.enter_context(nc.Block())

        @block.sync
        def _(sync):
            p = P(nc)
            ident = mk_ident_bf(nc, p, ctx)
            cmask9 = mk_causal_add(nc, p, ctx, "cmask9", -1e9)
            cmask30 = mk_causal_add(nc, p, ctx, "cmask30", -1e30)
            ones_col, ones_row = mk_ones(nc, p, ctx)
            strictU = mk_strict_upper(nc, p, ctx)
            lowI = mk_lower_incl(nc, p, ctx)
            iota64 = ctx.enter_context(nc.sbuf_tensor("iota64", [128, E0], F32))
            iota8 = ctx.enter_context(nc.sbuf_tensor("iota8", [128, E1], F32))
            p.g(lambda e: e.iota(iota64[:, :], pattern=[[1, E0]], base=0, channel_multiplier=0,
                                 allow_small_or_imprecise_dtypes=True))
            p.g(lambda e: e.iota(iota8[:, :], pattern=[[1, E1]], base=0, channel_multiplier=0,
                                 allow_small_or_imprecise_dtypes=True))
            p.g(lambda e: e.memset(eps_tile[:, :], 1e-5))
            breg0 = nc.gpsimd.alloc_register("b0")
            breg1 = nc.gpsimd.alloc_register("b1")
            breg2 = nc.gpsimd.alloc_register("b2")
            breg3 = nc.gpsimd.alloc_register("b3")
            cvec_sb = ctx.enter_context(nc.sbuf_tensor("cvec_sb", [1, 1], I32))
            p.dma(cvec_sb[:, :], cvec)
            nc.sync.wait_ge(p.sem, p.count)
            cval = nc.sync.value_load(cvec_sb[0:1, 0:1], min_val=0, max_val=7)
            nc.gpsimd.reg_mov(breg0, 8 * C0 - 1)
            nc.gpsimd.reg_mov(breg1, E0 * C0 - 1)
            nc.gpsimd.reg_mov(breg2, 2 * C1 - 1)
            nc.gpsimd.reg_mov(breg3, E1 * 2 * C1 - 1)

            with ExitStack() as ctx_a:
                xn = ctx_a.enter_context(nc.sbuf_tensor("xn", [128, 8 * S], BF16))
                scr1 = ctx_a.enter_context(nc.sbuf_tensor("scr1", [128, S], F32))
                scr2 = ctx_a.enter_context(nc.sbuf_tensor("scr2", [128, S], F32))
                for f in range(8):
                    p.dma(acc[:, f * S:(f + 1) * S], xT[f * 128:(f + 1) * 128, :])
                layernorm_fm(nc, p, ctx_a, acc, xn, scr1, ones_col, ones_row, eps_tile, tag="L0")

                with ExitStack() as c2:
                    xtok_sb = c2.enter_context(nc.sbuf_tensor("xtok_sb", [128, D], BF16))
                    ps_tr = c2.enter_context(nc.psum_tensor("ps_trA", [128, 128], BF16))
                    for nl in range(16):
                        for f in range(8):
                            p.run("tensor", lambda e, nl=nl, f=f: e.transpose(
                                ps_tr[:, :], xn[:, f * S + nl * 128:f * S + (nl + 1) * 128], ident[:, :]))
                            p.v(lambda e, f=f: e.tensor_copy(xtok_sb[:, f * 128:(f + 1) * 128], ps_tr[:, :]))
                        p.dma(xtok_m[nl * 128:(nl + 1) * 128, :], xtok_sb[:, :])
                p.run("gpsimd", lambda e: e.collective_compute(
                    "AllGather", ALU.bypass, RG, [xtok_m], [xtok_all]), inc=16)

                with ExitStack() as c2:
                    rw_bf = c2.enter_context(nc.sbuf_tensor("rw_bf", [128, 8 * E0], BF16))
                    scrw = c2.enter_context(nc.sbuf_tensor("scrwR", [128, E0], F32))
                    for k in range(8):
                        load_cast_bf(nc, p, rwT[k * 128:(k + 1) * 128, :], rw_bf, k * E0, E0, scrw)
                    probs = c2.enter_context(nc.sbuf_tensor("probsR", [128, E0], F32))
                    lgt = c2.enter_context(nc.sbuf_tensor("lgtR", [128, E0], F32))
                    mx = c2.enter_context(nc.sbuf_tensor("mxR", [128, 4], F32))
                    mx8 = c2.enter_context(nc.sbuf_tensor("mx8R", [128, 8], F32))
                    mi8 = c2.enter_context(nc.sbuf_tensor("mi8R", [128, 8], U32))
                    ps_r = c2.enter_context(nc.psum_tensor("ps_rR", [128, E0], F32))
                    for t in range(16):
                        for k in range(8):
                            p.mm(ps_r[:, :], xn[:, k * S + t * 128:k * S + (t + 1) * 128],
                                 rw_bf[:, k * E0:(k + 1) * E0], start=(k == 0), stop=(k == 7))
                        p.v(lambda e: e.tensor_copy(lgt[:, :], ps_r[:, :]))
                        softmax_row(nc, p, lgt, scr2, mx, probs, E0)
                        p.v(lambda e: e.max(out=mx8[:, :], in_=probs[:, 0:E0]))
                        p.v(lambda e: e.max_index(out=mi8[:, :], in_max=mx8[:, :], in_values=probs[:, 0:E0]))
                        p.v(lambda e: e.tensor_reduce(mx[:, 0:1], mx8[:, :], axis=AX.X, op=ALU.add))
                        p.v(lambda e: e.reciprocal(mx[:, 1:2], mx[:, 0:1]))
                        p.v(lambda e, t=t: e.tensor_scalar(out=wnorm_p[:, t * 8:(t + 1) * 8], in0=mx8[:, :],
                                                           scalar1=mx[:, 1:2], scalar2=None, op0=ALU.mult))
                        p.v(lambda e, t=t: e.tensor_copy(sel_p[:, t * 8:(t + 1) * 8], mi8[:, :]))
                p.dma(sel_m_d, sel_p[:, :])
                p.run("gpsimd", lambda e: e.collective_compute(
                    "AllGather", ALU.bypass, RG, [sel_m_d], [sel_all_d]), inc=16)

                with ExitStack() as c2m:
                  M = c2m.enter_context(nc.sbuf_tensor("M", [128, 16 * D], BF16))
                  with ExitStack() as c2:
                    wq_bf = c2.enter_context(nc.sbuf_tensor("wq_bf", [128, NH * DH], BF16))
                    wk_bf = c2.enter_context(nc.sbuf_tensor("wk_bf", [128, NH * DH], BF16))
                    wv_bf = c2.enter_context(nc.sbuf_tensor("wv_bf", [128, NH * DH], BF16))
                    wf32 = c2.enter_context(nc.sbuf_tensor("wf32", [128, NH * DH], F32))
                    for name, w_ap, dst in (("q", wq, wq_bf), ("k", wk, wk_bf), ("v", wv, wv_bf)):
                        for h in range(NH):
                            p.dma(wf32[0:64, h * DH:(h + 1) * DH], w_ap[h])
                            p.dma(wf32[64:128, h * DH:(h + 1) * DH], w_ap[h])
                        p.v(lambda e, dst=dst: e.tensor_copy(dst[:, :], wf32[:, :]))
                    QT = c2.enter_context(nc.sbuf_tensor("QT", [64, S], BF16))
                    KT = c2.enter_context(nc.sbuf_tensor("KT", [64, S], BF16))
                    V = c2.enter_context(nc.sbuf_tensor("V", [128, 16 * DH], BF16))
                    attn_bf = c2.enter_context(nc.sbuf_tensor("attn_bf", [128, S], BF16))
                    attnT = c2.enter_context(nc.sbuf_tensor("attnT", [128, 128], BF16))
                    mx = c2.enter_context(nc.sbuf_tensor("mx", [128, 4], F32))
                    ps_s = c2.enter_context(nc.psum_tensor("ps_s", [128, S], F32))
                    ps_qk = c2.enter_context(nc.psum_tensor("ps_qk", [128, 512], F32))
                    ps_v = c2.enter_context(nc.psum_tensor("ps_v", [128, 64], F32))
                    ps_tr = c2.enter_context(nc.psum_tensor("ps_tr", [128, 128], BF16))
                    ps_o = c2.enter_context(nc.psum_tensor("ps_o", [128, 64], F32))
                    for h in range(NH):
                        ft, fo = h // 2, (h % 2) * 64
                        xh = lambda a, b: xn[fo:fo + 64, ft * S + a: ft * S + b]
                        for n in range(4):
                            p.mm(ps_qk[0:64, :], wq_bf[fo:fo + 64, h * DH:(h + 1) * DH],
                                 xh(n * 512, (n + 1) * 512), start=True, stop=True)
                            p.v(lambda e, n=n: e.tensor_copy(QT[0:64, n * 512:(n + 1) * 512], ps_qk[0:64, :]))
                            p.mm(ps_qk[0:64, :], wk_bf[fo:fo + 64, h * DH:(h + 1) * DH],
                                 xh(n * 512, (n + 1) * 512), start=True, stop=True)
                            p.v(lambda e, n=n: e.tensor_copy(KT[0:64, n * 512:(n + 1) * 512], ps_qk[0:64, :]))
                        for tc in range(16):
                            p.mm(ps_v[0:128, 0:64], xh(tc * 128, (tc + 1) * 128),
                                 wv_bf[fo:fo + 64, h * DH:(h + 1) * DH], start=True, stop=True)
                            p.v(lambda e, tc=tc: e.tensor_copy(V[:, tc * DH:(tc + 1) * DH], ps_v[0:128, 0:64]))
                        for qi in range(16):
                            kt_len = (qi + 1) * 128
                            for st in range(0, kt_len, 512):
                                w = min(512, kt_len - st)
                                p.mm(ps_s[:, st:st + w], QT[0:64, qi * 128:(qi + 1) * 128],
                                     KT[0:64, st:st + w], start=True, stop=True)
                            p.s(lambda e, L=kt_len: e.activation(out=scr1[:, 0:L], in_=ps_s[:, 0:L],
                                                                 func=AF.Copy, scale=0.125))
                            dg = scr1[:, qi * 128:(qi + 1) * 128]
                            p.v(lambda e, dg=dg: e.tensor_add(dg, dg, cmask9[:, :]))
                            softmax_row(nc, p, scr1, scr2, mx, attn_bf, kt_len)
                            for kj in range(qi + 1):
                                p.run("tensor", lambda e, kj=kj: e.transpose(
                                    ps_tr[:, :], attn_bf[:, kj * 128:(kj + 1) * 128], ident[:, :]))
                                p.v(lambda e: e.tensor_copy(attnT[:, :], ps_tr[:, :]))
                                p.mm(ps_o[0:128, 0:64], attnT[:, :], V[:, kj * DH:(kj + 1) * DH],
                                     start=(kj == 0), stop=(kj == qi))
                            p.v(lambda e, qi=qi, h=h: e.tensor_copy(
                                M[:, qi * D + h * DH: qi * D + (h + 1) * DH], ps_o[0:128, 0:64]))
                  with ExitStack() as c2:
                    wo_bf = c2.enter_context(nc.sbuf_tensor("wo_bf", [128, 8 * D], BF16))
                    for k in range(8):
                        load_cast_bf(nc, p, wo[k * 128:(k + 1) * 128, :], wo_bf, k * D, D, scr1)
                    ps_b = c2.enter_context(nc.psum_tensor("ps_b", [128, 512], F32))
                    for j0 in range(2):
                        for mo in range(8):
                            for rc in range(2):
                                for k in range(8):
                                    p.mm(ps_b[:, :], wo_bf[:, k * D + mo * 128:k * D + (mo + 1) * 128],
                                         M[:, (8 * j0 + k) * D + rc * 512:(8 * j0 + k) * D + (rc + 1) * 512],
                                         start=(k == 0), stop=(k == 7))
                                dest = acc[:, mo * S:(mo + 1) * S].rearrange(
                                    "p (r two) -> p r two", two=2)[:, rc * 512:(rc + 1) * 512, j0]
                                p.v(lambda e, dest=dest: e.tensor_tensor(dest, dest, ps_b[:, :], ALU.add))

                with ExitStack() as c2:
                    wsgu_bf = c2.enter_context(nc.sbuf_tensor("wsgu_bf", [128, 8 * 2 * SF], BF16))
                    wsd_bf = c2.enter_context(nc.sbuf_tensor("wsd_bf", [128, 6 * D], BF16))
                    scrw = c2.enter_context(nc.sbuf_tensor("scrwS", [128, 2 * SF], F32))
                    for k in range(8):
                        load_cast_bf(nc, p, wsgu[k * 128:(k + 1) * 128, :], wsgu_bf, k * 2 * SF, 2 * SF, scrw)
                    for k in range(6):
                        load_cast_bf(nc, p, wsd[k * 128:(k + 1) * 128, :], wsd_bf, k * D, D, scrw)
                    hT = c2.enter_context(nc.sbuf_tensor("hT", [128, 6 * S], BF16))
                    ps_b = c2.enter_context(nc.psum_tensor("ps_b2", [128, 512], F32))
                    for m in range(12):
                        for n in range(4):
                            for k in range(8):
                                p.mm(ps_b[:, :], wsgu_bf[:, k * 2 * SF + m * 128:k * 2 * SF + (m + 1) * 128],
                                     xn[:, k * S + n * 512:k * S + (n + 1) * 512],
                                     start=(k == 0), stop=(k == 7))
                            if m < 6:
                                p.s(lambda e, m=m, n=n: e.activation(
                                    out=hT[:, m * S + n * 512:m * S + (n + 1) * 512],
                                    in_=ps_b[:, :], func=AF.Silu))
                            else:
                                dst = hT[:, (m - 6) * S + n * 512:(m - 6) * S + (n + 1) * 512]
                                p.v(lambda e, dst=dst: e.tensor_tensor(dst, ps_b[:, :], dst, ALU.mult))
                    for mo in range(8):
                        for n in range(4):
                            for k in range(6):
                                p.mm(ps_b[:, :], wsd_bf[:, k * D + mo * 128:k * D + (mo + 1) * 128],
                                     hT[:, k * S + n * 512:k * S + (n + 1) * 512],
                                     start=(k == 0), stop=(k == 5))
                            dest = acc[:, mo * S + n * 512:mo * S + (n + 1) * 512]
                            p.v(lambda e, dest=dest: e.tensor_tensor(dest, dest, ps_b[:, :], ALU.add))

                with ExitStack() as c2s:
                    slot_i = c2s.enter_context(nc.sbuf_tensor("slot_i", [128, 1], I32))
                    eid_b = c2s.enter_context(nc.sbuf_tensor("eid_b", [128, 8], F32))
                    eidoff = c2s.enter_context(nc.sbuf_tensor("eidoff", [128, 8], F32))
                    base_b = c2s.enter_context(nc.sbuf_tensor("base_b", [128, 1], F32))
                    eid_s = c2s.enter_context(nc.sbuf_tensor("eid_s", [1, 8], F32))
                    base_s = c2s.enter_context(nc.sbuf_tensor("base_s", [1, 1], F32))
                    A = c2s.enter_context(nc.sbuf_tensor("A", [128, 8], F32))
                    eq = c2s.enter_context(nc.sbuf_tensor("eq", [128, 8], F32))
                    pos = c2s.enter_context(nc.sbuf_tensor("pos", [128, 8], F32))
                    tmp = c2s.enter_context(nc.sbuf_tensor("tmp", [128, 8], F32))
                    bad = c2s.enter_context(nc.sbuf_tensor("bad", [128, 8], F32))
                    xtile = c2s.enter_context(nc.sbuf_tensor("xtile", [128, D], BF16))
                    zero_t = c2s.enter_context(nc.sbuf_tensor("zero_t", [128, D], BF16))
                    smap_sb = c2s.enter_context(nc.sbuf_tensor("smap_sb", [128, 128 * 8], F32))
                    with ExitStack() as c2:
                        sel_sb = c2.enter_context(nc.sbuf_tensor("sel_sbX", [128, 128 * 8], F32))
                        p.dma(sel_sb[:, :], sel_all_d.rearrange("(r p) c -> p r c", p=128))
                        p.dma(eid_s[:, :], eids)
                        p.dma(base_s[:, :], basev)
                        p.g(lambda e: e.memset(zero_t[:, :], 0.0))
                        for r in range(32):
                            p.dma(xbuf[r * 128:(r + 1) * 128, :], zero_t[:, :])
                        ps_bc = c2.enter_context(nc.psum_tensor("ps_bcF2", [128, 8], F32))
                        p.mm(ps_bc[:, :], ones_row[0:1, :], eid_s[0:1, :], start=True, stop=True)
                        p.v(lambda e: e.tensor_copy(eid_b[:, :], ps_bc[:, :]))
                        p.mm(ps_bc[:, 0:1], ones_row[0:1, :], base_s[0:1, :], start=True, stop=True)
                        p.v(lambda e: e.tensor_copy(base_b[:, :], ps_bc[:, 0:1]))
                        p.v(lambda e: e.tensor_scalar(out=eidoff[:, :], in0=eid_b[:, :], scalar1=float(C0),
                                                      scalar2=None, op0=ALU.mult))
                        ps_cum = c2.enter_context(nc.psum_tensor("ps_cum", [128, 8], F32))
                        for n in range(128):
                            p.v(lambda e, n=n: e.tensor_scalar(out=A[:, :], in0=eid_b[:, :],
                                                               scalar1=sel_sb[:, n * 8:n * 8 + 1],
                                                               scalar2=None, op0=ALU.is_equal))
                            for k in range(1, 8):
                                p.v(lambda e, n=n, k=k: e.tensor_scalar(out=eq[:, :], in0=eid_b[:, :],
                                                                        scalar1=sel_sb[:, n * 8 + k:n * 8 + k + 1],
                                                                        scalar2=None, op0=ALU.is_equal))
                                p.v(lambda e: e.tensor_add(A[:, :], A[:, :], eq[:, :]))
                            p.mm(ps_cum[:, :], strictU[:, :], A[:, :], start=(n == 0), stop=False)
                            p.v(lambda e: e.tensor_copy(pos[:, :], ps_cum[:, :]))
                            p.v(lambda e: e.tensor_scalar(out=bad[:, :], in0=pos[:, :], scalar1=float(C0),
                                                          scalar2=None, op0=ALU.is_ge))
                            p.v(lambda e: e.tensor_scalar(out=tmp[:, :], in0=A[:, :], scalar1=-1.0,
                                                          scalar2=1.0, op0=ALU.mult, op1=ALU.add))
                            p.v(lambda e: e.tensor_add(bad[:, :], bad[:, :], tmp[:, :]))
                            p.v(lambda e: e.tensor_add(tmp[:, :], eidoff[:, :], pos[:, :]))
                            p.v(lambda e, n=n: e.scalar_tensor_tensor(
                                out=smap_sb[:, n * 8:(n + 1) * 8], in0=bad[:, :], scalar=OOB, in1=tmp[:, :],
                                op0=ALU.mult, op1=ALU.add))
                            p.v(lambda e, n=n: e.tensor_scalar(out=tmp[:, :], in0=smap_sb[:, n * 8:(n + 1) * 8],
                                                               scalar1=base_b[:, 0:1], scalar2=None,
                                                               op0=ALU.subtract))
                            p.mm(ps_cum[:, :], lowI[:, :], A[:, :], start=False, stop=(n == 127))
                            p.dma(xtile[:, :], xtok_all[n * 128:(n + 1) * 128, :])
                            for k in range(8):
                                p.v(lambda e, k=k: e.tensor_copy(slot_i[:, :], tmp[:, k:k + 1]))
                                p.run("gpsimd", lambda e: e.indirect_dma_start(
                                    out=xbuf, out_offset=IndirectOffsetOnAxis(slot_i[:, :], axis=0),
                                    in_=xtile[:, :], in_offset=None,
                                    bounds_check=breg0, oob_is_err=False), inc=16)
                    p.dma(smap_m_d, smap_sb[:, :])
                    p.run("gpsimd", lambda e: e.collective_compute(
                        "AllGather", ALU.bypass, RG, [smap_m_d], [smap_all_d]), inc=16)

                    with ExitStack() as c2:
                        wgb = c2.enter_context(nc.sbuf_tensor("wgb", [128, 8 * F0], BF16))
                        wub = c2.enter_context(nc.sbuf_tensor("wub", [128, 8 * F0], BF16))
                        wdb = c2.enter_context(nc.sbuf_tensor("wdb", [128, 6 * D], BF16))
                        scrw = c2.enter_context(nc.sbuf_tensor("scrwE", [128, D], F32))
                        xb = c2.enter_context(nc.sbuf_tensor("xb", [128, 4 * D], BF16))
                        xbT = c2.enter_context(nc.sbuf_tensor("xbT", [128, 8 * C0], BF16))
                        gs = c2.enter_context(nc.sbuf_tensor("gs", [128, 6 * C0], BF16))
                        hh = c2.enter_context(nc.sbuf_tensor("hh", [128, 6 * C0], BF16))
                        yt = c2.enter_context(nc.sbuf_tensor("yt", [128, D], BF16))
                        ps_b = c2.enter_context(nc.psum_tensor("ps_bE", [128, 512], F32))
                        ps_tr = c2.enter_context(nc.psum_tensor("ps_trE", [128, 128], BF16))
                        for le in range(8):
                            for k in range(8):
                                load_cast_bf(nc, p, wg8[le, k * 128:(k + 1) * 128, :], wgb, k * F0, F0, scrw)
                                load_cast_bf(nc, p, wu8[le, k * 128:(k + 1) * 128, :], wub, k * F0, F0, scrw)
                            for k in range(6):
                                load_cast_bf(nc, p, wd8[le, k * 128:(k + 1) * 128, :], wdb, k * D, D, scrw)
                            for tc in range(4):
                                p.dma(xb[:, tc * D:(tc + 1) * D],
                                      xbuf[le * C0 + tc * 128: le * C0 + (tc + 1) * 128, :])
                            for tc in range(4):
                                for k in range(8):
                                    p.run("tensor", lambda e, tc=tc, k=k: e.transpose(
                                        ps_tr[:, :], xb[:, tc * D + k * 128:tc * D + (k + 1) * 128], ident[:, :]))
                                    p.v(lambda e, tc=tc, k=k: e.tensor_copy(
                                        xbT[:, k * C0 + tc * 128:k * C0 + (tc + 1) * 128], ps_tr[:, :]))
                            for mf in range(6):
                                for k in range(8):
                                    p.mm(ps_b[:, :], wgb[:, k * F0 + mf * 128:k * F0 + (mf + 1) * 128],
                                         xbT[:, k * C0:(k + 1) * C0], start=(k == 0), stop=(k == 7))
                                p.s(lambda e, mf=mf: e.activation(out=gs[:, mf * C0:(mf + 1) * C0],
                                                                  in_=ps_b[:, :], func=AF.Silu))
                                for k in range(8):
                                    p.mm(ps_b[:, :], wub[:, k * F0 + mf * 128:k * F0 + (mf + 1) * 128],
                                         xbT[:, k * C0:(k + 1) * C0], start=(k == 0), stop=(k == 7))
                                p.v(lambda e, mf=mf: e.tensor_tensor(
                                    hh[:, mf * C0:(mf + 1) * C0], ps_b[:, :],
                                    gs[:, mf * C0:(mf + 1) * C0], ALU.mult))
                            for tc in range(4):
                                for nch in range(2):
                                    for k in range(6):
                                        p.mm(ps_b[:, :], hh[:, k * C0 + tc * 128:k * C0 + (tc + 1) * 128],
                                             wdb[:, k * D + nch * 512:k * D + (nch + 1) * 512],
                                             start=(k == 0), stop=(k == 5))
                                    p.v(lambda e, nch=nch: e.tensor_copy(yt[:, nch * 512:(nch + 1) * 512],
                                                                         ps_b[:, :]))
                                p.dma(y_m_d[le * C0 + tc * 128:le * C0 + (tc + 1) * 128, :], yt[:, :])
                p.run("gpsimd", lambda e: e.collective_compute(
                    "AllGather", ALU.bypass, RG, [y_m_d], [y_all_d]), inc=16)

                with ExitStack() as c2:
                    smap3 = c2.enter_context(nc.sbuf_tensor("smap3", [128, 1024], F32))
                    oh = c2.enter_context(nc.sbuf_tensor("oh", [128, E0], F32))
                    ohs = c2.enter_context(nc.sbuf_tensor("ohs", [128, E0], F32))
                    slots = c2.enter_context(nc.sbuf_tensor("slots", [128, 8], F32))
                    slot_i = c2.enter_context(nc.sbuf_tensor("slot_i3", [128, 1], I32))
                    wmask = c2.enter_context(nc.sbuf_tensor("wmask", [128, 8], F32))
                    valid = c2.enter_context(nc.sbuf_tensor("valid", [128, 8], F32))
                    gbuf = c2.enter_context(nc.sbuf_tensor("gbuf", [128, D], BF16))
                    routed = c2.enter_context(nc.sbuf_tensor("routed", [128, D], F32))
                    routed_bf = c2.enter_context(nc.sbuf_tensor("routed_bf", [128, D], BF16))
                    ps_tr = c2.enter_context(nc.psum_tensor("ps_tr3", [128, 128], BF16))
                    smap_view = smap_all_d.rearrange("(r p) (n le) -> r p n le", p=128, le=8)
                    for r in range(8):
                        p.dma(smap3[:, r * 128:(r + 1) * 128],
                              smap_view[r][:, bass.ts(cval, 16), :])
                    p.g(lambda e: e.memset(gbuf[:, :], 0.0))
                    smap3v = smap3.rearrange("p (r nl le) -> p r nl le", r=8, nl=16)
                    ohv = oh.rearrange("p (a b) -> p a b", b=8)
                    ohsv = ohs.rearrange("p (a b) -> p a b", b=8)
                    for n in range(16):
                        for k in range(8):
                            p.v(lambda e, n=n, k=k: e.tensor_scalar(
                                out=oh[:, :], in0=iota64[:, :], scalar1=sel_p[:, n * 8 + k:n * 8 + k + 1],
                                scalar2=None, op0=ALU.is_equal))
                            p.v(lambda e, n=n: e.tensor_tensor(ohsv[:, :, :], ohv[:, :, :],
                                                               smap3v[:, :, n, :], ALU.mult))
                            p.v(lambda e, k=k: e.tensor_reduce(slots[:, k:k + 1], ohs[:, :], axis=AX.X, op=ALU.add))
                        p.v(lambda e: e.tensor_scalar(out=valid[:, :], in0=slots[:, :], scalar1=OOB / 2,
                                                      scalar2=None, op0=ALU.is_lt))
                        p.v(lambda e, n=n: e.tensor_mul(wmask[:, :], valid[:, :], wnorm_p[:, n * 8:(n + 1) * 8]))
                        for k in range(8):
                            p.v(lambda e, k=k: e.tensor_copy(slot_i[:, :], slots[:, k:k + 1]))
                            p.run("gpsimd", lambda e: e.indirect_dma_start(
                                out=gbuf[:, :], out_offset=None,
                                in_=y_all_d, in_offset=IndirectOffsetOnAxis(slot_i[:, :], axis=0),
                                bounds_check=breg1, oob_is_err=False), inc=16)
                            if k == 0:
                                p.v(lambda e: e.tensor_scalar(out=routed[:, :], in0=gbuf[:, :],
                                                              scalar1=wmask[:, 0:1], scalar2=None, op0=ALU.mult))
                            else:
                                p.v(lambda e, k=k: e.scalar_tensor_tensor(
                                    out=routed[:, :], in0=gbuf[:, :], scalar=wmask[:, k:k + 1],
                                    in1=routed[:, :], op0=ALU.mult, op1=ALU.add))
                        p.v(lambda e: e.tensor_copy(routed_bf[:, :], routed[:, :]))
                        for f in range(8):
                            p.run("tensor", lambda e, f=f: e.transpose(
                                ps_tr[:, :], routed_bf[:, f * 128:(f + 1) * 128], ident[:, :]))
                            dest = acc[:, f * S + n * 128:f * S + (n + 1) * 128]
                            p.v(lambda e, dest=dest: e.tensor_tensor(dest, dest, ps_tr[:, :], ALU.add))
                layernorm_fm(nc, p, ctx_a, acc, xn, scr1, ones_col, ones_row, eps_tile, tag="L1")

                with ExitStack() as c2:
                    xtok_sb = c2.enter_context(nc.sbuf_tensor("xtok_sb1", [128, D], BF16))
                    ps_tr = c2.enter_context(nc.psum_tensor("ps_trB", [128, 128], BF16))
                    for nl in range(16):
                        for f in range(8):
                            p.run("tensor", lambda e, nl=nl, f=f: e.transpose(
                                ps_tr[:, :], xn[:, f * S + nl * 128:f * S + (nl + 1) * 128], ident[:, :]))
                            p.v(lambda e, f=f: e.tensor_copy(xtok_sb[:, f * 128:(f + 1) * 128], ps_tr[:, :]))
                        p.dma(xtok1_m[nl * 128:(nl + 1) * 128, :], xtok_sb[:, :])
                p.run("gpsimd", lambda e: e.collective_compute(
                    "AllGather", ALU.bypass, RG, [xtok1_m], [xtok1_all]), inc=16)

                with ExitStack() as c2:
                    wrb = c2.enter_context(nc.sbuf_tensor("wrb", [128, 8 * E1], BF16))
                    scrw = c2.enter_context(nc.sbuf_tensor("scrwR1", [128, E1], F32))
                    for k in range(8):
                        load_cast_bf(nc, p, wr[k * 128:(k + 1) * 128, :], wrb, k * E1, E1, scrw)
                    probs = c2.enter_context(nc.sbuf_tensor("probs1", [128, E1], F32))
                    lgt = c2.enter_context(nc.sbuf_tensor("lgt1", [128, E1], F32))
                    mx = c2.enter_context(nc.sbuf_tensor("mx1", [128, 4], F32))
                    mx8 = c2.enter_context(nc.sbuf_tensor("mx81", [128, 8], F32))
                    mi8 = c2.enter_context(nc.sbuf_tensor("mi81", [128, 8], U32))
                    ps_r = c2.enter_context(nc.psum_tensor("ps_r1", [128, E1], F32))
                    for t in range(16):
                        for k in range(8):
                            p.mm(ps_r[:, :], xn[:, k * S + t * 128:k * S + (t + 1) * 128],
                                 wrb[:, k * E1:(k + 1) * E1], start=(k == 0), stop=(k == 7))
                        p.v(lambda e: e.tensor_copy(lgt[:, :], ps_r[:, :]))
                        softmax_row(nc, p, lgt, scr2, mx, probs, E1)
                        p.v(lambda e: e.max(out=mx8[:, :], in_=probs[:, 0:E1]))
                        p.v(lambda e: e.max_index(out=mi8[:, :], in_max=mx8[:, :], in_values=probs[:, 0:E1]))
                        p.v(lambda e, t=t: e.tensor_copy(gate1_p[:, t * 2:(t + 1) * 2], mx8[:, 0:2]))
                        p.v(lambda e, t=t: e.tensor_copy(sel1_p[:, t * 2:(t + 1) * 2], mi8[:, 0:2]))
                p.dma(sel1_m_d, sel1_p[:, :])
                p.run("gpsimd", lambda e: e.collective_compute(
                    "AllGather", ALU.bypass, RG, [sel1_m_d], [sel1_all_d]), inc=16)

                with ExitStack() as c2o:
                  outcat = c2o.enter_context(nc.sbuf_tensor("outcat", [128, 8 * S], BF16))
                  with ExitStack() as c2:
                    wqb = c2.enter_context(nc.sbuf_tensor("wqb", [128, 8 * DH], BF16))
                    wkb = c2.enter_context(nc.sbuf_tensor("wkb", [128, 8 * DH], BF16))
                    wvb = c2.enter_context(nc.sbuf_tensor("wvb", [128, 8 * DH], BF16))
                    QT = c2.enter_context(nc.sbuf_tensor("QT1", [64, S], BF16))
                    KT = c2.enter_context(nc.sbuf_tensor("KT1", [64, S], BF16))
                    V = c2.enter_context(nc.sbuf_tensor("V1", [128, 16 * DH], BF16))
                    attn_bf = c2.enter_context(nc.sbuf_tensor("attn_bf1", [128, S], BF16))
                    attnT = c2.enter_context(nc.sbuf_tensor("attnT1", [128, 128], BF16))
                    mx = c2.enter_context(nc.sbuf_tensor("mxA", [128, 4], F32))
                    ps_s = c2.enter_context(nc.psum_tensor("ps_s1", [128, S], F32))
                    ps_qk = c2.enter_context(nc.psum_tensor("ps_qk1", [128, 512], F32))
                    ps_v = c2.enter_context(nc.psum_tensor("ps_v1", [128, 64], F32))
                    ps_tr = c2.enter_context(nc.psum_tensor("ps_tr1", [128, 128], BF16))
                    ps_o = c2.enter_context(nc.psum_tensor("ps_o1", [128, 128], F32))
                    for h in range(NH):
                        ft, fo = h // 2, (h % 2) * 64
                        for k in range(8):
                            load_cast_bf(nc, p, qT[k * 128:(k + 1) * 128, h * DH:(h + 1) * DH],
                                         wqb, k * DH, DH, scr1)
                            load_cast_bf(nc, p, kT[k * 128:(k + 1) * 128, h * DH:(h + 1) * DH],
                                         wkb, k * DH, DH, scr1)
                            load_cast_bf(nc, p, vT[k * 128:(k + 1) * 128, h * DH:(h + 1) * DH],
                                         wvb, k * DH, DH, scr1)
                        for n in range(4):
                            for k in range(8):
                                p.mm(ps_qk[0:64, :], wqb[:, k * DH:(k + 1) * DH],
                                     xn[:, k * S + n * 512:k * S + (n + 1) * 512],
                                     start=(k == 0), stop=(k == 7))
                            p.v(lambda e, n=n: e.tensor_copy(QT[0:64, n * 512:(n + 1) * 512], ps_qk[0:64, :]))
                            for k in range(8):
                                p.mm(ps_qk[0:64, :], wkb[:, k * DH:(k + 1) * DH],
                                     xn[:, k * S + n * 512:k * S + (n + 1) * 512],
                                     start=(k == 0), stop=(k == 7))
                            p.v(lambda e, n=n: e.tensor_copy(KT[0:64, n * 512:(n + 1) * 512], ps_qk[0:64, :]))
                        for tc in range(16):
                            for k in range(8):
                                p.mm(ps_v[0:128, 0:64], xn[:, k * S + tc * 128:k * S + (tc + 1) * 128],
                                     wvb[:, k * DH:(k + 1) * DH],
                                     start=(k == 0), stop=(k == 7))
                            p.v(lambda e, tc=tc: e.tensor_copy(V[:, tc * DH:(tc + 1) * DH], ps_v[0:128, 0:64]))
                        for qi in range(16):
                            kt_len = (qi + 1) * 128
                            for st in range(0, kt_len, 512):
                                w = min(512, kt_len - st)
                                p.mm(ps_s[:, st:st + w], QT[0:64, qi * 128:(qi + 1) * 128],
                                     KT[0:64, st:st + w], start=True, stop=True)
                            p.s(lambda e, L=kt_len: e.activation(out=scr1[:, 0:L], in_=ps_s[:, 0:L],
                                                                 func=AF.Copy, scale=0.125))
                            dg = scr1[:, qi * 128:(qi + 1) * 128]
                            p.v(lambda e, dg=dg: e.tensor_add(dg, dg, cmask30[:, :]))
                            softmax_row(nc, p, scr1, scr2, mx, attn_bf, kt_len)
                            for kj in range(qi + 1):
                                p.run("tensor", lambda e, kj=kj: e.transpose(
                                    ps_tr[:, :], attn_bf[:, kj * 128:(kj + 1) * 128], ident[:, :]))
                                p.v(lambda e: e.tensor_copy(attnT[:, :], ps_tr[:, :]))
                                p.mm(ps_o[0:64, :], V[:, kj * DH:(kj + 1) * DH], attnT[:, :],
                                     start=(kj == 0), stop=(kj == qi))
                            p.v(lambda e, qi=qi, ft=ft, fo=fo: e.tensor_copy(
                                outcat[fo:fo + 64, ft * S + qi * 128:ft * S + (qi + 1) * 128], ps_o[0:64, :]))
                  with ExitStack() as c3:
                        wob = c3.enter_context(nc.sbuf_tensor("wobA", [128, 8 * D], BF16))
                        for k in range(8):
                            load_cast_bf(nc, p, oT[k * 128:(k + 1) * 128, :], wob, k * D, D, scr1)
                        ps_b = c3.enter_context(nc.psum_tensor("ps_bA", [128, 512], F32))
                        for mo in range(8):
                            for n in range(4):
                                for k in range(8):
                                    p.mm(ps_b[:, :], wob[:, k * D + mo * 128:k * D + (mo + 1) * 128],
                                         outcat[:, k * S + n * 512:k * S + (n + 1) * 512],
                                         start=(k == 0), stop=(k == 7))
                                dest = acc[:, mo * S + n * 512:mo * S + (n + 1) * 512]
                                p.v(lambda e, dest=dest: e.tensor_tensor(dest, dest, ps_b[:, :], ALU.add))

            with ExitStack() as c2s:
                sel_sb = c2s.enter_context(nc.sbuf_tensor("sel_sb4", [128, 128 * 2], F32))
                A_sb = c2s.enter_context(nc.sbuf_tensor("A_sb", [128, 128 * 2], F32))
                pos_sb = c2s.enter_context(nc.sbuf_tensor("pos_sb", [128, 128 * 2], F32))
                smap_sb = c2s.enter_context(nc.sbuf_tensor("smap_sb4", [128, 128 * 2], F32))
                slot_i = c2s.enter_context(nc.sbuf_tensor("slot_i4", [128, 1], I32))
                eid_b = c2s.enter_context(nc.sbuf_tensor("eid_b4", [128, 1], F32))
                eoff_b = c2s.enter_context(nc.sbuf_tensor("eoff_b4", [128, 1], F32))
                eid_s = c2s.enter_context(nc.sbuf_tensor("eid_s4", [1, 1], F32))
                tot_sb = c2s.enter_context(nc.sbuf_tensor("tot_sb", [1, 4], F32))
                tot0b = c2s.enter_context(nc.sbuf_tensor("tot0b", [128, 2], F32))
                pos = c2s.enter_context(nc.sbuf_tensor("pos4", [128, 2], F32))
                tmp = c2s.enter_context(nc.sbuf_tensor("tmp4", [128, 2], F32))
                tmp2 = c2s.enter_context(nc.sbuf_tensor("tmp24", [128, 2], F32))
                bad = c2s.enter_context(nc.sbuf_tensor("bad4", [128, 2], F32))
                xtile = c2s.enter_context(nc.sbuf_tensor("xtile4", [128, D], BF16))
                zero_t = c2s.enter_context(nc.sbuf_tensor("zero_t4", [128, D], BF16))
                p.dma(sel_sb[:, :], sel1_all_d.rearrange("(r p) c -> p r c", p=128))
                p.dma(eid_s[:, :], eid1)
                p.g(lambda e: e.memset(zero_t[:, :], 0.0))
                for r in range(10):
                    p.dma(xbuf1[r * 128:(r + 1) * 128, :], zero_t[:, :])
                with ExitStack() as c2:
                    ps_bc = c2.enter_context(nc.psum_tensor("ps_bc4", [128, 1], F32))
                    p.mm(ps_bc[:, :], ones_row[0:1, :], eid_s[0:1, :], start=True, stop=True)
                    p.v(lambda e: e.tensor_copy(eid_b[:, :], ps_bc[:, :]))
                    p.v(lambda e: e.tensor_scalar(out=eoff_b[:, :], in0=eid_b[:, :],
                                                  scalar1=float(2 * C1), scalar2=None, op0=ALU.mult))
                    ps_cum = c2.enter_context(nc.psum_tensor("ps_cum4", [128, 2], F32))
                    for n in range(128):
                        for k in range(2):
                            p.v(lambda e, n=n, k=k: e.tensor_scalar(
                                out=A_sb[:, n * 2 + k:n * 2 + k + 1], in0=eid_b[:, 0:1],
                                scalar1=sel_sb[:, n * 2 + k:n * 2 + k + 1],
                                scalar2=None, op0=ALU.is_equal))
                        p.mm(ps_cum[:, :], strictU[:, :], A_sb[:, n * 2:(n + 1) * 2],
                             start=(n % 64 == 0), stop=False)
                        p.v(lambda e, n=n: e.tensor_copy(pos_sb[:, n * 2:(n + 1) * 2], ps_cum[:, :]))
                        p.mm(ps_cum[:, :], lowI[:, :], A_sb[:, n * 2:(n + 1) * 2],
                             start=False, stop=(n % 64 == 63))
                        if n % 64 == 63:
                            g = n // 64
                            p.v(lambda e, g=g: e.tensor_copy(tot_sb[0:1, g * 2:(g + 1) * 2], ps_cum[0:1, :]))
                    for g in range(2):
                        p.mm(ps_bc[:, :], ones_row[0:1, :], tot_sb[0:1, g * 2:g * 2 + 1], start=True, stop=True)
                        p.v(lambda e, g=g: e.tensor_copy(tot0b[:, g:g + 1], ps_bc[:, :]))
                    for n in range(128):
                        g = n // 64
                        p.v(lambda e, n=n: e.tensor_copy(pos[:, :], pos_sb[:, n * 2:(n + 1) * 2]))
                        p.v(lambda e, g=g: e.tensor_scalar(out=pos[:, 1:2], in0=pos[:, 1:2],
                                                           scalar1=tot0b[:, g:g + 1], scalar2=None,
                                                           op0=ALU.add))
                        p.v(lambda e: e.tensor_scalar(out=bad[:, :], in0=pos[:, :], scalar1=float(C1),
                                                      scalar2=None, op0=ALU.is_ge))
                        p.v(lambda e, n=n: e.tensor_scalar(out=tmp[:, :], in0=A_sb[:, n * 2:(n + 1) * 2],
                                                           scalar1=-1.0, scalar2=1.0,
                                                           op0=ALU.mult, op1=ALU.add))
                        p.v(lambda e: e.tensor_add(bad[:, :], bad[:, :], tmp[:, :]))
                        p.v(lambda e, g=g: e.tensor_scalar(out=tmp[:, :], in0=pos[:, :],
                                                           scalar1=float(g * C1), scalar2=None, op0=ALU.add))
                        p.v(lambda e: e.scalar_tensor_tensor(out=tmp2[:, :], in0=bad[:, :], scalar=OOB,
                                                             in1=tmp[:, :], op0=ALU.mult, op1=ALU.add))
                        p.v(lambda e, n=n: e.tensor_scalar(out=smap_sb[:, n * 2:(n + 1) * 2], in0=tmp2[:, :],
                                                           scalar1=eoff_b[:, 0:1], scalar2=None, op0=ALU.add))
                        p.dma(xtile[:, :], xtok1_all[n * 128:(n + 1) * 128, :])
                        for k in range(2):
                            p.v(lambda e, k=k: e.tensor_copy(slot_i[:, :], tmp2[:, k:k + 1]))
                            p.run("gpsimd", lambda e: e.indirect_dma_start(
                                out=xbuf1, out_offset=IndirectOffsetOnAxis(slot_i[:, :], axis=0),
                                in_=xtile[:, :], in_offset=None,
                                bounds_check=breg2, oob_is_err=False), inc=16)
                p.dma(smap1_m_d, smap_sb[:, :])
                p.run("gpsimd", lambda e: e.collective_compute(
                    "AllGather", ALU.bypass, RG, [smap1_m_d], [smap1_all_d]), inc=16)

            with ExitStack() as c2:
                xbT = c2.enter_context(nc.sbuf_tensor("xbT4", [128, 8 * C1], BF16))
                hh = c2.enter_context(nc.sbuf_tensor("hh4", [128, 32 * C1], BF16))
                for g in range(2):
                    with ExitStack() as c3:
                        xb = c3.enter_context(nc.sbuf_tensor("xb4_%d" % g, [128, 5 * D], BF16))
                        ps_tr = c3.enter_context(nc.psum_tensor("ps_tr4_%d" % g, [128, 128], BF16))
                        for tc in range(5):
                            p.dma(xb[:, tc * D:(tc + 1) * D],
                                  xbuf1[g * C1 + tc * 128:g * C1 + (tc + 1) * 128, :])
                        for tc in range(5):
                            for k in range(8):
                                p.run("tensor", lambda e, tc=tc, k=k: e.transpose(
                                    ps_tr[:, :], xb[:, tc * D + k * 128:tc * D + (k + 1) * 128],
                                    ident[:, :]))
                                p.v(lambda e, tc=tc, k=k: e.tensor_copy(
                                    xbT[:, k * C1 + tc * 128:k * C1 + (tc + 1) * 128], ps_tr[:, :]))
                    with ExitStack() as c3:
                        wib = c3.enter_context(nc.sbuf_tensor("wib4_%d" % g, [128, 8 * F1], BF16))
                        scrw = c3.enter_context(nc.sbuf_tensor("scrw4_%d" % g, [128, 2048], F32))
                        ps_b = c3.enter_context(nc.psum_tensor("ps_b4_%d" % g, [128, 512], F32))
                        for k in range(8):
                            for half in range(2):
                                load_cast_bf(nc, p, wi_e[k * 128:(k + 1) * 128, half * 2048:(half + 1) * 2048],
                                             wib, k * F1 + half * 2048, 2048, scrw)
                        for mf in range(32):
                            for (n0, w) in ((0, 512), (512, 128)):
                                for k in range(8):
                                    p.mm(ps_b[:, 0:w], wib[:, k * F1 + mf * 128:k * F1 + (mf + 1) * 128],
                                         xbT[:, k * C1 + n0:k * C1 + n0 + w], start=(k == 0), stop=(k == 7))
                                p.s(lambda e, mf=mf, n0=n0, w=w: e.activation(
                                    out=hh[:, mf * C1 + n0:mf * C1 + n0 + w],
                                    in_=ps_b[:, 0:w], func=AF.Relu))
                    with ExitStack() as c3:
                        wob = c3.enter_context(nc.sbuf_tensor("wob4_%d" % g, [128, 32 * D], BF16))
                        scrw = c3.enter_context(nc.sbuf_tensor("scrw4b_%d" % g, [128, D], F32))
                        yt = c3.enter_context(nc.sbuf_tensor("yt4_%d" % g, [128, D], BF16))
                        ps_b = c3.enter_context(nc.psum_tensor("ps_b4b_%d" % g, [128, 512], F32))
                        for k in range(32):
                            load_cast_bf(nc, p, wo_e[k * 128:(k + 1) * 128, :], wob, k * D, D, scrw)
                        for tc in range(5):
                            for nch in range(2):
                                for k in range(32):
                                    p.mm(ps_b[:, :],
                                         hh[:, k * C1 + tc * 128:k * C1 + (tc + 1) * 128],
                                         wob[:, k * D + nch * 512:k * D + (nch + 1) * 512],
                                         start=(k == 0), stop=(k == 31))
                                p.v(lambda e, nch=nch: e.tensor_copy(yt[:, nch * 512:(nch + 1) * 512],
                                                                     ps_b[:, :]))
                            p.dma(y1_m_d[g * C1 + tc * 128:g * C1 + (tc + 1) * 128, :], yt[:, :])
            p.run("gpsimd", lambda e: e.collective_compute(
                "AllGather", ALU.bypass, RG, [y1_m_d], [y1_all_d]), inc=16)

            with ExitStack() as c2:
                smap5 = c2.enter_context(nc.sbuf_tensor("smap5", [128, 256], F32))
                oh = c2.enter_context(nc.sbuf_tensor("oh5", [128, E1], F32))
                ohs = c2.enter_context(nc.sbuf_tensor("ohs5", [128, E1], F32))
                slots = c2.enter_context(nc.sbuf_tensor("slots5", [128, 2], F32))
                slot_i = c2.enter_context(nc.sbuf_tensor("slot_i5", [128, 1], I32))
                wmask = c2.enter_context(nc.sbuf_tensor("wmask5", [128, 2], F32))
                valid = c2.enter_context(nc.sbuf_tensor("valid5", [128, 2], F32))
                gbuf = c2.enter_context(nc.sbuf_tensor("gbuf5", [128, D], BF16))
                routed = c2.enter_context(nc.sbuf_tensor("routed5", [128, D], F32))
                routed_bf = c2.enter_context(nc.sbuf_tensor("routed_bf5", [128, D], BF16))
                ps_tr = c2.enter_context(nc.psum_tensor("ps_tr5", [128, 128], BF16))
                smap1_view = smap1_all_d.rearrange("(r p) (n k) -> r p n k", p=128, k=2)
                for r in range(8):
                    p.dma(smap5[:, r * 32:(r + 1) * 32],
                          smap1_view[r][:, bass.ts(cval, 16), :])
                p.g(lambda e: e.memset(gbuf[:, :], 0.0))
                smap5v = smap5.rearrange("p (r c) -> p r c", r=8)
                for n in range(16):
                    for k in range(2):
                        p.v(lambda e, n=n, k=k: e.tensor_scalar(
                            out=oh[:, :], in0=iota8[:, :], scalar1=sel1_p[:, n * 2 + k:n * 2 + k + 1],
                            scalar2=None, op0=ALU.is_equal))
                        p.v(lambda e, n=n, k=k: e.tensor_tensor(ohs[:, :], oh[:, :],
                                                                smap5v[:, :, n * 2 + k], ALU.mult))
                        p.v(lambda e, k=k: e.tensor_reduce(slots[:, k:k + 1], ohs[:, :], axis=AX.X, op=ALU.add))
                    p.v(lambda e: e.tensor_scalar(out=valid[:, :], in0=slots[:, :], scalar1=OOB / 2,
                                                  scalar2=None, op0=ALU.is_lt))
                    p.v(lambda e, n=n: e.tensor_mul(wmask[:, :], valid[:, :], gate1_p[:, n * 2:(n + 1) * 2]))
                    for k in range(2):
                        p.v(lambda e, k=k: e.tensor_copy(slot_i[:, :], slots[:, k:k + 1]))
                        p.run("gpsimd", lambda e: e.indirect_dma_start(
                            out=gbuf[:, :], out_offset=None,
                            in_=y1_all_d, in_offset=IndirectOffsetOnAxis(slot_i[:, :], axis=0),
                            bounds_check=breg3, oob_is_err=False), inc=16)
                        if k == 0:
                            p.v(lambda e: e.tensor_scalar(out=routed[:, :], in0=gbuf[:, :],
                                                          scalar1=wmask[:, 0:1], scalar2=None, op0=ALU.mult))
                        else:
                            p.v(lambda e, k=k: e.scalar_tensor_tensor(
                                out=routed[:, :], in0=gbuf[:, :], scalar=wmask[:, k:k + 1],
                                in1=routed[:, :], op0=ALU.mult, op1=ALU.add))
                    p.v(lambda e: e.tensor_copy(routed_bf[:, :], routed[:, :]))
                    for f in range(8):
                        p.run("tensor", lambda e, f=f: e.transpose(
                            ps_tr[:, :], routed_bf[:, f * 128:(f + 1) * 128], ident[:, :]))
                        dest = acc[:, f * S + n * 128:f * S + (n + 1) * 128]
                        p.v(lambda e, dest=dest: e.tensor_tensor(dest, dest, ps_tr[:, :], ALU.add))
            for f in range(8):
                p.dma(outT_o[f * 128:(f + 1) * 128, :], acc[:, f * S:(f + 1) * S])
            p.finish()
    return nc


def kernel_fused(inp):
    x = np.asarray(inp["L0_attn_x"])
    cores = list(range(8))
    nc = _get("fused", build_fused)
    common = dict(
        wq=_f32c(inp["L0_attn_Wq"]), wk=_f32c(inp["L0_attn_Wk"]), wv=_f32c(inp["L0_attn_Wv"]),
        wo=_f32c(inp["L0_attn_W_O"]), wsgu=_f32c(inp["L0_ffn_ws_gate_up"]),
        wsd=_f32c(inp["L0_ffn_ws_down"]), rwT=_f32c(np.asarray(inp["L0_ffn_router_w"]).T),
        qT=_f32c(np.asarray(inp["L1_attn_q_proj_weight"]).T),
        kT=_f32c(np.asarray(inp["L1_attn_k_proj_weight"]).T),
        vT=_f32c(np.asarray(inp["L1_attn_v_proj_weight"]).T),
        oT=_f32c(np.asarray(inp["L1_attn_o_proj_weight"]).T),
        wr=_f32c(inp["L1_ffn_w_router"]))
    in_maps = []
    for c in cores:
        in_maps.append(dict(
            xT=_f32c(x[c].T),
            wg8=_f32c(inp["L0_ffn_w_gate"][8 * c:8 * c + 8]),
            wu8=_f32c(inp["L0_ffn_w_up"][8 * c:8 * c + 8]),
            wd8=_f32c(inp["L0_ffn_w_down"][8 * c:8 * c + 8]),
            eids=_f32c(np.arange(8 * c, 8 * c + 8, dtype=np.float32).reshape(1, 8)),
            basev=_f32c(np.array([[c * 8 * C0]], dtype=np.float32)),
            wi_e=_f32c(inp["L1_ffn_wi"][c]), wo_e=_f32c(inp["L1_ffn_wo"][c]),
            eid1=_f32c(np.array([[c]], dtype=np.float32)),
            cvec=np.array([[c]], dtype=np.int32),
            **common))
    res = run_bass_kernel_spmd(nc, in_maps, core_ids=cores).results
    out = np.stack([np.ascontiguousarray(np.asarray(res[c]["outT"]).T) for c in cores], axis=0)
    return out.astype(np.float32)


_cache = {}


def _get(name, builder):
    if name not in _cache:
        _cache[name] = builder()
    return _cache[name]


def _f32c(a):
    return np.ascontiguousarray(a, dtype=np.float32)


FUSED = False


def kernel(**inp):
    if FUSED:
        return kernel_fused(inp)
    return kernel_unfused(**inp)


def kernel_unfused(L0_attn_x, L0_attn_Wq, L0_attn_Wk, L0_attn_Wv, L0_attn_W_O,
           L0_ffn_router_w, L0_ffn_w_gate, L0_ffn_w_up, L0_ffn_w_down,
           L0_ffn_ws_gate_up, L0_ffn_ws_down,
           L1_attn_q_proj_weight, L1_attn_k_proj_weight, L1_attn_v_proj_weight,
           L1_attn_o_proj_weight, L1_ffn_w_router, L1_ffn_wi, L1_ffn_wo):
    x = np.asarray(L0_attn_x)
    cores = list(range(8))

    nc1 = _get("s1", build_s1)
    common1 = dict(
        wq=_f32c(L0_attn_Wq), wk=_f32c(L0_attn_Wk), wv=_f32c(L0_attn_Wv),
        wo=_f32c(L0_attn_W_O), wsgu=_f32c(L0_ffn_ws_gate_up), wsd=_f32c(L0_ffn_ws_down),
        rwT=_f32c(np.asarray(L0_ffn_router_w).T))
    in1 = [dict(xT=_f32c(x[c].T), **common1) for c in cores]
    r1 = run_bass_kernel_spmd(nc1, in1, core_ids=cores).results

    xn_tok = np.ascontiguousarray(
        np.concatenate([np.asarray(r1[c]["xnT"]).T for c in cores], axis=0))
    sel_all = np.ascontiguousarray(np.concatenate([r1[c]["sel_p"] for c in cores], axis=1))
    nc2 = _get("s2", build_s2)
    in2 = []
    for c in cores:
        in2.append(dict(
            xn_tok=xn_tok, sel_all=sel_all,
            wg8=_f32c(L0_ffn_w_gate[8 * c:8 * c + 8]),
            wu8=_f32c(L0_ffn_w_up[8 * c:8 * c + 8]),
            wd8=_f32c(L0_ffn_w_down[8 * c:8 * c + 8]),
            eids=_f32c(np.arange(8 * c, 8 * c + 8, dtype=np.float32).reshape(1, 8)),
            basev=_f32c(np.array([[c * 8 * C0]], dtype=np.float32))))
    r2 = run_bass_kernel_spmd(nc2, in2, core_ids=cores).results

    y_full = np.ascontiguousarray(np.concatenate([np.asarray(r2[c]["y_loc"]) for c in cores], axis=0))
    smaps = [np.asarray(r2[e]["smap_p"]).reshape(128, 128, 8) for e in cores]
    nc3 = _get("s3", build_s3)
    in3 = []
    for c in cores:
        so = np.concatenate([smaps[e][:, c * 16:(c + 1) * 16, :] for e in cores], axis=2)
        in3.append(dict(
            accT=np.asarray(r1[c]["accT"]), y_full=y_full,
            smap_own=np.ascontiguousarray(so.reshape(128, 16 * 64), dtype=np.float32),
            sel_own=np.asarray(r1[c]["sel_p"]), wnorm_own=np.asarray(r1[c]["wnorm_p"])))
    r3 = run_bass_kernel_spmd(nc3, in3, core_ids=cores).results

    nc3b = _get("s3b", build_s3b)
    common3b = dict(
        qT=_f32c(np.asarray(L1_attn_q_proj_weight).T), kT=_f32c(np.asarray(L1_attn_k_proj_weight).T),
        vT=_f32c(np.asarray(L1_attn_v_proj_weight).T), oT=_f32c(np.asarray(L1_attn_o_proj_weight).T),
        wr=_f32c(L1_ffn_w_router))
    in3b = [dict(x1T=np.asarray(r3[c]["x1T"]), xn1T=np.asarray(r3[c]["xn1T"]), **common3b)
            for c in cores]
    r3b = run_bass_kernel_spmd(nc3b, in3b, core_ids=cores).results

    xn1_tok = np.ascontiguousarray(
        np.concatenate([np.asarray(r3[c]["xn1T"]).T for c in cores], axis=0))
    sel1_all = np.ascontiguousarray(np.concatenate([r3b[c]["sel1_p"] for c in cores], axis=1))
    nc4 = _get("s4", build_s4)
    in4 = [dict(xn1_tok=xn1_tok, sel1_all=sel1_all,
                wi_e=_f32c(L1_ffn_wi[c]), wo_e=_f32c(L1_ffn_wo[c]),
                eid1=_f32c(np.array([[c]], dtype=np.float32))) for c in cores]
    r4 = run_bass_kernel_spmd(nc4, in4, core_ids=cores).results

    y1_full = np.ascontiguousarray(np.concatenate([np.asarray(r4[c]["y1_loc"]) for c in cores], axis=0))
    smap1s = [np.asarray(r4[e]["smap1_p"]).reshape(128, 128, 2) for e in cores]
    nc5 = _get("s5", build_s5)
    in5 = []
    for c in cores:
        so = np.stack([smap1s[e][:, c * 16:(c + 1) * 16, :] for e in cores], axis=3)
        in5.append(dict(
            acc2T=np.asarray(r3b[c]["acc2T"]), y1_full=y1_full,
            smap1_own=np.ascontiguousarray(so.reshape(128, 256), dtype=np.float32),
            sel1_own=np.asarray(r3b[c]["sel1_p"]), gate1_own=np.asarray(r3b[c]["gate1_p"])))
    r5 = run_bass_kernel_spmd(nc5, in5, core_ids=cores).results

    out = np.stack([np.ascontiguousarray(np.asarray(r5[c]["outT"]).T) for c in cores], axis=0)
    return out.astype(np.float32)


# version 08
# speedup vs baseline: 1.0591x; 1.0591x over previous
import numpy as np
from contextlib import ExitStack

import concourse.bass as bass
import concourse.mybir as mybir
from concourse.bass import IndirectOffsetOnAxis
from concourse.bass_utils import run_bass_kernel_spmd
import ml_dtypes

F32 = mybir.dt.float32
BF16 = mybir.dt.bfloat16
I32 = mybir.dt.int32
U32 = mybir.dt.uint32
AF = mybir.ActivationFunctionType
ALU = mybir.AluOpType
AX = mybir.AxisListType

S = 2048
D = 1024
T = 16384
NH = 16
DH = 64
E0, K0, C0, F0 = 64, 8, 512, 768
SF = 768
E1, K1, C1, F1 = 8, 2, 640, 4096
OOB = 6.0e8


class P:

    PIPELINE_SAME_ENGINE = True

    def __init__(self, nc):
        self.nc = nc
        self.sem = nc.alloc_semaphore("ser")
        self.count = 0
        self.last_engine = None
        self.wait_val = 0

    def run(self, engine, fn, inc=1, wait=None, pipe=False):
        eng = getattr(self.nc, engine)
        inst = fn(eng)
        if wait is not None:
            inst._wait_ge(self.sem, wait)
            self.wait_val = wait
        elif (pipe or (self.PIPELINE_SAME_ENGINE and engine in ("tensor", "sync"))) \
                and engine == self.last_engine:
            inst._wait_ge(self.sem, self.wait_val)
        else:
            inst._wait_ge(self.sem, self.count)
            self.wait_val = self.count
        inst.then_inc(self.sem, inc)
        self.count += inc
        self.last_engine = engine
        return inst

    def dma(self, out, in_, engine="sync", **kw):
        return self.run(engine, lambda e: e.dma_start(out, in_, **kw), inc=16)

    def mm(self, out, lhsT, rhs, start, stop):
        return self.run("tensor", lambda e: e.matmul(out, lhsT, rhs, start=start, stop=stop))

    def v(self, fn):
        return self.run("vector", fn)

    def s(self, fn):
        return self.run("scalar", fn)

    def g(self, fn):
        return self.run("gpsimd", fn)

    def finish(self):
        for e in ("sync", "tensor", "vector", "scalar", "gpsimd"):
            getattr(self.nc, e).wait_ge(self.sem, self.count)


def new_nc():
    return bass.Bass("TRN2", target_bir_lowering=False)


def mk_ident_bf(nc, p, ctx):
    t = ctx.enter_context(nc.sbuf_tensor("ident_bf", [128, 128], BF16))
    p.g(lambda e: e.memset(t[:, :], 1.0))
    p.g(lambda e: e.affine_select(out=t[:, :], in_=t[:, :], compare_op=ALU.is_equal,
                                  fill=0.0, base=0, pattern=[[-1, 128]], channel_multiplier=1))
    return t



def mk_causal_add(nc, p, ctx, name, val):
    t = ctx.enter_context(nc.sbuf_tensor(name, [128, 128], F32))
    p.g(lambda e: e.memset(t[:, :], 0.0))
    p.g(lambda e: e.affine_select(out=t[:, :], in_=t[:, :], compare_op=ALU.is_ge,
                                  fill=val, base=0, pattern=[[-1, 128]], channel_multiplier=1))
    return t

def mk_strict_upper(nc, p, ctx):
    t = ctx.enter_context(nc.sbuf_tensor("strictU", [128, 128], F32))
    p.g(lambda e: e.memset(t[:, :], 0.0))
    p.g(lambda e: e.affine_select(out=t[:, :], in_=t[:, :], compare_op=ALU.is_ge,
                                  fill=1.0, base=0, pattern=[[-1, 128]], channel_multiplier=1))
    return t


def mk_lower_incl(nc, p, ctx):
    t = ctx.enter_context(nc.sbuf_tensor("lowIncl", [128, 128], F32))
    p.g(lambda e: e.memset(t[:, :], 1.0))
    p.g(lambda e: e.affine_select(out=t[:, :], in_=t[:, :], compare_op=ALU.is_ge,
                                  fill=0.0, base=0, pattern=[[-1, 128]], channel_multiplier=1))
    return t


def mk_ones(nc, p, ctx):
    oc = ctx.enter_context(nc.sbuf_tensor("ones_col", [128, 1], F32))
    orow = ctx.enter_context(nc.sbuf_tensor("ones_row", [1, 128], F32))
    p.g(lambda e: e.memset(oc[:, :], 1.0))
    p.g(lambda e: e.memset(orow[:, :], 1.0))
    return oc, orow


def load_cast_bf(nc, p, w_ap, dst, dst_off, cols, scratch):
    p.dma(scratch[:, 0:cols], w_ap)
    p.v(lambda e: e.tensor_copy(dst[:, dst_off:dst_off + cols], scratch[:, 0:cols]))


def layernorm_fm(nc, p, ctx, acc, xn, scr1, ones_col, ones_row, eps_tile, ntiles=8, ntok=S, tag=""):
    nch = ntok // 512
    with ExitStack() as c2:
        ps_sum = c2.enter_context(nc.psum_tensor("ps_sum" + tag, [1, ntok], F32))
        ps_sq = c2.enter_context(nc.psum_tensor("ps_sq" + tag, [1, ntok], F32))
        mean_row = c2.enter_context(nc.sbuf_tensor("mean_row" + tag, [1, ntok], F32))
        var_row = c2.enter_context(nc.sbuf_tensor("var_row" + tag, [1, ntok], F32))
        msq_row = c2.enter_context(nc.sbuf_tensor("msq_row" + tag, [1, ntok], F32))
        rstd_row = c2.enter_context(nc.sbuf_tensor("rstd_row" + tag, [1, ntok], F32))
        for f in range(ntiles):
            p.s(lambda e, f=f: e.activation(out=scr1[:, 0:ntok], in_=acc[:, f * ntok:(f + 1) * ntok],
                                            func=AF.Square))
            for n in range(nch):
                p.mm(ps_sum[0:1, n * 512:(n + 1) * 512], ones_col[:, 0:1],
                     acc[:, f * ntok + n * 512: f * ntok + (n + 1) * 512],
                     start=(f == 0), stop=(f == ntiles - 1))
                p.mm(ps_sq[0:1, n * 512:(n + 1) * 512], ones_col[:, 0:1],
                     scr1[:, n * 512:(n + 1) * 512],
                     start=(f == 0), stop=(f == ntiles - 1))
        p.s(lambda e: e.activation(out=mean_row[0:1, :], in_=ps_sum[0:1, :], func=AF.Copy, scale=1.0 / D))
        p.s(lambda e: e.activation(out=var_row[0:1, :], in_=ps_sq[0:1, :], func=AF.Copy, scale=1.0 / D))
        p.v(lambda e: e.tensor_mul(msq_row[0:1, :], mean_row[0:1, :], mean_row[0:1, :]))
        p.v(lambda e: e.tensor_sub(var_row[0:1, :], var_row[0:1, :], msq_row[0:1, :]))
        p.s(lambda e: e.activation(out=var_row[0:1, :], in_=var_row[0:1, :], func=AF.Sqrt,
                                   bias=eps_tile[0:1, 0:1], scale=1.0))
        p.v(lambda e: e.reciprocal(rstd_row[0:1, :], var_row[0:1, :]))
    with ExitStack() as c2:
        ps_bc = c2.enter_context(nc.psum_tensor("ps_bc" + tag, [128, 512], F32))
        mean_b = c2.enter_context(nc.sbuf_tensor("mean_b" + tag, [128, ntok], F32))
        rstd_b = c2.enter_context(nc.sbuf_tensor("rstd_b" + tag, [128, ntok], F32))
        for n in range(nch):
            p.mm(ps_bc[:, :], ones_row[0:1, :], mean_row[0:1, n * 512:(n + 1) * 512], start=True, stop=True)
            p.v(lambda e, n=n: e.tensor_copy(mean_b[:, n * 512:(n + 1) * 512], ps_bc[:, :]))
            p.mm(ps_bc[:, :], ones_row[0:1, :], rstd_row[0:1, n * 512:(n + 1) * 512], start=True, stop=True)
            p.v(lambda e, n=n: e.tensor_copy(rstd_b[:, n * 512:(n + 1) * 512], ps_bc[:, :]))
        for f in range(ntiles):
            p.v(lambda e, f=f: e.tensor_sub(scr1[:, 0:ntok], acc[:, f * ntok:(f + 1) * ntok], mean_b[:, :]))
            p.v(lambda e, f=f: e.tensor_mul(xn[:, f * ntok:(f + 1) * ntok], scr1[:, 0:ntok], rstd_b[:, :]))


def softmax_row(nc, p, scr_in, scr_exp, mx, attn_out, kt_len, out_dtype_note=None):
    p.v(lambda e: e.tensor_reduce(mx[:, 0:1], scr_in[:, 0:kt_len], axis=AX.X, op=ALU.max))
    p.v(lambda e: e.tensor_scalar(out=mx[:, 1:2], in0=mx[:, 0:1], scalar1=-1.0, scalar2=None, op0=ALU.mult))
    p.s(lambda e: e.activation(out=scr_exp[:, 0:kt_len], in_=scr_in[:, 0:kt_len], func=AF.Exp,
                               bias=mx[:, 1:2], scale=1.0))
    p.v(lambda e: e.tensor_reduce(mx[:, 2:3], scr_exp[:, 0:kt_len], axis=AX.X, op=ALU.add))
    p.v(lambda e: e.reciprocal(mx[:, 3:4], mx[:, 2:3]))
    p.v(lambda e: e.tensor_scalar(out=attn_out[:, 0:kt_len], in0=scr_exp[:, 0:kt_len],
                                  scalar1=mx[:, 3:4], scalar2=None, op0=ALU.mult))


def build_s1():
    nc = new_nc()
    xT = nc.dram_tensor("xT", [D, S], F32, kind="ExternalInput").ap()
    wq = nc.dram_tensor("wq", [NH, DH, DH], F32, kind="ExternalInput").ap()
    wk = nc.dram_tensor("wk", [NH, DH, DH], F32, kind="ExternalInput").ap()
    wv = nc.dram_tensor("wv", [NH, DH, DH], F32, kind="ExternalInput").ap()
    wo = nc.dram_tensor("wo", [D, D], F32, kind="ExternalInput").ap()
    wsgu = nc.dram_tensor("wsgu", [D, 2 * SF], F32, kind="ExternalInput").ap()
    wsd = nc.dram_tensor("wsd", [SF, D], F32, kind="ExternalInput").ap()
    rwT = nc.dram_tensor("rwT", [D, E0], F32, kind="ExternalInput").ap()

    accT_o = nc.dram_tensor("accT", [D, S], F32, kind="ExternalOutput").ap()
    xnT_o = nc.dram_tensor("xnT", [D, S], BF16, kind="ExternalOutput").ap()
    sel_o = nc.dram_tensor("sel_p", [128, 128], F32, kind="ExternalOutput").ap()
    wnorm_o = nc.dram_tensor("wnorm_p", [128, 128], F32, kind="ExternalOutput").ap()

    with ExitStack() as ctx:
        acc = ctx.enter_context(nc.sbuf_tensor("acc", [128, 8 * S], F32))
        xn = ctx.enter_context(nc.sbuf_tensor("xn", [128, 8 * S], BF16))
        scr1 = ctx.enter_context(nc.sbuf_tensor("scr1", [128, S], F32))
        scr2 = ctx.enter_context(nc.sbuf_tensor("scr2", [128, S], F32))
        eps_tile = ctx.enter_context(nc.sbuf_tensor("eps", [1, 1], F32))
        block = ctx.enter_context(nc.Block())

        @block.sync
        def _(sync):
            p = P(nc)
            ident = mk_ident_bf(nc, p, ctx)
            cmask = mk_causal_add(nc, p, ctx, "cmask", -1e9)
            ones_col, ones_row = mk_ones(nc, p, ctx)
            p.g(lambda e: e.memset(eps_tile[:, :], 1e-5))
            for f in range(8):
                p.dma(acc[:, f * S:(f + 1) * S], xT[f * 128:(f + 1) * 128, :])
            layernorm_fm(nc, p, ctx, acc, xn, scr1, ones_col, ones_row, eps_tile)

            with ExitStack() as c2m:
              M = c2m.enter_context(nc.sbuf_tensor("M", [128, 16 * D], BF16))
              with ExitStack() as c2:
                wq_bf = c2.enter_context(nc.sbuf_tensor("wq_bf", [128, NH * DH], BF16))
                wk_bf = c2.enter_context(nc.sbuf_tensor("wk_bf", [128, NH * DH], BF16))
                wv_bf = c2.enter_context(nc.sbuf_tensor("wv_bf", [128, NH * DH], BF16))
                wf32 = c2.enter_context(nc.sbuf_tensor("wf32", [128, NH * DH], F32))
                for name, w_ap, dst in (("q", wq, wq_bf), ("k", wk, wk_bf), ("v", wv, wv_bf)):
                    for h in range(NH):
                        p.dma(wf32[0:64, h * DH:(h + 1) * DH], w_ap[h])
                        p.dma(wf32[64:128, h * DH:(h + 1) * DH], w_ap[h])
                    p.v(lambda e, dst=dst: e.tensor_copy(dst[:, :], wf32[:, :]))
                QT = c2.enter_context(nc.sbuf_tensor("QT", [64, S], BF16))
                KT = c2.enter_context(nc.sbuf_tensor("KT", [64, S], BF16))
                V = c2.enter_context(nc.sbuf_tensor("V", [128, 16 * DH], BF16))
                attn_bf = c2.enter_context(nc.sbuf_tensor("attn_bf", [128, S], BF16))
                attnT = c2.enter_context(nc.sbuf_tensor("attnT", [128, 256], BF16))
                mx = c2.enter_context(nc.sbuf_tensor("mx", [128, 4], F32))
                ps_s = c2.enter_context(nc.psum_tensor("ps_s", [128, S], F32))
                ps_qk = c2.enter_context(nc.psum_tensor("ps_qk", [128, 512], F32))
                ps_v = ps_qk
                ps_trA = c2.enter_context(nc.psum_tensor("ps_trA", [128, 128], BF16))
                ps_trB = c2.enter_context(nc.psum_tensor("ps_trB", [128, 128], BF16))
                ps_o = c2.enter_context(nc.psum_tensor("ps_o", [128, 64], F32))
                for h in range(NH):
                    ft, fo = h // 2, (h % 2) * 64
                    xh = lambda a, b: xn[fo:fo + 64, ft * S + a: ft * S + b]
                    for n in range(4):
                        p.mm(ps_qk[0:64, :], wq_bf[fo:fo + 64, h * DH:(h + 1) * DH],
                             xh(n * 512, (n + 1) * 512), start=True, stop=True)
                        p.v(lambda e, n=n: e.tensor_copy(QT[0:64, n * 512:(n + 1) * 512], ps_qk[0:64, :]))
                        p.mm(ps_qk[0:64, :], wk_bf[fo:fo + 64, h * DH:(h + 1) * DH],
                             xh(n * 512, (n + 1) * 512), start=True, stop=True)
                        p.v(lambda e, n=n: e.tensor_copy(KT[0:64, n * 512:(n + 1) * 512], ps_qk[0:64, :]))
                    for tc in range(16):
                        p.mm(ps_v[0:128, 0:64], xh(tc * 128, (tc + 1) * 128),
                             wv_bf[fo:fo + 64, h * DH:(h + 1) * DH], start=True, stop=True)
                        p.v(lambda e, tc=tc: e.tensor_copy(V[:, tc * DH:(tc + 1) * DH], ps_v[0:128, 0:64]))
                    for qi in range(16):
                        kt_len = (qi + 1) * 128
                        for st in range(0, kt_len, 512):
                            w = min(512, kt_len - st)
                            p.mm(ps_s[:, st:st + w], QT[0:64, qi * 128:(qi + 1) * 128],
                                 KT[0:64, st:st + w], start=True, stop=True)
                        p.s(lambda e, L=kt_len: e.activation(out=scr1[:, 0:L], in_=ps_s[:, 0:L],
                                                             func=AF.Copy, scale=0.125))
                        dg = scr1[:, qi * 128:(qi + 1) * 128]
                        p.v(lambda e, dg=dg: e.tensor_add(dg, dg, cmask[:, :]))
                        softmax_row(nc, p, scr1, scr2, mx, attn_bf, kt_len)
                        c_copy = {}
                        c_mm = {}
                        c_sm = p.count
                        for kj in range(qi + 1):
                            b = kj % 2
                            pst = ps_trA if b == 0 else ps_trB
                            w_tr = c_sm if kj < 2 else c_copy[kj - 2]
                            p.run("tensor", lambda e, kj=kj, pst=pst: e.transpose(
                                pst[:, :], attn_bf[:, kj * 128:(kj + 1) * 128], ident[:, :]),
                                wait=w_tr)
                            c_tr = p.count
                            w_cp = c_tr if kj < 2 else max(c_tr, c_mm[kj - 2])
                            p.run("vector", lambda e, b=b, pst=pst: e.tensor_copy(
                                attnT[:, b * 128:(b + 1) * 128], pst[:, :]), wait=w_cp)
                            c_copy[kj] = p.count
                            p.run("tensor", lambda e, kj=kj, b=b: e.matmul(
                                ps_o[0:128, 0:64], attnT[:, b * 128:(b + 1) * 128],
                                V[:, kj * DH:(kj + 1) * DH], start=(kj == 0), stop=(kj == qi)),
                                wait=c_copy[kj])
                            c_mm[kj] = p.count
                        p.v(lambda e, qi=qi, h=h: e.tensor_copy(
                            M[:, qi * D + h * DH: qi * D + (h + 1) * DH], ps_o[0:128, 0:64]))

              with ExitStack() as c2:
                wo_bf = c2.enter_context(nc.sbuf_tensor("wo_bf", [128, 8 * D], BF16))
                for k in range(8):
                    load_cast_bf(nc, p, wo[k * 128:(k + 1) * 128, :], wo_bf, k * D, D, scr1)
                ps_b = c2.enter_context(nc.psum_tensor("ps_b", [128, 512], F32))
                for j0 in range(2):
                    for mo in range(8):
                        for rc in range(2):
                            for k in range(8):
                                p.mm(ps_b[:, :], wo_bf[:, k * D + mo * 128:k * D + (mo + 1) * 128],
                                     M[:, (8 * j0 + k) * D + rc * 512:(8 * j0 + k) * D + (rc + 1) * 512],
                                     start=(k == 0), stop=(k == 7))
                            dest = acc[:, mo * S:(mo + 1) * S].rearrange(
                                "p (r two) -> p r two", two=2)[:, rc * 512:(rc + 1) * 512, j0]
                            p.v(lambda e, dest=dest: e.tensor_tensor(dest, dest, ps_b[:, :], ALU.add))

            with ExitStack() as c2:
                wsgu_bf = c2.enter_context(nc.sbuf_tensor("wsgu_bf", [128, 8 * 2 * SF], BF16))
                wsd_bf = c2.enter_context(nc.sbuf_tensor("wsd_bf", [128, 6 * D], BF16))
                scrw = c2.enter_context(nc.sbuf_tensor("scrw", [128, 2 * SF], F32))
                for k in range(8):
                    load_cast_bf(nc, p, wsgu[k * 128:(k + 1) * 128, :], wsgu_bf, k * 2 * SF, 2 * SF, scrw)
                for k in range(6):
                    load_cast_bf(nc, p, wsd[k * 128:(k + 1) * 128, :], wsd_bf, k * D, D, scrw)
                hT = c2.enter_context(nc.sbuf_tensor("hT", [128, 6 * S], BF16))
                ps_b = c2.enter_context(nc.psum_tensor("ps_b2", [128, 512], F32))
                for m in range(12):
                    for n in range(4):
                        for k in range(8):
                            p.mm(ps_b[:, :], wsgu_bf[:, k * 2 * SF + m * 128:k * 2 * SF + (m + 1) * 128],
                                 xn[:, k * S + n * 512:k * S + (n + 1) * 512],
                                 start=(k == 0), stop=(k == 7))
                        if m < 6:
                            p.s(lambda e, m=m, n=n: e.activation(
                                out=hT[:, m * S + n * 512:m * S + (n + 1) * 512],
                                in_=ps_b[:, :], func=AF.Silu))
                        else:
                            dst = hT[:, (m - 6) * S + n * 512:(m - 6) * S + (n + 1) * 512]
                            p.v(lambda e, dst=dst: e.tensor_tensor(dst, ps_b[:, :], dst, ALU.mult))
                for mo in range(8):
                    for n in range(4):
                        for k in range(6):
                            p.mm(ps_b[:, :], wsd_bf[:, k * D + mo * 128:k * D + (mo + 1) * 128],
                                 hT[:, k * S + n * 512:k * S + (n + 1) * 512],
                                 start=(k == 0), stop=(k == 5))
                        dest = acc[:, mo * S + n * 512:mo * S + (n + 1) * 512]
                        p.v(lambda e, dest=dest: e.tensor_tensor(dest, dest, ps_b[:, :], ALU.add))

            with ExitStack() as c2:
                rw_bf = c2.enter_context(nc.sbuf_tensor("rw_bf", [128, 8 * E0], BF16))
                scrw = c2.enter_context(nc.sbuf_tensor("scrw2", [128, E0], F32))
                for k in range(8):
                    load_cast_bf(nc, p, rwT[k * 128:(k + 1) * 128, :], rw_bf, k * E0, E0, scrw)
                sel_p = c2.enter_context(nc.sbuf_tensor("sel_sb", [128, 128], F32))
                wnorm_p = c2.enter_context(nc.sbuf_tensor("wnorm_sb", [128, 128], F32))
                probs = c2.enter_context(nc.sbuf_tensor("probs", [128, E0], F32))
                lgt = c2.enter_context(nc.sbuf_tensor("lgt", [128, E0], F32))
                mx = c2.enter_context(nc.sbuf_tensor("mx2", [128, 4], F32))
                mx8 = c2.enter_context(nc.sbuf_tensor("mx8", [128, 8], F32))
                mi8 = c2.enter_context(nc.sbuf_tensor("mi8", [128, 8], U32))
                ps_r = c2.enter_context(nc.psum_tensor("ps_r", [128, E0], F32))
                for t in range(16):
                    for k in range(8):
                        p.mm(ps_r[:, :], xn[:, k * S + t * 128:k * S + (t + 1) * 128],
                             rw_bf[:, k * E0:(k + 1) * E0], start=(k == 0), stop=(k == 7))
                    p.v(lambda e: e.tensor_copy(lgt[:, :], ps_r[:, :]))
                    softmax_row(nc, p, lgt, scr2, mx, probs, E0)
                    p.v(lambda e: e.max(out=mx8[:, :], in_=probs[:, 0:E0]))
                    p.v(lambda e: e.max_index(out=mi8[:, :], in_max=mx8[:, :], in_values=probs[:, 0:E0]))
                    p.v(lambda e: e.tensor_reduce(mx[:, 0:1], mx8[:, :], axis=AX.X, op=ALU.add))
                    p.v(lambda e: e.reciprocal(mx[:, 1:2], mx[:, 0:1]))
                    p.v(lambda e, t=t: e.tensor_scalar(out=wnorm_p[:, t * 8:(t + 1) * 8], in0=mx8[:, :],
                                                       scalar1=mx[:, 1:2], scalar2=None, op0=ALU.mult))
                    p.v(lambda e, t=t: e.tensor_copy(sel_p[:, t * 8:(t + 1) * 8], mi8[:, :]))
                p.dma(sel_o, sel_p[:, :])
                p.dma(wnorm_o, wnorm_p[:, :])

            for f in range(8):
                p.dma(accT_o[f * 128:(f + 1) * 128, :], acc[:, f * S:(f + 1) * S])
                p.dma(xnT_o[f * 128:(f + 1) * 128, :], xn[:, f * S:(f + 1) * S])
            p.finish()
    return nc


def build_s2():
    nc = new_nc()
    xn_tok = nc.dram_tensor("xn_tok", [T, D], BF16, kind="ExternalInput").ap()
    sel_all = nc.dram_tensor("sel_all", [128, 128 * 8], F32, kind="ExternalInput").ap()
    wg8 = nc.dram_tensor("wg8", [8, D, F0], F32, kind="ExternalInput").ap()
    wu8 = nc.dram_tensor("wu8", [8, D, F0], F32, kind="ExternalInput").ap()
    wd8 = nc.dram_tensor("wd8", [8, F0, D], F32, kind="ExternalInput").ap()
    eids = nc.dram_tensor("eids", [1, 8], F32, kind="ExternalInput").ap()
    basev = nc.dram_tensor("basev", [1, 1], F32, kind="ExternalInput").ap()

    y_o = nc.dram_tensor("y_loc", [8 * C0, D], BF16, kind="ExternalOutput").ap()
    smap_o = nc.dram_tensor("smap_p", [128, 128 * 8], F32, kind="ExternalOutput").ap()
    xbuf = nc.dram_tensor("xbuf", [8 * C0, D], BF16, kind="Internal").ap()

    with ExitStack() as ctx:
        block = None
        sel_sb = ctx.enter_context(nc.sbuf_tensor("sel_sb", [128, 128 * 8], F32))
        smap_sb = ctx.enter_context(nc.sbuf_tensor("smap_sb", [128, 128 * 8], F32))
        slot_i = ctx.enter_context(nc.sbuf_tensor("slot_i", [128, 8], I32))
        eid_b = ctx.enter_context(nc.sbuf_tensor("eid_b", [128, 8], F32))
        eidoff = ctx.enter_context(nc.sbuf_tensor("eidoff", [128, 8], F32))
        base_b = ctx.enter_context(nc.sbuf_tensor("base_b", [128, 1], F32))
        eid_s = ctx.enter_context(nc.sbuf_tensor("eid_s", [1, 8], F32))
        base_s = ctx.enter_context(nc.sbuf_tensor("base_s", [1, 1], F32))
        A = ctx.enter_context(nc.sbuf_tensor("A", [128, 8], F32))
        eq = ctx.enter_context(nc.sbuf_tensor("eq", [128, 8], F32))
        pos = ctx.enter_context(nc.sbuf_tensor("pos", [128, 8], F32))
        tmp = ctx.enter_context(nc.sbuf_tensor("tmp", [128, 8], F32))
        bad = ctx.enter_context(nc.sbuf_tensor("bad", [128, 8], F32))
        xtile = ctx.enter_context(nc.sbuf_tensor("xtile", [128, D], BF16))
        zero_t = ctx.enter_context(nc.sbuf_tensor("zero_t", [128, D], BF16))
        block = ctx.enter_context(nc.Block())

        @block.sync
        def _(sync):
            p = P(nc)
            ones_col, ones_row = mk_ones(nc, p, ctx)
            strictU = mk_strict_upper(nc, p, ctx)
            lowI = mk_lower_incl(nc, p, ctx)
            ident = mk_ident_bf(nc, p, ctx)
            breg = nc.gpsimd.alloc_register("bnd")
            nc.gpsimd.reg_mov(breg, 8 * C0 - 1)
            p.dma(sel_sb[:, :], sel_all)
            p.dma(eid_s[:, :], eids)
            p.dma(base_s[:, :], basev)
            p.g(lambda e: e.memset(zero_t[:, :], 0.0))
            for r in range(32):
                p.dma(xbuf[r * 128:(r + 1) * 128, :], zero_t[:, :])
            with ExitStack() as c2:
                ps_bc = c2.enter_context(nc.psum_tensor("ps_bcS2", [128, 8], F32))
                p.mm(ps_bc[:, :], ones_row[0:1, :], eid_s[0:1, :], start=True, stop=True)
                p.v(lambda e: e.tensor_copy(eid_b[:, :], ps_bc[:, :]))
                p.mm(ps_bc[:, 0:1], ones_row[0:1, :], base_s[0:1, :], start=True, stop=True)
                p.v(lambda e: e.tensor_copy(base_b[:, :], ps_bc[:, 0:1]))
            p.v(lambda e: e.tensor_scalar(out=eidoff[:, :], in0=eid_b[:, :], scalar1=float(C0),
                                          scalar2=None, op0=ALU.mult))
            with ExitStack() as c2:
                ps_cum = c2.enter_context(nc.psum_tensor("ps_cum", [128, 8], F32))
                for n in range(128):
                    p.v(lambda e, n=n: e.tensor_scalar(out=A[:, :], in0=eid_b[:, :],
                                                       scalar1=sel_sb[:, n * 8:n * 8 + 1],
                                                       scalar2=None, op0=ALU.is_equal))
                    for k in range(1, 8):
                        p.v(lambda e, n=n, k=k: e.tensor_scalar(out=eq[:, :], in0=eid_b[:, :],
                                                                scalar1=sel_sb[:, n * 8 + k:n * 8 + k + 1],
                                                                scalar2=None, op0=ALU.is_equal))
                        p.v(lambda e: e.tensor_add(A[:, :], A[:, :], eq[:, :]))
                    p.mm(ps_cum[:, :], strictU[:, :], A[:, :], start=(n == 0), stop=False)
                    p.v(lambda e: e.tensor_copy(pos[:, :], ps_cum[:, :]))
                    p.v(lambda e: e.tensor_scalar(out=bad[:, :], in0=pos[:, :], scalar1=float(C0),
                                                  scalar2=None, op0=ALU.is_ge))
                    p.v(lambda e: e.tensor_scalar(out=tmp[:, :], in0=A[:, :], scalar1=-1.0,
                                                  scalar2=1.0, op0=ALU.mult, op1=ALU.add))
                    p.v(lambda e: e.tensor_add(bad[:, :], bad[:, :], tmp[:, :]))
                    p.v(lambda e: e.tensor_add(tmp[:, :], eidoff[:, :], pos[:, :]))
                    p.v(lambda e, n=n: e.scalar_tensor_tensor(
                        out=smap_sb[:, n * 8:(n + 1) * 8], in0=bad[:, :], scalar=OOB, in1=tmp[:, :],
                        op0=ALU.mult, op1=ALU.add))
                    p.v(lambda e, n=n: e.tensor_scalar(out=tmp[:, :], in0=smap_sb[:, n * 8:(n + 1) * 8],
                                                       scalar1=base_b[:, 0:1], scalar2=None,
                                                       op0=ALU.subtract))
                    p.mm(ps_cum[:, :], lowI[:, :], A[:, :], start=False, stop=(n == 127))
                    p.dma(xtile[:, :], xn_tok[n * 128:(n + 1) * 128, :])
                    p.v(lambda e: e.tensor_copy(slot_i[:, :], tmp[:, :]))
                    for k in range(8):
                        p.run("gpsimd", lambda e, k=k: e.indirect_dma_start(
                            out=xbuf, out_offset=IndirectOffsetOnAxis(slot_i[:, k:k + 1], axis=0),
                            in_=xtile[:, :], in_offset=None,
                            bounds_check=breg, oob_is_err=False), inc=16, pipe=(k > 0))
            p.dma(smap_o, smap_sb[:, :])

            with ExitStack() as c2:
                wgb = c2.enter_context(nc.sbuf_tensor("wgb", [128, 8 * F0], BF16))
                wub = c2.enter_context(nc.sbuf_tensor("wub", [128, 8 * F0], BF16))
                wdb = c2.enter_context(nc.sbuf_tensor("wdb", [128, 6 * D], BF16))
                scrw = c2.enter_context(nc.sbuf_tensor("scrw", [128, D], F32))
                xb = c2.enter_context(nc.sbuf_tensor("xb", [128, 4 * D], BF16))
                xbT = c2.enter_context(nc.sbuf_tensor("xbT", [128, 8 * C0], BF16))
                gs = c2.enter_context(nc.sbuf_tensor("gs", [128, 6 * C0], BF16))
                hh = c2.enter_context(nc.sbuf_tensor("hh", [128, 6 * C0], BF16))
                yt = c2.enter_context(nc.sbuf_tensor("yt", [128, D], BF16))
                ps_b = c2.enter_context(nc.psum_tensor("ps_b", [128, 512], F32))
                ps_tr = c2.enter_context(nc.psum_tensor("ps_tr2", [128, 128], BF16))
                for le in range(8):
                    for k in range(8):
                        load_cast_bf(nc, p, wg8[le, k * 128:(k + 1) * 128, :], wgb, k * F0, F0, scrw)
                        load_cast_bf(nc, p, wu8[le, k * 128:(k + 1) * 128, :], wub, k * F0, F0, scrw)
                    for k in range(6):
                        load_cast_bf(nc, p, wd8[le, k * 128:(k + 1) * 128, :], wdb, k * D, D, scrw)
                    for tc in range(4):
                        p.dma(xb[:, tc * D:(tc + 1) * D],
                              xbuf[le * C0 + tc * 128: le * C0 + (tc + 1) * 128, :])
                    for tc in range(4):
                        for k in range(8):
                            p.run("tensor", lambda e, tc=tc, k=k: e.transpose(
                                ps_tr[:, :], xb[:, tc * D + k * 128:tc * D + (k + 1) * 128], ident[:, :]))
                            p.v(lambda e, tc=tc, k=k: e.tensor_copy(
                                xbT[:, k * C0 + tc * 128:k * C0 + (tc + 1) * 128], ps_tr[:, :]))
                    for mf in range(6):
                        for k in range(8):
                            p.mm(ps_b[:, :], wgb[:, k * F0 + mf * 128:k * F0 + (mf + 1) * 128],
                                 xbT[:, k * C0:(k + 1) * C0], start=(k == 0), stop=(k == 7))
                        p.s(lambda e, mf=mf: e.activation(out=gs[:, mf * C0:(mf + 1) * C0],
                                                          in_=ps_b[:, :], func=AF.Silu))
                        for k in range(8):
                            p.mm(ps_b[:, :], wub[:, k * F0 + mf * 128:k * F0 + (mf + 1) * 128],
                                 xbT[:, k * C0:(k + 1) * C0], start=(k == 0), stop=(k == 7))
                        p.v(lambda e, mf=mf: e.tensor_tensor(
                            hh[:, mf * C0:(mf + 1) * C0], ps_b[:, :],
                            gs[:, mf * C0:(mf + 1) * C0], ALU.mult))
                    for tc in range(4):
                        for nch in range(2):
                            for k in range(6):
                                p.mm(ps_b[:, :], hh[:, k * C0 + tc * 128:k * C0 + (tc + 1) * 128],
                                     wdb[:, k * D + nch * 512:k * D + (nch + 1) * 512],
                                     start=(k == 0), stop=(k == 5))
                            p.v(lambda e, nch=nch: e.tensor_copy(yt[:, nch * 512:(nch + 1) * 512],
                                                                 ps_b[:, :]))
                        p.dma(y_o[le * C0 + tc * 128:le * C0 + (tc + 1) * 128, :], yt[:, :])
            p.finish()
    return nc


def build_s3():
    nc = new_nc()
    accT = nc.dram_tensor("accT", [D, S], F32, kind="ExternalInput").ap()
    y_full = nc.dram_tensor("y_full", [E0 * C0, D], BF16, kind="ExternalInput").ap()
    smap_own = nc.dram_tensor("smap_own", [128, 16 * E0], F32, kind="ExternalInput").ap()
    sel_own = nc.dram_tensor("sel_own", [128, 128], F32, kind="ExternalInput").ap()
    wnorm_own = nc.dram_tensor("wnorm_own", [128, 128], F32, kind="ExternalInput").ap()

    x1T_o = nc.dram_tensor("x1T", [D, S], F32, kind="ExternalOutput").ap()
    xn1T_o = nc.dram_tensor("xn1T", [D, S], BF16, kind="ExternalOutput").ap()

    with ExitStack() as ctx:
        acc = ctx.enter_context(nc.sbuf_tensor("acc", [128, 8 * S], F32))
        xn = ctx.enter_context(nc.sbuf_tensor("xn", [128, 8 * S], BF16))
        scr1 = ctx.enter_context(nc.sbuf_tensor("scr1", [128, S], F32))
        eps_tile = ctx.enter_context(nc.sbuf_tensor("eps", [1, 1], F32))
        smap_sb = ctx.enter_context(nc.sbuf_tensor("smap_sb", [128, 16 * E0], F32))
        sel_sb = ctx.enter_context(nc.sbuf_tensor("sel_sb", [128, 128], F32))
        wn_sb = ctx.enter_context(nc.sbuf_tensor("wn_sb", [128, 128], F32))
        iota64 = ctx.enter_context(nc.sbuf_tensor("iota64", [128, E0], F32))
        oh = ctx.enter_context(nc.sbuf_tensor("oh", [128, E0], F32))
        ohs = ctx.enter_context(nc.sbuf_tensor("ohs", [128, E0], F32))
        slots = ctx.enter_context(nc.sbuf_tensor("slots", [128, 8], F32))
        slot_i = ctx.enter_context(nc.sbuf_tensor("slot_i", [128, 8], I32))
        wmask = ctx.enter_context(nc.sbuf_tensor("wmask", [128, 8], F32))
        valid = ctx.enter_context(nc.sbuf_tensor("valid", [128, 8], F32))
        gbuf = ctx.enter_context(nc.sbuf_tensor("gbuf", [128, 8 * D], BF16))
        routed = ctx.enter_context(nc.sbuf_tensor("routed", [128, D], F32))
        routed_bf = ctx.enter_context(nc.sbuf_tensor("routed_bf", [128, D], BF16))
        block = ctx.enter_context(nc.Block())

        @block.sync
        def _(sync):
            p = P(nc)
            ones_col, ones_row = mk_ones(nc, p, ctx)
            ident = mk_ident_bf(nc, p, ctx)
            p.g(lambda e: e.memset(eps_tile[:, :], 1e-5))
            p.g(lambda e: e.iota(iota64[:, :], pattern=[[1, E0]], base=0, channel_multiplier=0,
                                 allow_small_or_imprecise_dtypes=True))
            p.g(lambda e: e.memset(gbuf[:, :], 0.0))
            for f in range(8):
                p.dma(acc[:, f * S:(f + 1) * S], accT[f * 128:(f + 1) * 128, :])
            breg = nc.gpsimd.alloc_register("bnd")
            nc.gpsimd.reg_mov(breg, E0 * C0 - 1)
            p.dma(smap_sb[:, :], smap_own)
            p.dma(sel_sb[:, :], sel_own)
            p.dma(wn_sb[:, :], wnorm_own)
            with ExitStack() as c2:
                ps_tr = c2.enter_context(nc.psum_tensor("ps_tr", [128, 128], BF16))
                for n in range(16):
                    for k in range(8):
                        p.v(lambda e, n=n, k=k: e.tensor_scalar(
                            out=oh[:, :], in0=iota64[:, :], scalar1=sel_sb[:, n * 8 + k:n * 8 + k + 1],
                            scalar2=None, op0=ALU.is_equal))
                        p.v(lambda e, n=n: e.tensor_mul(ohs[:, :], oh[:, :], smap_sb[:, n * E0:(n + 1) * E0]))
                        p.v(lambda e, k=k: e.tensor_reduce(slots[:, k:k + 1], ohs[:, :], axis=AX.X, op=ALU.add))
                    p.v(lambda e: e.tensor_scalar(out=valid[:, :], in0=slots[:, :], scalar1=OOB / 2,
                                                  scalar2=None, op0=ALU.is_lt))
                    p.v(lambda e, n=n: e.tensor_mul(wmask[:, :], valid[:, :], wn_sb[:, n * 8:(n + 1) * 8]))
                    p.v(lambda e: e.tensor_copy(slot_i[:, :], slots[:, :]))
                    for k in range(8):
                        p.run("gpsimd", lambda e, k=k: e.indirect_dma_start(
                            out=gbuf[:, k * D:(k + 1) * D], out_offset=None,
                            in_=y_full, in_offset=IndirectOffsetOnAxis(slot_i[:, k:k + 1], axis=0),
                            bounds_check=breg, oob_is_err=False), inc=16, pipe=(k > 0))
                    for k in range(8):
                        if k == 0:
                            p.v(lambda e: e.tensor_scalar(out=routed[:, :], in0=gbuf[:, 0:D],
                                                          scalar1=wmask[:, 0:1], scalar2=None, op0=ALU.mult))
                        else:
                            p.v(lambda e, k=k: e.scalar_tensor_tensor(
                                out=routed[:, :], in0=gbuf[:, k * D:(k + 1) * D], scalar=wmask[:, k:k + 1],
                                in1=routed[:, :], op0=ALU.mult, op1=ALU.add))
                    p.v(lambda e: e.tensor_copy(routed_bf[:, :], routed[:, :]))
                    for f in range(8):
                        p.run("tensor", lambda e, f=f: e.transpose(
                            ps_tr[:, :], routed_bf[:, f * 128:(f + 1) * 128], ident[:, :]))
                        dest = acc[:, f * S + n * 128:f * S + (n + 1) * 128]
                        p.v(lambda e, dest=dest: e.tensor_tensor(dest, dest, ps_tr[:, :], ALU.add))
            layernorm_fm(nc, p, ctx, acc, xn, scr1, ones_col, ones_row, eps_tile)
            for f in range(8):
                p.dma(x1T_o[f * 128:(f + 1) * 128, :], acc[:, f * S:(f + 1) * S])
                p.dma(xn1T_o[f * 128:(f + 1) * 128, :], xn[:, f * S:(f + 1) * S])
            p.finish()
    return nc


def build_s3b():
    nc = new_nc()
    x1T = nc.dram_tensor("x1T", [D, S], F32, kind="ExternalInput").ap()
    xn1T = nc.dram_tensor("xn1T", [D, S], BF16, kind="ExternalInput").ap()
    qT = nc.dram_tensor("qT", [D, D], F32, kind="ExternalInput").ap()
    kT = nc.dram_tensor("kT", [D, D], F32, kind="ExternalInput").ap()
    vT = nc.dram_tensor("vT", [D, D], F32, kind="ExternalInput").ap()
    oT = nc.dram_tensor("oT", [D, D], F32, kind="ExternalInput").ap()
    wr = nc.dram_tensor("wr", [D, E1], F32, kind="ExternalInput").ap()

    acc2T_o = nc.dram_tensor("acc2T", [D, S], F32, kind="ExternalOutput").ap()
    sel1_o = nc.dram_tensor("sel1_p", [128, 32], F32, kind="ExternalOutput").ap()
    gate1_o = nc.dram_tensor("gate1_p", [128, 32], F32, kind="ExternalOutput").ap()

    with ExitStack() as ctx:
        xn = ctx.enter_context(nc.sbuf_tensor("xn", [128, 8 * S], BF16))
        scr1 = ctx.enter_context(nc.sbuf_tensor("scr1", [128, S], F32))
        scr2 = ctx.enter_context(nc.sbuf_tensor("scr2", [128, S], F32))
        outcat = ctx.enter_context(nc.sbuf_tensor("outcat", [128, 8 * S], BF16))
        block = ctx.enter_context(nc.Block())

        @block.sync
        def _(sync):
            p = P(nc)
            ident = mk_ident_bf(nc, p, ctx)
            cmask = mk_causal_add(nc, p, ctx, "cmask", -1e30)
            for f in range(8):
                p.dma(xn[:, f * S:(f + 1) * S], xn1T[f * 128:(f + 1) * 128, :])
            with ExitStack() as c2:
                V = c2.enter_context(nc.sbuf_tensor("Vall", [128, 16 * D], BF16))
                with ExitStack() as c3:
                    wvb = c3.enter_context(nc.sbuf_tensor("wvb", [128, 8 * D], BF16))
                    ps_b = c3.enter_context(nc.psum_tensor("ps_vb", [128, 512], F32))
                    for k in range(8):
                        load_cast_bf(nc, p, vT[k * 128:(k + 1) * 128, :], wvb, k * D, D, scr1)
                    for tc in range(16):
                        for nch in range(2):
                            for k in range(8):
                                p.mm(ps_b[:, :], xn[:, k * S + tc * 128:k * S + (tc + 1) * 128],
                                     wvb[:, k * D + nch * 512:k * D + (nch + 1) * 512],
                                     start=(k == 0), stop=(k == 7))
                            p.v(lambda e, tc=tc, nch=nch: e.tensor_copy(
                                V[:, tc * D + nch * 512:tc * D + (nch + 1) * 512], ps_b[:, :]))
                wqb = c2.enter_context(nc.sbuf_tensor("wqb", [128, 8 * D], BF16))
                wkb = c2.enter_context(nc.sbuf_tensor("wkb", [128, 8 * D], BF16))
                for k in range(8):
                    load_cast_bf(nc, p, qT[k * 128:(k + 1) * 128, :], wqb, k * D, D, scr1)
                    load_cast_bf(nc, p, kT[k * 128:(k + 1) * 128, :], wkb, k * D, D, scr1)
                QT = c2.enter_context(nc.sbuf_tensor("QT", [64, S], BF16))
                KT = c2.enter_context(nc.sbuf_tensor("KT", [64, S], BF16))
                attn_bf = c2.enter_context(nc.sbuf_tensor("attn_bf", [128, S], BF16))
                attnT = c2.enter_context(nc.sbuf_tensor("attnT", [128, 256], BF16))
                mx = c2.enter_context(nc.sbuf_tensor("mx", [128, 4], F32))
                ps_s = c2.enter_context(nc.psum_tensor("ps_s", [128, S], F32))
                ps_qk = c2.enter_context(nc.psum_tensor("ps_qk", [128, 512], F32))
                ps_trA = c2.enter_context(nc.psum_tensor("ps_trA", [128, 128], BF16))
                ps_trB = c2.enter_context(nc.psum_tensor("ps_trB", [128, 128], BF16))
                ps_o = c2.enter_context(nc.psum_tensor("ps_o", [128, 128], F32))
                for h in range(NH):
                    ft, fo = h // 2, (h % 2) * 64
                    for n in range(4):
                        for k in range(8):
                            p.mm(ps_qk[0:64, :], wqb[:, k * D + h * DH:k * D + (h + 1) * DH],
                                 xn[:, k * S + n * 512:k * S + (n + 1) * 512],
                                 start=(k == 0), stop=(k == 7))
                        p.v(lambda e, n=n: e.tensor_copy(QT[0:64, n * 512:(n + 1) * 512], ps_qk[0:64, :]))
                        for k in range(8):
                            p.mm(ps_qk[0:64, :], wkb[:, k * D + h * DH:k * D + (h + 1) * DH],
                                 xn[:, k * S + n * 512:k * S + (n + 1) * 512],
                                 start=(k == 0), stop=(k == 7))
                        p.v(lambda e, n=n: e.tensor_copy(KT[0:64, n * 512:(n + 1) * 512], ps_qk[0:64, :]))
                    for qi in range(16):
                        kt_len = (qi + 1) * 128
                        for st in range(0, kt_len, 512):
                            w = min(512, kt_len - st)
                            p.mm(ps_s[:, st:st + w], QT[0:64, qi * 128:(qi + 1) * 128],
                                 KT[0:64, st:st + w], start=True, stop=True)
                        p.s(lambda e, L=kt_len: e.activation(out=scr1[:, 0:L], in_=ps_s[:, 0:L],
                                                             func=AF.Copy, scale=0.125))
                        dg = scr1[:, qi * 128:(qi + 1) * 128]
                        p.v(lambda e, dg=dg: e.tensor_add(dg, dg, cmask[:, :]))
                        softmax_row(nc, p, scr1, scr2, mx, attn_bf, kt_len)
                        c_copy = {}
                        c_mm = {}
                        c_sm = p.count
                        for kj in range(qi + 1):
                            b = kj % 2
                            pst = ps_trA if b == 0 else ps_trB
                            w_tr = c_sm if kj < 2 else c_copy[kj - 2]
                            p.run("tensor", lambda e, kj=kj, pst=pst: e.transpose(
                                pst[:, :], attn_bf[:, kj * 128:(kj + 1) * 128], ident[:, :]),
                                wait=w_tr)
                            c_tr = p.count
                            w_cp = c_tr if kj < 2 else max(c_tr, c_mm[kj - 2])
                            p.run("vector", lambda e, b=b, pst=pst: e.tensor_copy(
                                attnT[:, b * 128:(b + 1) * 128], pst[:, :]), wait=w_cp)
                            c_copy[kj] = p.count
                            p.run("tensor", lambda e, kj=kj, b=b, h=h: e.matmul(
                                ps_o[0:64, :], V[:, kj * D + h * DH:kj * D + (h + 1) * DH],
                                attnT[:, b * 128:(b + 1) * 128], start=(kj == 0), stop=(kj == qi)),
                                wait=c_copy[kj])
                            c_mm[kj] = p.count
                        p.v(lambda e, qi=qi, ft=ft, fo=fo: e.tensor_copy(
                            outcat[fo:fo + 64, ft * S + qi * 128:ft * S + (qi + 1) * 128], ps_o[0:64, :]))
            with ExitStack() as c3:
                wob = c3.enter_context(nc.sbuf_tensor("wobA", [128, 8 * D], BF16))
                awin = c3.enter_context(nc.sbuf_tensor("awin", [128, S], F32))
                for k in range(8):
                    load_cast_bf(nc, p, oT[k * 128:(k + 1) * 128, :], wob, k * D, D, scr1)
                ps_b = c3.enter_context(nc.psum_tensor("ps_bA", [128, 512], F32))
                for mo in range(8):
                    p.dma(awin[:, :], x1T[mo * 128:(mo + 1) * 128, :])
                    for n in range(4):
                        for k in range(8):
                            p.mm(ps_b[:, :], wob[:, k * D + mo * 128:k * D + (mo + 1) * 128],
                                 outcat[:, k * S + n * 512:k * S + (n + 1) * 512],
                                 start=(k == 0), stop=(k == 7))
                        dest = awin[:, n * 512:(n + 1) * 512]
                        p.v(lambda e, dest=dest: e.tensor_tensor(dest, dest, ps_b[:, :], ALU.add))
                    p.dma(acc2T_o[mo * 128:(mo + 1) * 128, :], awin[:, :])
            with ExitStack() as c2:
                wrb = c2.enter_context(nc.sbuf_tensor("wrb", [128, 8 * E1], BF16))
                scrw = c2.enter_context(nc.sbuf_tensor("scrw", [128, E1], F32))
                for k in range(8):
                    load_cast_bf(nc, p, wr[k * 128:(k + 1) * 128, :], wrb, k * E1, E1, scrw)
                sel1_p = c2.enter_context(nc.sbuf_tensor("sel1_sb", [128, 32], F32))
                gate1_p = c2.enter_context(nc.sbuf_tensor("gate1_sb", [128, 32], F32))
                probs = c2.enter_context(nc.sbuf_tensor("probs", [128, E1], F32))
                lgt = c2.enter_context(nc.sbuf_tensor("lgt", [128, E1], F32))
                mx = c2.enter_context(nc.sbuf_tensor("mx2", [128, 4], F32))
                mx8 = c2.enter_context(nc.sbuf_tensor("mx8", [128, 8], F32))
                mi8 = c2.enter_context(nc.sbuf_tensor("mi8", [128, 8], U32))
                ps_r = c2.enter_context(nc.psum_tensor("ps_r", [128, E1], F32))
                for t in range(16):
                    for k in range(8):
                        p.mm(ps_r[:, :], xn[:, k * S + t * 128:k * S + (t + 1) * 128],
                             wrb[:, k * E1:(k + 1) * E1], start=(k == 0), stop=(k == 7))
                    p.v(lambda e: e.tensor_copy(lgt[:, :], ps_r[:, :]))
                    softmax_row(nc, p, lgt, scr2, mx, probs, E1)
                    p.v(lambda e: e.max(out=mx8[:, :], in_=probs[:, 0:E1]))
                    p.v(lambda e: e.max_index(out=mi8[:, :], in_max=mx8[:, :], in_values=probs[:, 0:E1]))
                    p.v(lambda e, t=t: e.tensor_copy(gate1_p[:, t * 2:(t + 1) * 2], mx8[:, 0:2]))
                    p.v(lambda e, t=t: e.tensor_copy(sel1_p[:, t * 2:(t + 1) * 2], mi8[:, 0:2]))
                p.dma(sel1_o, sel1_p[:, :])
                p.dma(gate1_o, gate1_p[:, :])
            p.finish()
    return nc


def build_s4():
    nc = new_nc()
    xn1_tok = nc.dram_tensor("xn1_tok", [T, D], BF16, kind="ExternalInput").ap()
    sel1_all = nc.dram_tensor("sel1_all", [128, 128 * 2], F32, kind="ExternalInput").ap()
    wi_e = nc.dram_tensor("wi_e", [D, F1], F32, kind="ExternalInput").ap()
    wo_e = nc.dram_tensor("wo_e", [F1, D], F32, kind="ExternalInput").ap()
    eid1 = nc.dram_tensor("eid1", [1, 1], F32, kind="ExternalInput").ap()

    y1_o = nc.dram_tensor("y1_loc", [2 * C1, D], BF16, kind="ExternalOutput").ap()
    smap1_o = nc.dram_tensor("smap1_p", [128, 128 * 2], F32, kind="ExternalOutput").ap()
    xbuf1 = nc.dram_tensor("xbuf1", [2 * C1, D], BF16, kind="Internal").ap()

    with ExitStack() as ctx:
        sel_sb = ctx.enter_context(nc.sbuf_tensor("sel_sb", [128, 128 * 2], F32))
        A_sb = ctx.enter_context(nc.sbuf_tensor("A_sb", [128, 128 * 2], F32))
        pos_sb = ctx.enter_context(nc.sbuf_tensor("pos_sb", [128, 128 * 2], F32))
        smap_sb = ctx.enter_context(nc.sbuf_tensor("smap_sb", [128, 128 * 2], F32))
        slot_i = ctx.enter_context(nc.sbuf_tensor("slot_i", [128, 8], I32))
        eid_b = ctx.enter_context(nc.sbuf_tensor("eid_b", [128, 1], F32))
        eoff_b = ctx.enter_context(nc.sbuf_tensor("eoff_b", [128, 1], F32))
        eid_s = ctx.enter_context(nc.sbuf_tensor("eid_s", [1, 1], F32))
        tot_sb = ctx.enter_context(nc.sbuf_tensor("tot_sb", [1, 4], F32))
        tot0b = ctx.enter_context(nc.sbuf_tensor("tot0b", [128, 2], F32))
        pos = ctx.enter_context(nc.sbuf_tensor("pos", [128, 2], F32))
        tmp = ctx.enter_context(nc.sbuf_tensor("tmp", [128, 2], F32))
        tmp2 = ctx.enter_context(nc.sbuf_tensor("tmp2", [128, 2], F32))
        bad = ctx.enter_context(nc.sbuf_tensor("bad", [128, 2], F32))
        xtile = ctx.enter_context(nc.sbuf_tensor("xtile", [128, D], BF16))
        zero_t = ctx.enter_context(nc.sbuf_tensor("zero_t", [128, D], BF16))
        block = ctx.enter_context(nc.Block())

        @block.sync
        def _(sync):
            p = P(nc)
            ones_col, ones_row = mk_ones(nc, p, ctx)
            strictU = mk_strict_upper(nc, p, ctx)
            lowI = mk_lower_incl(nc, p, ctx)
            ident = mk_ident_bf(nc, p, ctx)
            breg = nc.gpsimd.alloc_register("bnd")
            nc.gpsimd.reg_mov(breg, 2 * C1 - 1)
            p.dma(sel_sb[:, :], sel1_all)
            p.dma(eid_s[:, :], eid1)
            p.g(lambda e: e.memset(zero_t[:, :], 0.0))
            for r in range(10):
                p.dma(xbuf1[r * 128:(r + 1) * 128, :], zero_t[:, :])
            with ExitStack() as c2:
                ps_bc = c2.enter_context(nc.psum_tensor("ps_bcS4", [128, 1], F32))
                p.mm(ps_bc[:, :], ones_row[0:1, :], eid_s[0:1, :], start=True, stop=True)
                p.v(lambda e: e.tensor_copy(eid_b[:, :], ps_bc[:, :]))
                p.v(lambda e: e.tensor_scalar(out=eoff_b[:, :], in0=eid_b[:, :],
                                              scalar1=float(2 * C1), scalar2=None, op0=ALU.mult))
                ps_cum = c2.enter_context(nc.psum_tensor("ps_cum", [128, 2], F32))
                for n in range(128):
                    for k in range(2):
                        p.v(lambda e, n=n, k=k: e.tensor_scalar(
                            out=A_sb[:, n * 2 + k:n * 2 + k + 1], in0=eid_b[:, 0:1],
                            scalar1=sel_sb[:, n * 2 + k:n * 2 + k + 1],
                            scalar2=None, op0=ALU.is_equal))
                    p.mm(ps_cum[:, :], strictU[:, :], A_sb[:, n * 2:(n + 1) * 2],
                         start=(n % 64 == 0), stop=False)
                    p.v(lambda e, n=n: e.tensor_copy(pos_sb[:, n * 2:(n + 1) * 2], ps_cum[:, :]))
                    p.mm(ps_cum[:, :], lowI[:, :], A_sb[:, n * 2:(n + 1) * 2],
                         start=False, stop=(n % 64 == 63))
                    if n % 64 == 63:
                        g = n // 64
                        p.v(lambda e, g=g: e.tensor_copy(tot_sb[0:1, g * 2:(g + 1) * 2], ps_cum[0:1, :]))
                for g in range(2):
                    p.mm(ps_bc[:, :], ones_row[0:1, :], tot_sb[0:1, g * 2:g * 2 + 1], start=True, stop=True)
                    p.v(lambda e, g=g: e.tensor_copy(tot0b[:, g:g + 1], ps_bc[:, :]))
                for n in range(128):
                    g = n // 64
                    p.v(lambda e, n=n: e.tensor_copy(pos[:, :], pos_sb[:, n * 2:(n + 1) * 2]))
                    p.v(lambda e, g=g: e.tensor_scalar(out=pos[:, 1:2], in0=pos[:, 1:2],
                                                       scalar1=tot0b[:, g:g + 1], scalar2=None,
                                                       op0=ALU.add))
                    p.v(lambda e: e.tensor_scalar(out=bad[:, :], in0=pos[:, :], scalar1=float(C1),
                                                  scalar2=None, op0=ALU.is_ge))
                    p.v(lambda e, n=n: e.tensor_scalar(out=tmp[:, :], in0=A_sb[:, n * 2:(n + 1) * 2],
                                                       scalar1=-1.0, scalar2=1.0,
                                                       op0=ALU.mult, op1=ALU.add))
                    p.v(lambda e: e.tensor_add(bad[:, :], bad[:, :], tmp[:, :]))
                    p.v(lambda e, g=g: e.tensor_scalar(out=tmp[:, :], in0=pos[:, :],
                                                       scalar1=float(g * C1), scalar2=None, op0=ALU.add))
                    p.v(lambda e: e.scalar_tensor_tensor(out=tmp2[:, :], in0=bad[:, :], scalar=OOB,
                                                         in1=tmp[:, :], op0=ALU.mult, op1=ALU.add))
                    p.v(lambda e, n=n: e.tensor_scalar(out=smap_sb[:, n * 2:(n + 1) * 2], in0=tmp2[:, :],
                                                       scalar1=eoff_b[:, 0:1], scalar2=None, op0=ALU.add))
                    p.dma(xtile[:, :], xn1_tok[n * 128:(n + 1) * 128, :])
                    p.v(lambda e: e.tensor_copy(slot_i[:, 0:2], tmp2[:, :]))
                    for k in range(2):
                        p.run("gpsimd", lambda e, k=k: e.indirect_dma_start(
                            out=xbuf1, out_offset=IndirectOffsetOnAxis(slot_i[:, k:k + 1], axis=0),
                            in_=xtile[:, :], in_offset=None,
                            bounds_check=breg, oob_is_err=False), inc=16, pipe=(k > 0))
            p.dma(smap1_o, smap_sb[:, :])

            with ExitStack() as c2:
                xbT = c2.enter_context(nc.sbuf_tensor("xbT", [128, 2 * 8 * C1], BF16))
                hh = c2.enter_context(nc.sbuf_tensor("hh", [128, 2 * 32 * C1], BF16))
                with ExitStack() as c3:
                    xb = c3.enter_context(nc.sbuf_tensor("xb", [128, 5 * D], BF16))
                    ps_tr = c3.enter_context(nc.psum_tensor("ps_tr2", [128, 128], BF16))
                    for g in range(2):
                        for tc in range(5):
                            p.dma(xb[:, tc * D:(tc + 1) * D],
                                  xbuf1[g * C1 + tc * 128:g * C1 + (tc + 1) * 128, :])
                        for tc in range(5):
                            for k in range(8):
                                p.run("tensor", lambda e, tc=tc, k=k: e.transpose(
                                    ps_tr[:, :], xb[:, tc * D + k * 128:tc * D + (k + 1) * 128],
                                    ident[:, :]))
                                p.v(lambda e, g=g, tc=tc, k=k: e.tensor_copy(
                                    xbT[:, g * 8 * C1 + k * C1 + tc * 128:
                                        g * 8 * C1 + k * C1 + (tc + 1) * 128], ps_tr[:, :]))
                with ExitStack() as c3:
                    wib = c3.enter_context(nc.sbuf_tensor("wib", [128, 8 * F1], BF16))
                    scrw = c3.enter_context(nc.sbuf_tensor("scrw", [128, 2048], F32))
                    ps_b = c3.enter_context(nc.psum_tensor("ps_b", [128, 512], F32))
                    for k in range(8):
                        for half in range(2):
                            load_cast_bf(nc, p, wi_e[k * 128:(k + 1) * 128, half * 2048:(half + 1) * 2048],
                                         wib, k * F1 + half * 2048, 2048, scrw)
                    for g in range(2):
                        for mf in range(32):
                            for (n0, w) in ((0, 512), (512, 128)):
                                for k in range(8):
                                    p.mm(ps_b[:, 0:w], wib[:, k * F1 + mf * 128:k * F1 + (mf + 1) * 128],
                                         xbT[:, g * 8 * C1 + k * C1 + n0:g * 8 * C1 + k * C1 + n0 + w],
                                         start=(k == 0), stop=(k == 7))
                                p.s(lambda e, g=g, mf=mf, n0=n0, w=w: e.activation(
                                    out=hh[:, g * 32 * C1 + mf * C1 + n0:g * 32 * C1 + mf * C1 + n0 + w],
                                    in_=ps_b[:, 0:w], func=AF.Relu))
                with ExitStack() as c3:
                    wob = c3.enter_context(nc.sbuf_tensor("wob", [128, 32 * D], BF16))
                    scrw = c3.enter_context(nc.sbuf_tensor("scrw2", [128, D], F32))
                    yt = c3.enter_context(nc.sbuf_tensor("yt", [128, D], BF16))
                    ps_b = c3.enter_context(nc.psum_tensor("ps_b2", [128, 512], F32))
                    for k in range(32):
                        load_cast_bf(nc, p, wo_e[k * 128:(k + 1) * 128, :], wob, k * D, D, scrw)
                    for g in range(2):
                        for tc in range(5):
                            for nch in range(2):
                                for k in range(32):
                                    p.mm(ps_b[:, :],
                                         hh[:, g * 32 * C1 + k * C1 + tc * 128:
                                            g * 32 * C1 + k * C1 + (tc + 1) * 128],
                                         wob[:, k * D + nch * 512:k * D + (nch + 1) * 512],
                                         start=(k == 0), stop=(k == 31))
                                p.v(lambda e, nch=nch: e.tensor_copy(yt[:, nch * 512:(nch + 1) * 512],
                                                                     ps_b[:, :]))
                            p.dma(y1_o[g * C1 + tc * 128:g * C1 + (tc + 1) * 128, :], yt[:, :])
            p.finish()
    return nc


def build_s5():
    nc = new_nc()
    acc2T = nc.dram_tensor("acc2T", [D, S], F32, kind="ExternalInput").ap()
    y1_full = nc.dram_tensor("y1_full", [E1 * 2 * C1, D], BF16, kind="ExternalInput").ap()
    smap1_own = nc.dram_tensor("smap1_own", [128, 16 * 2 * E1], F32, kind="ExternalInput").ap()
    sel1_own = nc.dram_tensor("sel1_own", [128, 32], F32, kind="ExternalInput").ap()
    gate1_own = nc.dram_tensor("gate1_own", [128, 32], F32, kind="ExternalInput").ap()

    outT_o = nc.dram_tensor("outT", [D, S], F32, kind="ExternalOutput").ap()

    with ExitStack() as ctx:
        acc = ctx.enter_context(nc.sbuf_tensor("acc", [128, 8 * S], F32))
        smap_sb = ctx.enter_context(nc.sbuf_tensor("smap_sb", [128, 16 * 2 * E1], F32))
        sel_sb = ctx.enter_context(nc.sbuf_tensor("sel_sb", [128, 32], F32))
        gt_sb = ctx.enter_context(nc.sbuf_tensor("gt_sb", [128, 32], F32))
        iota8 = ctx.enter_context(nc.sbuf_tensor("iota8", [128, E1], F32))
        oh = ctx.enter_context(nc.sbuf_tensor("oh", [128, E1], F32))
        ohs = ctx.enter_context(nc.sbuf_tensor("ohs", [128, E1], F32))
        slots = ctx.enter_context(nc.sbuf_tensor("slots", [128, 2], F32))
        slot_i = ctx.enter_context(nc.sbuf_tensor("slot_i", [128, 2], I32))
        wmask = ctx.enter_context(nc.sbuf_tensor("wmask", [128, 2], F32))
        valid = ctx.enter_context(nc.sbuf_tensor("valid", [128, 2], F32))
        gbuf = ctx.enter_context(nc.sbuf_tensor("gbuf", [128, 8 * D], BF16))
        routed = ctx.enter_context(nc.sbuf_tensor("routed", [128, D], F32))
        routed_bf = ctx.enter_context(nc.sbuf_tensor("routed_bf", [128, D], BF16))
        block = ctx.enter_context(nc.Block())

        @block.sync
        def _(sync):
            p = P(nc)
            ident = mk_ident_bf(nc, p, ctx)
            p.g(lambda e: e.iota(iota8[:, :], pattern=[[1, E1]], base=0, channel_multiplier=0,
                                 allow_small_or_imprecise_dtypes=True))
            p.g(lambda e: e.memset(gbuf[:, :], 0.0))
            for f in range(8):
                p.dma(acc[:, f * S:(f + 1) * S], acc2T[f * 128:(f + 1) * 128, :])
            breg = nc.gpsimd.alloc_register("bnd")
            nc.gpsimd.reg_mov(breg, E1 * 2 * C1 - 1)
            p.dma(smap_sb[:, :], smap1_own)
            p.dma(sel_sb[:, :], sel1_own)
            p.dma(gt_sb[:, :], gate1_own)
            with ExitStack() as c2:
                ps_tr = c2.enter_context(nc.psum_tensor("ps_tr", [128, 128], BF16))
                for n in range(16):
                    for k in range(2):
                        p.v(lambda e, n=n, k=k: e.tensor_scalar(
                            out=oh[:, :], in0=iota8[:, :], scalar1=sel_sb[:, n * 2 + k:n * 2 + k + 1],
                            scalar2=None, op0=ALU.is_equal))
                        p.v(lambda e, n=n, k=k: e.tensor_mul(
                            ohs[:, :], oh[:, :], smap_sb[:, n * 16 + k * 8:n * 16 + (k + 1) * 8]))
                        p.v(lambda e, k=k: e.tensor_reduce(slots[:, k:k + 1], ohs[:, :], axis=AX.X, op=ALU.add))
                    p.v(lambda e: e.tensor_scalar(out=valid[:, :], in0=slots[:, :], scalar1=OOB / 2,
                                                  scalar2=None, op0=ALU.is_lt))
                    p.v(lambda e, n=n: e.tensor_mul(wmask[:, :], valid[:, :], gt_sb[:, n * 2:(n + 1) * 2]))
                    p.v(lambda e: e.tensor_copy(slot_i[:, :], slots[:, :]))
                    for k in range(2):
                        p.run("gpsimd", lambda e, k=k: e.indirect_dma_start(
                            out=gbuf[:, k * D:(k + 1) * D], out_offset=None,
                            in_=y1_full, in_offset=IndirectOffsetOnAxis(slot_i[:, k:k + 1], axis=0),
                            bounds_check=breg, oob_is_err=False), inc=16, pipe=(k > 0))
                    for k in range(2):
                        if k == 0:
                            p.v(lambda e: e.tensor_scalar(out=routed[:, :], in0=gbuf[:, 0:D],
                                                          scalar1=wmask[:, 0:1], scalar2=None, op0=ALU.mult))
                        else:
                            p.v(lambda e, k=k: e.scalar_tensor_tensor(
                                out=routed[:, :], in0=gbuf[:, k * D:(k + 1) * D], scalar=wmask[:, k:k + 1],
                                in1=routed[:, :], op0=ALU.mult, op1=ALU.add))
                    p.v(lambda e: e.tensor_copy(routed_bf[:, :], routed[:, :]))
                    for f in range(8):
                        p.run("tensor", lambda e, f=f: e.transpose(
                            ps_tr[:, :], routed_bf[:, f * 128:(f + 1) * 128], ident[:, :]))
                        dest = acc[:, f * S + n * 128:f * S + (n + 1) * 128]
                        p.v(lambda e, dest=dest: e.tensor_tensor(dest, dest, ps_tr[:, :], ALU.add))
            for f in range(8):
                p.dma(outT_o[f * 128:(f + 1) * 128, :], acc[:, f * S:(f + 1) * S])
            p.finish()
    return nc


RG = [[0, 1, 2, 3, 4, 5, 6, 7]]


def build_fused():
    nc = new_nc()
    xT = nc.dram_tensor("xT", [D, S], F32, kind="ExternalInput").ap()
    wq = nc.dram_tensor("wq", [NH, DH, DH], F32, kind="ExternalInput").ap()
    wk = nc.dram_tensor("wk", [NH, DH, DH], F32, kind="ExternalInput").ap()
    wv = nc.dram_tensor("wv", [NH, DH, DH], F32, kind="ExternalInput").ap()
    wo = nc.dram_tensor("wo", [D, D], F32, kind="ExternalInput").ap()
    wsgu = nc.dram_tensor("wsgu", [D, 2 * SF], F32, kind="ExternalInput").ap()
    wsd = nc.dram_tensor("wsd", [SF, D], F32, kind="ExternalInput").ap()
    rwT = nc.dram_tensor("rwT", [D, E0], F32, kind="ExternalInput").ap()
    qT = nc.dram_tensor("qT", [D, D], F32, kind="ExternalInput").ap()
    kT = nc.dram_tensor("kT", [D, D], F32, kind="ExternalInput").ap()
    vT = nc.dram_tensor("vT", [D, D], F32, kind="ExternalInput").ap()
    oT = nc.dram_tensor("oT", [D, D], F32, kind="ExternalInput").ap()
    wr = nc.dram_tensor("wr", [D, E1], F32, kind="ExternalInput").ap()
    wg8 = nc.dram_tensor("wg8", [8, D, F0], F32, kind="ExternalInput").ap()
    wu8 = nc.dram_tensor("wu8", [8, D, F0], F32, kind="ExternalInput").ap()
    wd8 = nc.dram_tensor("wd8", [8, F0, D], F32, kind="ExternalInput").ap()
    eids = nc.dram_tensor("eids", [1, 8], F32, kind="ExternalInput").ap()
    basev = nc.dram_tensor("basev", [1, 1], F32, kind="ExternalInput").ap()
    wi_e = nc.dram_tensor("wi_e", [D, F1], F32, kind="ExternalInput").ap()
    wo_e = nc.dram_tensor("wo_e", [F1, D], F32, kind="ExternalInput").ap()
    eid1 = nc.dram_tensor("eid1", [1, 1], F32, kind="ExternalInput").ap()
    cvec = nc.dram_tensor("cvec", [1, 1], I32, kind="ExternalInput").ap()
    outT_o = nc.dram_tensor("outT", [D, S], F32, kind="ExternalOutput").ap()
    xtok_m = nc.dram_tensor("xtok_m", [S, D], BF16, kind="Internal").ap()
    xtok_all = nc.dram_tensor("xtok_all", [T, D], BF16, kind="Internal").ap()
    sel_m_d = nc.dram_tensor("sel_m_d", [128, 128], F32, kind="Internal").ap()
    sel_all_d = nc.dram_tensor("sel_all_d", [1024, 128], F32, kind="Internal").ap()
    smap_m_d = nc.dram_tensor("smap_m_d", [128, 1024], F32, kind="Internal").ap()
    smap_all_d = nc.dram_tensor("smap_all_d", [1024, 1024], F32, kind="Internal").ap()
    y_m_d = nc.dram_tensor("y_m_d", [8 * C0, D], BF16, kind="Internal").ap()
    y_all_d = nc.dram_tensor("y_all_d", [E0 * C0, D], BF16, kind="Internal").ap()
    xbuf = nc.dram_tensor("xbuf", [8 * C0, D], BF16, kind="Internal").ap()
    xtok1_m = nc.dram_tensor("xtok1_m", [S, D], BF16, kind="Internal").ap()
    xtok1_all = nc.dram_tensor("xtok1_all", [T, D], BF16, kind="Internal").ap()
    sel1_m_d = nc.dram_tensor("sel1_m_d", [128, 32], F32, kind="Internal").ap()
    sel1_all_d = nc.dram_tensor("sel1_all_d", [1024, 32], F32, kind="Internal").ap()
    smap1_m_d = nc.dram_tensor("smap1_m_d", [128, 256], F32, kind="Internal").ap()
    smap1_all_d = nc.dram_tensor("smap1_all_d", [1024, 256], F32, kind="Internal").ap()
    y1_m_d = nc.dram_tensor("y1_m_d", [2 * C1, D], BF16, kind="Internal").ap()
    y1_all_d = nc.dram_tensor("y1_all_d", [E1 * 2 * C1, D], BF16, kind="Internal").ap()
    xbuf1 = nc.dram_tensor("xbuf1", [2 * C1, D], BF16, kind="Internal").ap()

    with ExitStack() as ctx:
        acc = ctx.enter_context(nc.sbuf_tensor("acc", [128, 8 * S], F32))
        eps_tile = ctx.enter_context(nc.sbuf_tensor("eps", [1, 1], F32))
        sel_p = ctx.enter_context(nc.sbuf_tensor("sel_p_sb", [128, 128], F32))
        wnorm_p = ctx.enter_context(nc.sbuf_tensor("wnorm_p_sb", [128, 128], F32))
        sel1_p = ctx.enter_context(nc.sbuf_tensor("sel1_p_sb", [128, 32], F32))
        gate1_p = ctx.enter_context(nc.sbuf_tensor("gate1_p_sb", [128, 32], F32))
        block = ctx.enter_context(nc.Block())

        @block.sync
        def _(sync):
            p = P(nc)
            ident = mk_ident_bf(nc, p, ctx)
            cmask9 = mk_causal_add(nc, p, ctx, "cmask9", -1e9)
            cmask30 = mk_causal_add(nc, p, ctx, "cmask30", -1e30)
            ones_col, ones_row = mk_ones(nc, p, ctx)
            strictU = mk_strict_upper(nc, p, ctx)
            lowI = mk_lower_incl(nc, p, ctx)
            iota64 = ctx.enter_context(nc.sbuf_tensor("iota64", [128, E0], F32))
            iota8 = ctx.enter_context(nc.sbuf_tensor("iota8", [128, E1], F32))
            p.g(lambda e: e.iota(iota64[:, :], pattern=[[1, E0]], base=0, channel_multiplier=0,
                                 allow_small_or_imprecise_dtypes=True))
            p.g(lambda e: e.iota(iota8[:, :], pattern=[[1, E1]], base=0, channel_multiplier=0,
                                 allow_small_or_imprecise_dtypes=True))
            p.g(lambda e: e.memset(eps_tile[:, :], 1e-5))
            breg0 = nc.gpsimd.alloc_register("b0")
            breg1 = nc.gpsimd.alloc_register("b1")
            breg2 = nc.gpsimd.alloc_register("b2")
            breg3 = nc.gpsimd.alloc_register("b3")
            cvec_sb = ctx.enter_context(nc.sbuf_tensor("cvec_sb", [1, 1], I32))
            p.dma(cvec_sb[:, :], cvec)
            nc.sync.wait_ge(p.sem, p.count)
            cval = nc.sync.value_load(cvec_sb[0:1, 0:1], min_val=0, max_val=7)
            nc.gpsimd.reg_mov(breg0, 8 * C0 - 1)
            nc.gpsimd.reg_mov(breg1, E0 * C0 - 1)
            nc.gpsimd.reg_mov(breg2, 2 * C1 - 1)
            nc.gpsimd.reg_mov(breg3, E1 * 2 * C1 - 1)

            with ExitStack() as ctx_a:
                xn = ctx_a.enter_context(nc.sbuf_tensor("xn", [128, 8 * S], BF16))
                scr1 = ctx_a.enter_context(nc.sbuf_tensor("scr1", [128, S], F32))
                scr2 = ctx_a.enter_context(nc.sbuf_tensor("scr2", [128, S], F32))
                for f in range(8):
                    p.dma(acc[:, f * S:(f + 1) * S], xT[f * 128:(f + 1) * 128, :])
                layernorm_fm(nc, p, ctx_a, acc, xn, scr1, ones_col, ones_row, eps_tile, tag="L0")

                with ExitStack() as c2:
                    xtok_sb = c2.enter_context(nc.sbuf_tensor("xtok_sb", [128, D], BF16))
                    ps_tr = c2.enter_context(nc.psum_tensor("ps_trA", [128, 128], BF16))
                    for nl in range(16):
                        for f in range(8):
                            p.run("tensor", lambda e, nl=nl, f=f: e.transpose(
                                ps_tr[:, :], xn[:, f * S + nl * 128:f * S + (nl + 1) * 128], ident[:, :]))
                            p.v(lambda e, f=f: e.tensor_copy(xtok_sb[:, f * 128:(f + 1) * 128], ps_tr[:, :]))
                        p.dma(xtok_m[nl * 128:(nl + 1) * 128, :], xtok_sb[:, :])
                p.run("gpsimd", lambda e: e.collective_compute(
                    "AllGather", ALU.bypass, RG, [xtok_m], [xtok_all]), inc=16)

                with ExitStack() as c2:
                    rw_bf = c2.enter_context(nc.sbuf_tensor("rw_bf", [128, 8 * E0], BF16))
                    scrw = c2.enter_context(nc.sbuf_tensor("scrwR", [128, E0], F32))
                    for k in range(8):
                        load_cast_bf(nc, p, rwT[k * 128:(k + 1) * 128, :], rw_bf, k * E0, E0, scrw)
                    probs = c2.enter_context(nc.sbuf_tensor("probsR", [128, E0], F32))
                    lgt = c2.enter_context(nc.sbuf_tensor("lgtR", [128, E0], F32))
                    mx = c2.enter_context(nc.sbuf_tensor("mxR", [128, 4], F32))
                    mx8 = c2.enter_context(nc.sbuf_tensor("mx8R", [128, 8], F32))
                    mi8 = c2.enter_context(nc.sbuf_tensor("mi8R", [128, 8], U32))
                    ps_r = c2.enter_context(nc.psum_tensor("ps_rR", [128, E0], F32))
                    for t in range(16):
                        for k in range(8):
                            p.mm(ps_r[:, :], xn[:, k * S + t * 128:k * S + (t + 1) * 128],
                                 rw_bf[:, k * E0:(k + 1) * E0], start=(k == 0), stop=(k == 7))
                        p.v(lambda e: e.tensor_copy(lgt[:, :], ps_r[:, :]))
                        softmax_row(nc, p, lgt, scr2, mx, probs, E0)
                        p.v(lambda e: e.max(out=mx8[:, :], in_=probs[:, 0:E0]))
                        p.v(lambda e: e.max_index(out=mi8[:, :], in_max=mx8[:, :], in_values=probs[:, 0:E0]))
                        p.v(lambda e: e.tensor_reduce(mx[:, 0:1], mx8[:, :], axis=AX.X, op=ALU.add))
                        p.v(lambda e: e.reciprocal(mx[:, 1:2], mx[:, 0:1]))
                        p.v(lambda e, t=t: e.tensor_scalar(out=wnorm_p[:, t * 8:(t + 1) * 8], in0=mx8[:, :],
                                                           scalar1=mx[:, 1:2], scalar2=None, op0=ALU.mult))
                        p.v(lambda e, t=t: e.tensor_copy(sel_p[:, t * 8:(t + 1) * 8], mi8[:, :]))
                p.dma(sel_m_d, sel_p[:, :])
                p.run("gpsimd", lambda e: e.collective_compute(
                    "AllGather", ALU.bypass, RG, [sel_m_d], [sel_all_d]), inc=16)

                with ExitStack() as c2m:
                  M = c2m.enter_context(nc.sbuf_tensor("M", [128, 16 * D], BF16))
                  with ExitStack() as c2:
                    wq_bf = c2.enter_context(nc.sbuf_tensor("wq_bf", [128, NH * DH], BF16))
                    wk_bf = c2.enter_context(nc.sbuf_tensor("wk_bf", [128, NH * DH], BF16))
                    wv_bf = c2.enter_context(nc.sbuf_tensor("wv_bf", [128, NH * DH], BF16))
                    wf32 = c2.enter_context(nc.sbuf_tensor("wf32", [128, NH * DH], F32))
                    for name, w_ap, dst in (("q", wq, wq_bf), ("k", wk, wk_bf), ("v", wv, wv_bf)):
                        for h in range(NH):
                            p.dma(wf32[0:64, h * DH:(h + 1) * DH], w_ap[h])
                            p.dma(wf32[64:128, h * DH:(h + 1) * DH], w_ap[h])
                        p.v(lambda e, dst=dst: e.tensor_copy(dst[:, :], wf32[:, :]))
                    QT = c2.enter_context(nc.sbuf_tensor("QT", [64, S], BF16))
                    KT = c2.enter_context(nc.sbuf_tensor("KT", [64, S], BF16))
                    V = c2.enter_context(nc.sbuf_tensor("V", [128, 16 * DH], BF16))
                    attn_bf = c2.enter_context(nc.sbuf_tensor("attn_bf", [128, S], BF16))
                    attnT = c2.enter_context(nc.sbuf_tensor("attnT", [128, 128], BF16))
                    mx = c2.enter_context(nc.sbuf_tensor("mx", [128, 4], F32))
                    ps_s = c2.enter_context(nc.psum_tensor("ps_s", [128, S], F32))
                    ps_qk = c2.enter_context(nc.psum_tensor("ps_qk", [128, 512], F32))
                    ps_v = c2.enter_context(nc.psum_tensor("ps_v", [128, 64], F32))
                    ps_tr = c2.enter_context(nc.psum_tensor("ps_tr", [128, 128], BF16))
                    ps_o = c2.enter_context(nc.psum_tensor("ps_o", [128, 64], F32))
                    for h in range(NH):
                        ft, fo = h // 2, (h % 2) * 64
                        xh = lambda a, b: xn[fo:fo + 64, ft * S + a: ft * S + b]
                        for n in range(4):
                            p.mm(ps_qk[0:64, :], wq_bf[fo:fo + 64, h * DH:(h + 1) * DH],
                                 xh(n * 512, (n + 1) * 512), start=True, stop=True)
                            p.v(lambda e, n=n: e.tensor_copy(QT[0:64, n * 512:(n + 1) * 512], ps_qk[0:64, :]))
                            p.mm(ps_qk[0:64, :], wk_bf[fo:fo + 64, h * DH:(h + 1) * DH],
                                 xh(n * 512, (n + 1) * 512), start=True, stop=True)
                            p.v(lambda e, n=n: e.tensor_copy(KT[0:64, n * 512:(n + 1) * 512], ps_qk[0:64, :]))
                        for tc in range(16):
                            p.mm(ps_v[0:128, 0:64], xh(tc * 128, (tc + 1) * 128),
                                 wv_bf[fo:fo + 64, h * DH:(h + 1) * DH], start=True, stop=True)
                            p.v(lambda e, tc=tc: e.tensor_copy(V[:, tc * DH:(tc + 1) * DH], ps_v[0:128, 0:64]))
                        for qi in range(16):
                            kt_len = (qi + 1) * 128
                            for st in range(0, kt_len, 512):
                                w = min(512, kt_len - st)
                                p.mm(ps_s[:, st:st + w], QT[0:64, qi * 128:(qi + 1) * 128],
                                     KT[0:64, st:st + w], start=True, stop=True)
                            p.s(lambda e, L=kt_len: e.activation(out=scr1[:, 0:L], in_=ps_s[:, 0:L],
                                                                 func=AF.Copy, scale=0.125))
                            dg = scr1[:, qi * 128:(qi + 1) * 128]
                            p.v(lambda e, dg=dg: e.tensor_add(dg, dg, cmask9[:, :]))
                            softmax_row(nc, p, scr1, scr2, mx, attn_bf, kt_len)
                            for kj in range(qi + 1):
                                p.run("tensor", lambda e, kj=kj: e.transpose(
                                    ps_tr[:, :], attn_bf[:, kj * 128:(kj + 1) * 128], ident[:, :]))
                                p.v(lambda e: e.tensor_copy(attnT[:, :], ps_tr[:, :]))
                                p.mm(ps_o[0:128, 0:64], attnT[:, :], V[:, kj * DH:(kj + 1) * DH],
                                     start=(kj == 0), stop=(kj == qi))
                            p.v(lambda e, qi=qi, h=h: e.tensor_copy(
                                M[:, qi * D + h * DH: qi * D + (h + 1) * DH], ps_o[0:128, 0:64]))
                  with ExitStack() as c2:
                    wo_bf = c2.enter_context(nc.sbuf_tensor("wo_bf", [128, 8 * D], BF16))
                    for k in range(8):
                        load_cast_bf(nc, p, wo[k * 128:(k + 1) * 128, :], wo_bf, k * D, D, scr1)
                    ps_b = c2.enter_context(nc.psum_tensor("ps_b", [128, 512], F32))
                    for j0 in range(2):
                        for mo in range(8):
                            for rc in range(2):
                                for k in range(8):
                                    p.mm(ps_b[:, :], wo_bf[:, k * D + mo * 128:k * D + (mo + 1) * 128],
                                         M[:, (8 * j0 + k) * D + rc * 512:(8 * j0 + k) * D + (rc + 1) * 512],
                                         start=(k == 0), stop=(k == 7))
                                dest = acc[:, mo * S:(mo + 1) * S].rearrange(
                                    "p (r two) -> p r two", two=2)[:, rc * 512:(rc + 1) * 512, j0]
                                p.v(lambda e, dest=dest: e.tensor_tensor(dest, dest, ps_b[:, :], ALU.add))

                with ExitStack() as c2:
                    wsgu_bf = c2.enter_context(nc.sbuf_tensor("wsgu_bf", [128, 8 * 2 * SF], BF16))
                    wsd_bf = c2.enter_context(nc.sbuf_tensor("wsd_bf", [128, 6 * D], BF16))
                    scrw = c2.enter_context(nc.sbuf_tensor("scrwS", [128, 2 * SF], F32))
                    for k in range(8):
                        load_cast_bf(nc, p, wsgu[k * 128:(k + 1) * 128, :], wsgu_bf, k * 2 * SF, 2 * SF, scrw)
                    for k in range(6):
                        load_cast_bf(nc, p, wsd[k * 128:(k + 1) * 128, :], wsd_bf, k * D, D, scrw)
                    hT = c2.enter_context(nc.sbuf_tensor("hT", [128, 6 * S], BF16))
                    ps_b = c2.enter_context(nc.psum_tensor("ps_b2", [128, 512], F32))
                    for m in range(12):
                        for n in range(4):
                            for k in range(8):
                                p.mm(ps_b[:, :], wsgu_bf[:, k * 2 * SF + m * 128:k * 2 * SF + (m + 1) * 128],
                                     xn[:, k * S + n * 512:k * S + (n + 1) * 512],
                                     start=(k == 0), stop=(k == 7))
                            if m < 6:
                                p.s(lambda e, m=m, n=n: e.activation(
                                    out=hT[:, m * S + n * 512:m * S + (n + 1) * 512],
                                    in_=ps_b[:, :], func=AF.Silu))
                            else:
                                dst = hT[:, (m - 6) * S + n * 512:(m - 6) * S + (n + 1) * 512]
                                p.v(lambda e, dst=dst: e.tensor_tensor(dst, ps_b[:, :], dst, ALU.mult))
                    for mo in range(8):
                        for n in range(4):
                            for k in range(6):
                                p.mm(ps_b[:, :], wsd_bf[:, k * D + mo * 128:k * D + (mo + 1) * 128],
                                     hT[:, k * S + n * 512:k * S + (n + 1) * 512],
                                     start=(k == 0), stop=(k == 5))
                            dest = acc[:, mo * S + n * 512:mo * S + (n + 1) * 512]
                            p.v(lambda e, dest=dest: e.tensor_tensor(dest, dest, ps_b[:, :], ALU.add))

                with ExitStack() as c2s:
                    slot_i = c2s.enter_context(nc.sbuf_tensor("slot_i", [128, 1], I32))
                    eid_b = c2s.enter_context(nc.sbuf_tensor("eid_b", [128, 8], F32))
                    eidoff = c2s.enter_context(nc.sbuf_tensor("eidoff", [128, 8], F32))
                    base_b = c2s.enter_context(nc.sbuf_tensor("base_b", [128, 1], F32))
                    eid_s = c2s.enter_context(nc.sbuf_tensor("eid_s", [1, 8], F32))
                    base_s = c2s.enter_context(nc.sbuf_tensor("base_s", [1, 1], F32))
                    A = c2s.enter_context(nc.sbuf_tensor("A", [128, 8], F32))
                    eq = c2s.enter_context(nc.sbuf_tensor("eq", [128, 8], F32))
                    pos = c2s.enter_context(nc.sbuf_tensor("pos", [128, 8], F32))
                    tmp = c2s.enter_context(nc.sbuf_tensor("tmp", [128, 8], F32))
                    bad = c2s.enter_context(nc.sbuf_tensor("bad", [128, 8], F32))
                    xtile = c2s.enter_context(nc.sbuf_tensor("xtile", [128, D], BF16))
                    zero_t = c2s.enter_context(nc.sbuf_tensor("zero_t", [128, D], BF16))
                    smap_sb = c2s.enter_context(nc.sbuf_tensor("smap_sb", [128, 128 * 8], F32))
                    with ExitStack() as c2:
                        sel_sb = c2.enter_context(nc.sbuf_tensor("sel_sbX", [128, 128 * 8], F32))
                        p.dma(sel_sb[:, :], sel_all_d.rearrange("(r p) c -> p r c", p=128))
                        p.dma(eid_s[:, :], eids)
                        p.dma(base_s[:, :], basev)
                        p.g(lambda e: e.memset(zero_t[:, :], 0.0))
                        for r in range(32):
                            p.dma(xbuf[r * 128:(r + 1) * 128, :], zero_t[:, :])
                        ps_bc = c2.enter_context(nc.psum_tensor("ps_bcF2", [128, 8], F32))
                        p.mm(ps_bc[:, :], ones_row[0:1, :], eid_s[0:1, :], start=True, stop=True)
                        p.v(lambda e: e.tensor_copy(eid_b[:, :], ps_bc[:, :]))
                        p.mm(ps_bc[:, 0:1], ones_row[0:1, :], base_s[0:1, :], start=True, stop=True)
                        p.v(lambda e: e.tensor_copy(base_b[:, :], ps_bc[:, 0:1]))
                        p.v(lambda e: e.tensor_scalar(out=eidoff[:, :], in0=eid_b[:, :], scalar1=float(C0),
                                                      scalar2=None, op0=ALU.mult))
                        ps_cum = c2.enter_context(nc.psum_tensor("ps_cum", [128, 8], F32))
                        for n in range(128):
                            p.v(lambda e, n=n: e.tensor_scalar(out=A[:, :], in0=eid_b[:, :],
                                                               scalar1=sel_sb[:, n * 8:n * 8 + 1],
                                                               scalar2=None, op0=ALU.is_equal))
                            for k in range(1, 8):
                                p.v(lambda e, n=n, k=k: e.tensor_scalar(out=eq[:, :], in0=eid_b[:, :],
                                                                        scalar1=sel_sb[:, n * 8 + k:n * 8 + k + 1],
                                                                        scalar2=None, op0=ALU.is_equal))
                                p.v(lambda e: e.tensor_add(A[:, :], A[:, :], eq[:, :]))
                            p.mm(ps_cum[:, :], strictU[:, :], A[:, :], start=(n == 0), stop=False)
                            p.v(lambda e: e.tensor_copy(pos[:, :], ps_cum[:, :]))
                            p.v(lambda e: e.tensor_scalar(out=bad[:, :], in0=pos[:, :], scalar1=float(C0),
                                                          scalar2=None, op0=ALU.is_ge))
                            p.v(lambda e: e.tensor_scalar(out=tmp[:, :], in0=A[:, :], scalar1=-1.0,
                                                          scalar2=1.0, op0=ALU.mult, op1=ALU.add))
                            p.v(lambda e: e.tensor_add(bad[:, :], bad[:, :], tmp[:, :]))
                            p.v(lambda e: e.tensor_add(tmp[:, :], eidoff[:, :], pos[:, :]))
                            p.v(lambda e, n=n: e.scalar_tensor_tensor(
                                out=smap_sb[:, n * 8:(n + 1) * 8], in0=bad[:, :], scalar=OOB, in1=tmp[:, :],
                                op0=ALU.mult, op1=ALU.add))
                            p.v(lambda e, n=n: e.tensor_scalar(out=tmp[:, :], in0=smap_sb[:, n * 8:(n + 1) * 8],
                                                               scalar1=base_b[:, 0:1], scalar2=None,
                                                               op0=ALU.subtract))
                            p.mm(ps_cum[:, :], lowI[:, :], A[:, :], start=False, stop=(n == 127))
                            p.dma(xtile[:, :], xtok_all[n * 128:(n + 1) * 128, :])
                            for k in range(8):
                                p.v(lambda e, k=k: e.tensor_copy(slot_i[:, :], tmp[:, k:k + 1]))
                                p.run("gpsimd", lambda e: e.indirect_dma_start(
                                    out=xbuf, out_offset=IndirectOffsetOnAxis(slot_i[:, :], axis=0),
                                    in_=xtile[:, :], in_offset=None,
                                    bounds_check=breg0, oob_is_err=False), inc=16)
                    p.dma(smap_m_d, smap_sb[:, :])
                    p.run("gpsimd", lambda e: e.collective_compute(
                        "AllGather", ALU.bypass, RG, [smap_m_d], [smap_all_d]), inc=16)

                    with ExitStack() as c2:
                        wgb = c2.enter_context(nc.sbuf_tensor("wgb", [128, 8 * F0], BF16))
                        wub = c2.enter_context(nc.sbuf_tensor("wub", [128, 8 * F0], BF16))
                        wdb = c2.enter_context(nc.sbuf_tensor("wdb", [128, 6 * D], BF16))
                        scrw = c2.enter_context(nc.sbuf_tensor("scrwE", [128, D], F32))
                        xb = c2.enter_context(nc.sbuf_tensor("xb", [128, 4 * D], BF16))
                        xbT = c2.enter_context(nc.sbuf_tensor("xbT", [128, 8 * C0], BF16))
                        gs = c2.enter_context(nc.sbuf_tensor("gs", [128, 6 * C0], BF16))
                        hh = c2.enter_context(nc.sbuf_tensor("hh", [128, 6 * C0], BF16))
                        yt = c2.enter_context(nc.sbuf_tensor("yt", [128, D], BF16))
                        ps_b = c2.enter_context(nc.psum_tensor("ps_bE", [128, 512], F32))
                        ps_tr = c2.enter_context(nc.psum_tensor("ps_trE", [128, 128], BF16))
                        for le in range(8):
                            for k in range(8):
                                load_cast_bf(nc, p, wg8[le, k * 128:(k + 1) * 128, :], wgb, k * F0, F0, scrw)
                                load_cast_bf(nc, p, wu8[le, k * 128:(k + 1) * 128, :], wub, k * F0, F0, scrw)
                            for k in range(6):
                                load_cast_bf(nc, p, wd8[le, k * 128:(k + 1) * 128, :], wdb, k * D, D, scrw)
                            for tc in range(4):
                                p.dma(xb[:, tc * D:(tc + 1) * D],
                                      xbuf[le * C0 + tc * 128: le * C0 + (tc + 1) * 128, :])
                            for tc in range(4):
                                for k in range(8):
                                    p.run("tensor", lambda e, tc=tc, k=k: e.transpose(
                                        ps_tr[:, :], xb[:, tc * D + k * 128:tc * D + (k + 1) * 128], ident[:, :]))
                                    p.v(lambda e, tc=tc, k=k: e.tensor_copy(
                                        xbT[:, k * C0 + tc * 128:k * C0 + (tc + 1) * 128], ps_tr[:, :]))
                            for mf in range(6):
                                for k in range(8):
                                    p.mm(ps_b[:, :], wgb[:, k * F0 + mf * 128:k * F0 + (mf + 1) * 128],
                                         xbT[:, k * C0:(k + 1) * C0], start=(k == 0), stop=(k == 7))
                                p.s(lambda e, mf=mf: e.activation(out=gs[:, mf * C0:(mf + 1) * C0],
                                                                  in_=ps_b[:, :], func=AF.Silu))
                                for k in range(8):
                                    p.mm(ps_b[:, :], wub[:, k * F0 + mf * 128:k * F0 + (mf + 1) * 128],
                                         xbT[:, k * C0:(k + 1) * C0], start=(k == 0), stop=(k == 7))
                                p.v(lambda e, mf=mf: e.tensor_tensor(
                                    hh[:, mf * C0:(mf + 1) * C0], ps_b[:, :],
                                    gs[:, mf * C0:(mf + 1) * C0], ALU.mult))
                            for tc in range(4):
                                for nch in range(2):
                                    for k in range(6):
                                        p.mm(ps_b[:, :], hh[:, k * C0 + tc * 128:k * C0 + (tc + 1) * 128],
                                             wdb[:, k * D + nch * 512:k * D + (nch + 1) * 512],
                                             start=(k == 0), stop=(k == 5))
                                    p.v(lambda e, nch=nch: e.tensor_copy(yt[:, nch * 512:(nch + 1) * 512],
                                                                         ps_b[:, :]))
                                p.dma(y_m_d[le * C0 + tc * 128:le * C0 + (tc + 1) * 128, :], yt[:, :])
                p.run("gpsimd", lambda e: e.collective_compute(
                    "AllGather", ALU.bypass, RG, [y_m_d], [y_all_d]), inc=16)

                with ExitStack() as c2:
                    smap3 = c2.enter_context(nc.sbuf_tensor("smap3", [128, 1024], F32))
                    oh = c2.enter_context(nc.sbuf_tensor("oh", [128, E0], F32))
                    ohs = c2.enter_context(nc.sbuf_tensor("ohs", [128, E0], F32))
                    slots = c2.enter_context(nc.sbuf_tensor("slots", [128, 8], F32))
                    slot_i = c2.enter_context(nc.sbuf_tensor("slot_i3", [128, 1], I32))
                    wmask = c2.enter_context(nc.sbuf_tensor("wmask", [128, 8], F32))
                    valid = c2.enter_context(nc.sbuf_tensor("valid", [128, 8], F32))
                    gbuf = c2.enter_context(nc.sbuf_tensor("gbuf", [128, D], BF16))
                    routed = c2.enter_context(nc.sbuf_tensor("routed", [128, D], F32))
                    routed_bf = c2.enter_context(nc.sbuf_tensor("routed_bf", [128, D], BF16))
                    ps_tr = c2.enter_context(nc.psum_tensor("ps_tr3", [128, 128], BF16))
                    smap_view = smap_all_d.rearrange("(r p) (n le) -> r p n le", p=128, le=8)
                    for r in range(8):
                        p.dma(smap3[:, r * 128:(r + 1) * 128],
                              smap_view[r][:, bass.ts(cval, 16), :])
                    p.g(lambda e: e.memset(gbuf[:, :], 0.0))
                    smap3v = smap3.rearrange("p (r nl le) -> p r nl le", r=8, nl=16)
                    ohv = oh.rearrange("p (a b) -> p a b", b=8)
                    ohsv = ohs.rearrange("p (a b) -> p a b", b=8)
                    for n in range(16):
                        for k in range(8):
                            p.v(lambda e, n=n, k=k: e.tensor_scalar(
                                out=oh[:, :], in0=iota64[:, :], scalar1=sel_p[:, n * 8 + k:n * 8 + k + 1],
                                scalar2=None, op0=ALU.is_equal))
                            p.v(lambda e, n=n: e.tensor_tensor(ohsv[:, :, :], ohv[:, :, :],
                                                               smap3v[:, :, n, :], ALU.mult))
                            p.v(lambda e, k=k: e.tensor_reduce(slots[:, k:k + 1], ohs[:, :], axis=AX.X, op=ALU.add))
                        p.v(lambda e: e.tensor_scalar(out=valid[:, :], in0=slots[:, :], scalar1=OOB / 2,
                                                      scalar2=None, op0=ALU.is_lt))
                        p.v(lambda e, n=n: e.tensor_mul(wmask[:, :], valid[:, :], wnorm_p[:, n * 8:(n + 1) * 8]))
                        for k in range(8):
                            p.v(lambda e, k=k: e.tensor_copy(slot_i[:, :], slots[:, k:k + 1]))
                            p.run("gpsimd", lambda e: e.indirect_dma_start(
                                out=gbuf[:, :], out_offset=None,
                                in_=y_all_d, in_offset=IndirectOffsetOnAxis(slot_i[:, :], axis=0),
                                bounds_check=breg1, oob_is_err=False), inc=16)
                            if k == 0:
                                p.v(lambda e: e.tensor_scalar(out=routed[:, :], in0=gbuf[:, :],
                                                              scalar1=wmask[:, 0:1], scalar2=None, op0=ALU.mult))
                            else:
                                p.v(lambda e, k=k: e.scalar_tensor_tensor(
                                    out=routed[:, :], in0=gbuf[:, :], scalar=wmask[:, k:k + 1],
                                    in1=routed[:, :], op0=ALU.mult, op1=ALU.add))
                        p.v(lambda e: e.tensor_copy(routed_bf[:, :], routed[:, :]))
                        for f in range(8):
                            p.run("tensor", lambda e, f=f: e.transpose(
                                ps_tr[:, :], routed_bf[:, f * 128:(f + 1) * 128], ident[:, :]))
                            dest = acc[:, f * S + n * 128:f * S + (n + 1) * 128]
                            p.v(lambda e, dest=dest: e.tensor_tensor(dest, dest, ps_tr[:, :], ALU.add))
                layernorm_fm(nc, p, ctx_a, acc, xn, scr1, ones_col, ones_row, eps_tile, tag="L1")

                with ExitStack() as c2:
                    xtok_sb = c2.enter_context(nc.sbuf_tensor("xtok_sb1", [128, D], BF16))
                    ps_tr = c2.enter_context(nc.psum_tensor("ps_trB", [128, 128], BF16))
                    for nl in range(16):
                        for f in range(8):
                            p.run("tensor", lambda e, nl=nl, f=f: e.transpose(
                                ps_tr[:, :], xn[:, f * S + nl * 128:f * S + (nl + 1) * 128], ident[:, :]))
                            p.v(lambda e, f=f: e.tensor_copy(xtok_sb[:, f * 128:(f + 1) * 128], ps_tr[:, :]))
                        p.dma(xtok1_m[nl * 128:(nl + 1) * 128, :], xtok_sb[:, :])
                p.run("gpsimd", lambda e: e.collective_compute(
                    "AllGather", ALU.bypass, RG, [xtok1_m], [xtok1_all]), inc=16)

                with ExitStack() as c2:
                    wrb = c2.enter_context(nc.sbuf_tensor("wrb", [128, 8 * E1], BF16))
                    scrw = c2.enter_context(nc.sbuf_tensor("scrwR1", [128, E1], F32))
                    for k in range(8):
                        load_cast_bf(nc, p, wr[k * 128:(k + 1) * 128, :], wrb, k * E1, E1, scrw)
                    probs = c2.enter_context(nc.sbuf_tensor("probs1", [128, E1], F32))
                    lgt = c2.enter_context(nc.sbuf_tensor("lgt1", [128, E1], F32))
                    mx = c2.enter_context(nc.sbuf_tensor("mx1", [128, 4], F32))
                    mx8 = c2.enter_context(nc.sbuf_tensor("mx81", [128, 8], F32))
                    mi8 = c2.enter_context(nc.sbuf_tensor("mi81", [128, 8], U32))
                    ps_r = c2.enter_context(nc.psum_tensor("ps_r1", [128, E1], F32))
                    for t in range(16):
                        for k in range(8):
                            p.mm(ps_r[:, :], xn[:, k * S + t * 128:k * S + (t + 1) * 128],
                                 wrb[:, k * E1:(k + 1) * E1], start=(k == 0), stop=(k == 7))
                        p.v(lambda e: e.tensor_copy(lgt[:, :], ps_r[:, :]))
                        softmax_row(nc, p, lgt, scr2, mx, probs, E1)
                        p.v(lambda e: e.max(out=mx8[:, :], in_=probs[:, 0:E1]))
                        p.v(lambda e: e.max_index(out=mi8[:, :], in_max=mx8[:, :], in_values=probs[:, 0:E1]))
                        p.v(lambda e, t=t: e.tensor_copy(gate1_p[:, t * 2:(t + 1) * 2], mx8[:, 0:2]))
                        p.v(lambda e, t=t: e.tensor_copy(sel1_p[:, t * 2:(t + 1) * 2], mi8[:, 0:2]))
                p.dma(sel1_m_d, sel1_p[:, :])
                p.run("gpsimd", lambda e: e.collective_compute(
                    "AllGather", ALU.bypass, RG, [sel1_m_d], [sel1_all_d]), inc=16)

                with ExitStack() as c2o:
                  outcat = c2o.enter_context(nc.sbuf_tensor("outcat", [128, 8 * S], BF16))
                  with ExitStack() as c2:
                    wqb = c2.enter_context(nc.sbuf_tensor("wqb", [128, 8 * DH], BF16))
                    wkb = c2.enter_context(nc.sbuf_tensor("wkb", [128, 8 * DH], BF16))
                    wvb = c2.enter_context(nc.sbuf_tensor("wvb", [128, 8 * DH], BF16))
                    QT = c2.enter_context(nc.sbuf_tensor("QT1", [64, S], BF16))
                    KT = c2.enter_context(nc.sbuf_tensor("KT1", [64, S], BF16))
                    V = c2.enter_context(nc.sbuf_tensor("V1", [128, 16 * DH], BF16))
                    attn_bf = c2.enter_context(nc.sbuf_tensor("attn_bf1", [128, S], BF16))
                    attnT = c2.enter_context(nc.sbuf_tensor("attnT1", [128, 128], BF16))
                    mx = c2.enter_context(nc.sbuf_tensor("mxA", [128, 4], F32))
                    ps_s = c2.enter_context(nc.psum_tensor("ps_s1", [128, S], F32))
                    ps_qk = c2.enter_context(nc.psum_tensor("ps_qk1", [128, 512], F32))
                    ps_v = c2.enter_context(nc.psum_tensor("ps_v1", [128, 64], F32))
                    ps_tr = c2.enter_context(nc.psum_tensor("ps_tr1", [128, 128], BF16))
                    ps_o = c2.enter_context(nc.psum_tensor("ps_o1", [128, 128], F32))
                    for h in range(NH):
                        ft, fo = h // 2, (h % 2) * 64
                        for k in range(8):
                            load_cast_bf(nc, p, qT[k * 128:(k + 1) * 128, h * DH:(h + 1) * DH],
                                         wqb, k * DH, DH, scr1)
                            load_cast_bf(nc, p, kT[k * 128:(k + 1) * 128, h * DH:(h + 1) * DH],
                                         wkb, k * DH, DH, scr1)
                            load_cast_bf(nc, p, vT[k * 128:(k + 1) * 128, h * DH:(h + 1) * DH],
                                         wvb, k * DH, DH, scr1)
                        for n in range(4):
                            for k in range(8):
                                p.mm(ps_qk[0:64, :], wqb[:, k * DH:(k + 1) * DH],
                                     xn[:, k * S + n * 512:k * S + (n + 1) * 512],
                                     start=(k == 0), stop=(k == 7))
                            p.v(lambda e, n=n: e.tensor_copy(QT[0:64, n * 512:(n + 1) * 512], ps_qk[0:64, :]))
                            for k in range(8):
                                p.mm(ps_qk[0:64, :], wkb[:, k * DH:(k + 1) * DH],
                                     xn[:, k * S + n * 512:k * S + (n + 1) * 512],
                                     start=(k == 0), stop=(k == 7))
                            p.v(lambda e, n=n: e.tensor_copy(KT[0:64, n * 512:(n + 1) * 512], ps_qk[0:64, :]))
                        for tc in range(16):
                            for k in range(8):
                                p.mm(ps_v[0:128, 0:64], xn[:, k * S + tc * 128:k * S + (tc + 1) * 128],
                                     wvb[:, k * DH:(k + 1) * DH],
                                     start=(k == 0), stop=(k == 7))
                            p.v(lambda e, tc=tc: e.tensor_copy(V[:, tc * DH:(tc + 1) * DH], ps_v[0:128, 0:64]))
                        for qi in range(16):
                            kt_len = (qi + 1) * 128
                            for st in range(0, kt_len, 512):
                                w = min(512, kt_len - st)
                                p.mm(ps_s[:, st:st + w], QT[0:64, qi * 128:(qi + 1) * 128],
                                     KT[0:64, st:st + w], start=True, stop=True)
                            p.s(lambda e, L=kt_len: e.activation(out=scr1[:, 0:L], in_=ps_s[:, 0:L],
                                                                 func=AF.Copy, scale=0.125))
                            dg = scr1[:, qi * 128:(qi + 1) * 128]
                            p.v(lambda e, dg=dg: e.tensor_add(dg, dg, cmask30[:, :]))
                            softmax_row(nc, p, scr1, scr2, mx, attn_bf, kt_len)
                            for kj in range(qi + 1):
                                p.run("tensor", lambda e, kj=kj: e.transpose(
                                    ps_tr[:, :], attn_bf[:, kj * 128:(kj + 1) * 128], ident[:, :]))
                                p.v(lambda e: e.tensor_copy(attnT[:, :], ps_tr[:, :]))
                                p.mm(ps_o[0:64, :], V[:, kj * DH:(kj + 1) * DH], attnT[:, :],
                                     start=(kj == 0), stop=(kj == qi))
                            p.v(lambda e, qi=qi, ft=ft, fo=fo: e.tensor_copy(
                                outcat[fo:fo + 64, ft * S + qi * 128:ft * S + (qi + 1) * 128], ps_o[0:64, :]))
                  with ExitStack() as c3:
                        wob = c3.enter_context(nc.sbuf_tensor("wobA", [128, 8 * D], BF16))
                        for k in range(8):
                            load_cast_bf(nc, p, oT[k * 128:(k + 1) * 128, :], wob, k * D, D, scr1)
                        ps_b = c3.enter_context(nc.psum_tensor("ps_bA", [128, 512], F32))
                        for mo in range(8):
                            for n in range(4):
                                for k in range(8):
                                    p.mm(ps_b[:, :], wob[:, k * D + mo * 128:k * D + (mo + 1) * 128],
                                         outcat[:, k * S + n * 512:k * S + (n + 1) * 512],
                                         start=(k == 0), stop=(k == 7))
                                dest = acc[:, mo * S + n * 512:mo * S + (n + 1) * 512]
                                p.v(lambda e, dest=dest: e.tensor_tensor(dest, dest, ps_b[:, :], ALU.add))

            with ExitStack() as c2s:
                sel_sb = c2s.enter_context(nc.sbuf_tensor("sel_sb4", [128, 128 * 2], F32))
                A_sb = c2s.enter_context(nc.sbuf_tensor("A_sb", [128, 128 * 2], F32))
                pos_sb = c2s.enter_context(nc.sbuf_tensor("pos_sb", [128, 128 * 2], F32))
                smap_sb = c2s.enter_context(nc.sbuf_tensor("smap_sb4", [128, 128 * 2], F32))
                slot_i = c2s.enter_context(nc.sbuf_tensor("slot_i4", [128, 1], I32))
                eid_b = c2s.enter_context(nc.sbuf_tensor("eid_b4", [128, 1], F32))
                eoff_b = c2s.enter_context(nc.sbuf_tensor("eoff_b4", [128, 1], F32))
                eid_s = c2s.enter_context(nc.sbuf_tensor("eid_s4", [1, 1], F32))
                tot_sb = c2s.enter_context(nc.sbuf_tensor("tot_sb", [1, 4], F32))
                tot0b = c2s.enter_context(nc.sbuf_tensor("tot0b", [128, 2], F32))
                pos = c2s.enter_context(nc.sbuf_tensor("pos4", [128, 2], F32))
                tmp = c2s.enter_context(nc.sbuf_tensor("tmp4", [128, 2], F32))
                tmp2 = c2s.enter_context(nc.sbuf_tensor("tmp24", [128, 2], F32))
                bad = c2s.enter_context(nc.sbuf_tensor("bad4", [128, 2], F32))
                xtile = c2s.enter_context(nc.sbuf_tensor("xtile4", [128, D], BF16))
                zero_t = c2s.enter_context(nc.sbuf_tensor("zero_t4", [128, D], BF16))
                p.dma(sel_sb[:, :], sel1_all_d.rearrange("(r p) c -> p r c", p=128))
                p.dma(eid_s[:, :], eid1)
                p.g(lambda e: e.memset(zero_t[:, :], 0.0))
                for r in range(10):
                    p.dma(xbuf1[r * 128:(r + 1) * 128, :], zero_t[:, :])
                with ExitStack() as c2:
                    ps_bc = c2.enter_context(nc.psum_tensor("ps_bc4", [128, 1], F32))
                    p.mm(ps_bc[:, :], ones_row[0:1, :], eid_s[0:1, :], start=True, stop=True)
                    p.v(lambda e: e.tensor_copy(eid_b[:, :], ps_bc[:, :]))
                    p.v(lambda e: e.tensor_scalar(out=eoff_b[:, :], in0=eid_b[:, :],
                                                  scalar1=float(2 * C1), scalar2=None, op0=ALU.mult))
                    ps_cum = c2.enter_context(nc.psum_tensor("ps_cum4", [128, 2], F32))
                    for n in range(128):
                        for k in range(2):
                            p.v(lambda e, n=n, k=k: e.tensor_scalar(
                                out=A_sb[:, n * 2 + k:n * 2 + k + 1], in0=eid_b[:, 0:1],
                                scalar1=sel_sb[:, n * 2 + k:n * 2 + k + 1],
                                scalar2=None, op0=ALU.is_equal))
                        p.mm(ps_cum[:, :], strictU[:, :], A_sb[:, n * 2:(n + 1) * 2],
                             start=(n % 64 == 0), stop=False)
                        p.v(lambda e, n=n: e.tensor_copy(pos_sb[:, n * 2:(n + 1) * 2], ps_cum[:, :]))
                        p.mm(ps_cum[:, :], lowI[:, :], A_sb[:, n * 2:(n + 1) * 2],
                             start=False, stop=(n % 64 == 63))
                        if n % 64 == 63:
                            g = n // 64
                            p.v(lambda e, g=g: e.tensor_copy(tot_sb[0:1, g * 2:(g + 1) * 2], ps_cum[0:1, :]))
                    for g in range(2):
                        p.mm(ps_bc[:, :], ones_row[0:1, :], tot_sb[0:1, g * 2:g * 2 + 1], start=True, stop=True)
                        p.v(lambda e, g=g: e.tensor_copy(tot0b[:, g:g + 1], ps_bc[:, :]))
                    for n in range(128):
                        g = n // 64
                        p.v(lambda e, n=n: e.tensor_copy(pos[:, :], pos_sb[:, n * 2:(n + 1) * 2]))
                        p.v(lambda e, g=g: e.tensor_scalar(out=pos[:, 1:2], in0=pos[:, 1:2],
                                                           scalar1=tot0b[:, g:g + 1], scalar2=None,
                                                           op0=ALU.add))
                        p.v(lambda e: e.tensor_scalar(out=bad[:, :], in0=pos[:, :], scalar1=float(C1),
                                                      scalar2=None, op0=ALU.is_ge))
                        p.v(lambda e, n=n: e.tensor_scalar(out=tmp[:, :], in0=A_sb[:, n * 2:(n + 1) * 2],
                                                           scalar1=-1.0, scalar2=1.0,
                                                           op0=ALU.mult, op1=ALU.add))
                        p.v(lambda e: e.tensor_add(bad[:, :], bad[:, :], tmp[:, :]))
                        p.v(lambda e, g=g: e.tensor_scalar(out=tmp[:, :], in0=pos[:, :],
                                                           scalar1=float(g * C1), scalar2=None, op0=ALU.add))
                        p.v(lambda e: e.scalar_tensor_tensor(out=tmp2[:, :], in0=bad[:, :], scalar=OOB,
                                                             in1=tmp[:, :], op0=ALU.mult, op1=ALU.add))
                        p.v(lambda e, n=n: e.tensor_scalar(out=smap_sb[:, n * 2:(n + 1) * 2], in0=tmp2[:, :],
                                                           scalar1=eoff_b[:, 0:1], scalar2=None, op0=ALU.add))
                        p.dma(xtile[:, :], xtok1_all[n * 128:(n + 1) * 128, :])
                        for k in range(2):
                            p.v(lambda e, k=k: e.tensor_copy(slot_i[:, :], tmp2[:, k:k + 1]))
                            p.run("gpsimd", lambda e: e.indirect_dma_start(
                                out=xbuf1, out_offset=IndirectOffsetOnAxis(slot_i[:, :], axis=0),
                                in_=xtile[:, :], in_offset=None,
                                bounds_check=breg2, oob_is_err=False), inc=16)
                p.dma(smap1_m_d, smap_sb[:, :])
                p.run("gpsimd", lambda e: e.collective_compute(
                    "AllGather", ALU.bypass, RG, [smap1_m_d], [smap1_all_d]), inc=16)

            with ExitStack() as c2:
                xbT = c2.enter_context(nc.sbuf_tensor("xbT4", [128, 8 * C1], BF16))
                hh = c2.enter_context(nc.sbuf_tensor("hh4", [128, 32 * C1], BF16))
                for g in range(2):
                    with ExitStack() as c3:
                        xb = c3.enter_context(nc.sbuf_tensor("xb4_%d" % g, [128, 5 * D], BF16))
                        ps_tr = c3.enter_context(nc.psum_tensor("ps_tr4_%d" % g, [128, 128], BF16))
                        for tc in range(5):
                            p.dma(xb[:, tc * D:(tc + 1) * D],
                                  xbuf1[g * C1 + tc * 128:g * C1 + (tc + 1) * 128, :])
                        for tc in range(5):
                            for k in range(8):
                                p.run("tensor", lambda e, tc=tc, k=k: e.transpose(
                                    ps_tr[:, :], xb[:, tc * D + k * 128:tc * D + (k + 1) * 128],
                                    ident[:, :]))
                                p.v(lambda e, tc=tc, k=k: e.tensor_copy(
                                    xbT[:, k * C1 + tc * 128:k * C1 + (tc + 1) * 128], ps_tr[:, :]))
                    with ExitStack() as c3:
                        wib = c3.enter_context(nc.sbuf_tensor("wib4_%d" % g, [128, 8 * F1], BF16))
                        scrw = c3.enter_context(nc.sbuf_tensor("scrw4_%d" % g, [128, 2048], F32))
                        ps_b = c3.enter_context(nc.psum_tensor("ps_b4_%d" % g, [128, 512], F32))
                        for k in range(8):
                            for half in range(2):
                                load_cast_bf(nc, p, wi_e[k * 128:(k + 1) * 128, half * 2048:(half + 1) * 2048],
                                             wib, k * F1 + half * 2048, 2048, scrw)
                        for mf in range(32):
                            for (n0, w) in ((0, 512), (512, 128)):
                                for k in range(8):
                                    p.mm(ps_b[:, 0:w], wib[:, k * F1 + mf * 128:k * F1 + (mf + 1) * 128],
                                         xbT[:, k * C1 + n0:k * C1 + n0 + w], start=(k == 0), stop=(k == 7))
                                p.s(lambda e, mf=mf, n0=n0, w=w: e.activation(
                                    out=hh[:, mf * C1 + n0:mf * C1 + n0 + w],
                                    in_=ps_b[:, 0:w], func=AF.Relu))
                    with ExitStack() as c3:
                        wob = c3.enter_context(nc.sbuf_tensor("wob4_%d" % g, [128, 32 * D], BF16))
                        scrw = c3.enter_context(nc.sbuf_tensor("scrw4b_%d" % g, [128, D], F32))
                        yt = c3.enter_context(nc.sbuf_tensor("yt4_%d" % g, [128, D], BF16))
                        ps_b = c3.enter_context(nc.psum_tensor("ps_b4b_%d" % g, [128, 512], F32))
                        for k in range(32):
                            load_cast_bf(nc, p, wo_e[k * 128:(k + 1) * 128, :], wob, k * D, D, scrw)
                        for tc in range(5):
                            for nch in range(2):
                                for k in range(32):
                                    p.mm(ps_b[:, :],
                                         hh[:, k * C1 + tc * 128:k * C1 + (tc + 1) * 128],
                                         wob[:, k * D + nch * 512:k * D + (nch + 1) * 512],
                                         start=(k == 0), stop=(k == 31))
                                p.v(lambda e, nch=nch: e.tensor_copy(yt[:, nch * 512:(nch + 1) * 512],
                                                                     ps_b[:, :]))
                            p.dma(y1_m_d[g * C1 + tc * 128:g * C1 + (tc + 1) * 128, :], yt[:, :])
            p.run("gpsimd", lambda e: e.collective_compute(
                "AllGather", ALU.bypass, RG, [y1_m_d], [y1_all_d]), inc=16)

            with ExitStack() as c2:
                smap5 = c2.enter_context(nc.sbuf_tensor("smap5", [128, 256], F32))
                oh = c2.enter_context(nc.sbuf_tensor("oh5", [128, E1], F32))
                ohs = c2.enter_context(nc.sbuf_tensor("ohs5", [128, E1], F32))
                slots = c2.enter_context(nc.sbuf_tensor("slots5", [128, 2], F32))
                slot_i = c2.enter_context(nc.sbuf_tensor("slot_i5", [128, 1], I32))
                wmask = c2.enter_context(nc.sbuf_tensor("wmask5", [128, 2], F32))
                valid = c2.enter_context(nc.sbuf_tensor("valid5", [128, 2], F32))
                gbuf = c2.enter_context(nc.sbuf_tensor("gbuf5", [128, D], BF16))
                routed = c2.enter_context(nc.sbuf_tensor("routed5", [128, D], F32))
                routed_bf = c2.enter_context(nc.sbuf_tensor("routed_bf5", [128, D], BF16))
                ps_tr = c2.enter_context(nc.psum_tensor("ps_tr5", [128, 128], BF16))
                smap1_view = smap1_all_d.rearrange("(r p) (n k) -> r p n k", p=128, k=2)
                for r in range(8):
                    p.dma(smap5[:, r * 32:(r + 1) * 32],
                          smap1_view[r][:, bass.ts(cval, 16), :])
                p.g(lambda e: e.memset(gbuf[:, :], 0.0))
                smap5v = smap5.rearrange("p (r c) -> p r c", r=8)
                for n in range(16):
                    for k in range(2):
                        p.v(lambda e, n=n, k=k: e.tensor_scalar(
                            out=oh[:, :], in0=iota8[:, :], scalar1=sel1_p[:, n * 2 + k:n * 2 + k + 1],
                            scalar2=None, op0=ALU.is_equal))
                        p.v(lambda e, n=n, k=k: e.tensor_tensor(ohs[:, :], oh[:, :],
                                                                smap5v[:, :, n * 2 + k], ALU.mult))
                        p.v(lambda e, k=k: e.tensor_reduce(slots[:, k:k + 1], ohs[:, :], axis=AX.X, op=ALU.add))
                    p.v(lambda e: e.tensor_scalar(out=valid[:, :], in0=slots[:, :], scalar1=OOB / 2,
                                                  scalar2=None, op0=ALU.is_lt))
                    p.v(lambda e, n=n: e.tensor_mul(wmask[:, :], valid[:, :], gate1_p[:, n * 2:(n + 1) * 2]))
                    for k in range(2):
                        p.v(lambda e, k=k: e.tensor_copy(slot_i[:, :], slots[:, k:k + 1]))
                        p.run("gpsimd", lambda e: e.indirect_dma_start(
                            out=gbuf[:, :], out_offset=None,
                            in_=y1_all_d, in_offset=IndirectOffsetOnAxis(slot_i[:, :], axis=0),
                            bounds_check=breg3, oob_is_err=False), inc=16)
                        if k == 0:
                            p.v(lambda e: e.tensor_scalar(out=routed[:, :], in0=gbuf[:, :],
                                                          scalar1=wmask[:, 0:1], scalar2=None, op0=ALU.mult))
                        else:
                            p.v(lambda e, k=k: e.scalar_tensor_tensor(
                                out=routed[:, :], in0=gbuf[:, :], scalar=wmask[:, k:k + 1],
                                in1=routed[:, :], op0=ALU.mult, op1=ALU.add))
                    p.v(lambda e: e.tensor_copy(routed_bf[:, :], routed[:, :]))
                    for f in range(8):
                        p.run("tensor", lambda e, f=f: e.transpose(
                            ps_tr[:, :], routed_bf[:, f * 128:(f + 1) * 128], ident[:, :]))
                        dest = acc[:, f * S + n * 128:f * S + (n + 1) * 128]
                        p.v(lambda e, dest=dest: e.tensor_tensor(dest, dest, ps_tr[:, :], ALU.add))
            for f in range(8):
                p.dma(outT_o[f * 128:(f + 1) * 128, :], acc[:, f * S:(f + 1) * S])
            p.finish()
    return nc


def kernel_fused(inp):
    x = np.asarray(inp["L0_attn_x"])
    cores = list(range(8))
    nc = _get("fused", build_fused)
    common = dict(
        wq=_f32c(inp["L0_attn_Wq"]), wk=_f32c(inp["L0_attn_Wk"]), wv=_f32c(inp["L0_attn_Wv"]),
        wo=_f32c(inp["L0_attn_W_O"]), wsgu=_f32c(inp["L0_ffn_ws_gate_up"]),
        wsd=_f32c(inp["L0_ffn_ws_down"]), rwT=_f32c(np.asarray(inp["L0_ffn_router_w"]).T),
        qT=_f32c(np.asarray(inp["L1_attn_q_proj_weight"]).T),
        kT=_f32c(np.asarray(inp["L1_attn_k_proj_weight"]).T),
        vT=_f32c(np.asarray(inp["L1_attn_v_proj_weight"]).T),
        oT=_f32c(np.asarray(inp["L1_attn_o_proj_weight"]).T),
        wr=_f32c(inp["L1_ffn_w_router"]))
    in_maps = []
    for c in cores:
        in_maps.append(dict(
            xT=_f32c(x[c].T),
            wg8=_f32c(inp["L0_ffn_w_gate"][8 * c:8 * c + 8]),
            wu8=_f32c(inp["L0_ffn_w_up"][8 * c:8 * c + 8]),
            wd8=_f32c(inp["L0_ffn_w_down"][8 * c:8 * c + 8]),
            eids=_f32c(np.arange(8 * c, 8 * c + 8, dtype=np.float32).reshape(1, 8)),
            basev=_f32c(np.array([[c * 8 * C0]], dtype=np.float32)),
            wi_e=_f32c(inp["L1_ffn_wi"][c]), wo_e=_f32c(inp["L1_ffn_wo"][c]),
            eid1=_f32c(np.array([[c]], dtype=np.float32)),
            cvec=np.array([[c]], dtype=np.int32),
            **common))
    res = run_bass_kernel_spmd(nc, in_maps, core_ids=cores).results
    out = np.stack([np.ascontiguousarray(np.asarray(res[c]["outT"]).T) for c in cores], axis=0)
    return out.astype(np.float32)


_cache = {}


def _get(name, builder):
    if name not in _cache:
        _cache[name] = builder()
    return _cache[name]


def _f32c(a):
    return np.ascontiguousarray(a, dtype=np.float32)


FUSED = False


def kernel(**inp):
    if FUSED:
        return kernel_fused(inp)
    return kernel_unfused(**inp)


def kernel_unfused(L0_attn_x, L0_attn_Wq, L0_attn_Wk, L0_attn_Wv, L0_attn_W_O,
           L0_ffn_router_w, L0_ffn_w_gate, L0_ffn_w_up, L0_ffn_w_down,
           L0_ffn_ws_gate_up, L0_ffn_ws_down,
           L1_attn_q_proj_weight, L1_attn_k_proj_weight, L1_attn_v_proj_weight,
           L1_attn_o_proj_weight, L1_ffn_w_router, L1_ffn_wi, L1_ffn_wo):
    x = np.asarray(L0_attn_x)
    cores = list(range(8))

    nc1 = _get("s1", build_s1)
    common1 = dict(
        wq=_f32c(L0_attn_Wq), wk=_f32c(L0_attn_Wk), wv=_f32c(L0_attn_Wv),
        wo=_f32c(L0_attn_W_O), wsgu=_f32c(L0_ffn_ws_gate_up), wsd=_f32c(L0_ffn_ws_down),
        rwT=_f32c(np.asarray(L0_ffn_router_w).T))
    in1 = [dict(xT=_f32c(x[c].T), **common1) for c in cores]
    r1 = run_bass_kernel_spmd(nc1, in1, core_ids=cores).results

    xn_tok = np.ascontiguousarray(
        np.concatenate([np.asarray(r1[c]["xnT"]).T for c in cores], axis=0))
    sel_all = np.ascontiguousarray(np.concatenate([r1[c]["sel_p"] for c in cores], axis=1))
    nc2 = _get("s2", build_s2)
    in2 = []
    for c in cores:
        in2.append(dict(
            xn_tok=xn_tok, sel_all=sel_all,
            wg8=_f32c(L0_ffn_w_gate[8 * c:8 * c + 8]),
            wu8=_f32c(L0_ffn_w_up[8 * c:8 * c + 8]),
            wd8=_f32c(L0_ffn_w_down[8 * c:8 * c + 8]),
            eids=_f32c(np.arange(8 * c, 8 * c + 8, dtype=np.float32).reshape(1, 8)),
            basev=_f32c(np.array([[c * 8 * C0]], dtype=np.float32))))
    r2 = run_bass_kernel_spmd(nc2, in2, core_ids=cores).results

    y_full = np.ascontiguousarray(np.concatenate([np.asarray(r2[c]["y_loc"]) for c in cores], axis=0))
    smaps = [np.asarray(r2[e]["smap_p"]).reshape(128, 128, 8) for e in cores]
    nc3 = _get("s3", build_s3)
    in3 = []
    for c in cores:
        so = np.concatenate([smaps[e][:, c * 16:(c + 1) * 16, :] for e in cores], axis=2)
        in3.append(dict(
            accT=np.asarray(r1[c]["accT"]), y_full=y_full,
            smap_own=np.ascontiguousarray(so.reshape(128, 16 * 64), dtype=np.float32),
            sel_own=np.asarray(r1[c]["sel_p"]), wnorm_own=np.asarray(r1[c]["wnorm_p"])))
    r3 = run_bass_kernel_spmd(nc3, in3, core_ids=cores).results

    nc3b = _get("s3b", build_s3b)
    common3b = dict(
        qT=_f32c(np.asarray(L1_attn_q_proj_weight).T), kT=_f32c(np.asarray(L1_attn_k_proj_weight).T),
        vT=_f32c(np.asarray(L1_attn_v_proj_weight).T), oT=_f32c(np.asarray(L1_attn_o_proj_weight).T),
        wr=_f32c(L1_ffn_w_router))
    in3b = [dict(x1T=np.asarray(r3[c]["x1T"]), xn1T=np.asarray(r3[c]["xn1T"]), **common3b)
            for c in cores]
    r3b = run_bass_kernel_spmd(nc3b, in3b, core_ids=cores).results

    xn1_tok = np.ascontiguousarray(
        np.concatenate([np.asarray(r3[c]["xn1T"]).T for c in cores], axis=0))
    sel1_all = np.ascontiguousarray(np.concatenate([r3b[c]["sel1_p"] for c in cores], axis=1))
    nc4 = _get("s4", build_s4)
    in4 = [dict(xn1_tok=xn1_tok, sel1_all=sel1_all,
                wi_e=_f32c(L1_ffn_wi[c]), wo_e=_f32c(L1_ffn_wo[c]),
                eid1=_f32c(np.array([[c]], dtype=np.float32))) for c in cores]
    r4 = run_bass_kernel_spmd(nc4, in4, core_ids=cores).results

    y1_full = np.ascontiguousarray(np.concatenate([np.asarray(r4[c]["y1_loc"]) for c in cores], axis=0))
    smap1s = [np.asarray(r4[e]["smap1_p"]).reshape(128, 128, 2) for e in cores]
    nc5 = _get("s5", build_s5)
    in5 = []
    for c in cores:
        so = np.stack([smap1s[e][:, c * 16:(c + 1) * 16, :] for e in cores], axis=3)
        in5.append(dict(
            acc2T=np.asarray(r3b[c]["acc2T"]), y1_full=y1_full,
            smap1_own=np.ascontiguousarray(so.reshape(128, 256), dtype=np.float32),
            sel1_own=np.asarray(r3b[c]["sel1_p"]), gate1_own=np.asarray(r3b[c]["gate1_p"])))
    r5 = run_bass_kernel_spmd(nc5, in5, core_ids=cores).results

    out = np.stack([np.ascontiguousarray(np.asarray(r5[c]["outT"]).T) for c in cores], axis=0)
    return out.astype(np.float32)
